# version 03
# baseline (speedup 1.0000x reference)
import math
import contextlib
import numpy as np
import ml_dtypes

import concourse.bass as bass
import concourse.mybir as mybir
from concourse.bass_utils import run_bass_kernel_spmd
from concourse.alu_op_type import AluOpType as ALU

F32 = mybir.dt.float32
BF16 = mybir.dt.bfloat16
AF = mybir.ActivationFunctionType
AX = mybir.AxisListType.X

D = 1024
T = 4096
NB = 8
DT = D // 128

ENGINES = ("sync", "gpsimd", "tensor", "vector", "scalar")


class Sem:
    __slots__ = ("h", "n", "name")

    def __init__(self, handle, name):
        self.h = handle
        self.n = 0
        self.name = name

    def inc(self, k=1):
        self.n += k
        return self.n


class DmaPool:

    def __init__(self, ctx, name, n=4):
        self.sems = [ctx.sem(f"{name}{i}") for i in range(n)]
        self.i = 0

    def issue(self, P, eng, fn, waits=()):
        s = self.sems[self.i % len(self.sems)]
        self.i += 1
        pre = ((s, s.n),)
        P.do(eng, fn, waits=tuple(waits) + pre, inc=s, incby=16)
        return (s, s.n)


class Ctx:

    def __init__(self, detect_races=True):
        self.nc = bass.Bass("TRN2", target_bir_lowering=False,
                            detect_race_conditions=detect_races)
        self.stack = contextlib.ExitStack()
        self.sems = {}
        self._bufs = {}
        self.sbuf_bytes = 0

    def sem(self, name):
        if name not in self.sems:
            self.sems[name] = Sem(
                self.stack.enter_context(self.nc.semaphore(name)), name)
        return self.sems[name]

    def pool(self, name, n=4):
        if not hasattr(self, "_pools"):
            self._pools = {}
        if name not in self._pools:
            self._pools[name] = DmaPool(self, name, n)
        return self._pools[name]

    def sbuf(self, name, shape, dtype):
        key = ("sb", name)
        if key not in self._bufs:
            self._bufs[key] = self.stack.enter_context(
                self.nc.sbuf_tensor("sb_" + name, list(shape), dtype))
            self.sbuf_bytes += shape[0] * shape[1] * (2 if dtype == BF16 else 4)
        return self._bufs[key]

    def psum(self, name, shape, dtype=F32):
        key = ("ps", name)
        if key not in self._bufs:
            self._bufs[key] = self.stack.enter_context(
                self.nc.psum_tensor(name, list(shape), dtype))
        return self._bufs[key]

    def constcol(self, value):
        if not hasattr(self, "_consts"):
            self._consts = {}
            self.cpool = self.sbuf("cpool", [128, 16], F32)
        if value not in self._consts:
            assert len(self._consts) < 16
            self._consts[value] = len(self._consts)
        i = self._consts[value]
        return self.cpool[:, i:i + 1]

    def const_values(self):
        vals = np.zeros((128, 16), np.float32)
        if hasattr(self, "_consts"):
            for v, i in self._consts.items():
                vals[:, i] = v
        return vals

    def close(self):
        self.stack.close()


class Plan:

    def __init__(self, ctx):
        self.ctx = ctx
        self.q = {e: [] for e in ENGINES}

    def do(self, eng, fn, waits=(), inc=None, incby=1):
        waits = tuple((s, v) for (s, v) in waits)

        def thunk(e, fn=fn, waits=waits, inc=inc, incby=incby):
            for s, v in waits:
                e.wait_ge(s.h, v)
            instr = fn(e)
            if inc is not None:
                if instr is None:
                    instr = e.nop()
                instr.then_inc(inc.h, incby)
        self.q[eng].append(thunk)
        if inc is not None:
            inc.inc(incby)
        return inc.n if inc is not None else None

    def wait(self, eng, sem, val):
        self.do(eng, lambda e: None, waits=((sem, val),))

    def emit(self):
        nc = self.ctx.nc
        with nc.Block() as block:
            @block.sync
            def _(e):
                for fn in self.q["sync"]:
                    fn(e)

            @block.gpsimd
            def _(e):
                for fn in self.q["gpsimd"]:
                    fn(e)

            @block.tensor
            def _(e):
                for fn in self.q["tensor"]:
                    fn(e)

            @block.vector
            def _(e):
                for fn in self.q["vector"]:
                    fn(e)

            @block.scalar
            def _(e):
                for fn in self.q["scalar"]:
                    fn(e)


def drain_pools(P, ctx, eng="gpsimd"):
    if hasattr(ctx, "_pools"):
        for pool in ctx._pools.values():
            for s in pool.sems:
                if s.n:
                    P.wait(eng, s, s.n)


def rearr_dT(ap, c0, c1):
    return ap[:, c0:c1].rearrange("(a p) t -> p a t", p=128)


def sb3(ap, n):
    return ap.rearrange("p (a t) -> p a t", a=n)



def plan_plain_ln(P, ctx, *, xt, TC, sq, ps_st, st_col, rowm, rowr, ps_bc,
                  bc_col, xnb, ld_sem, ld_val, done_sem,
                  ones_row, ones_col, eps=1e-5, pref=""):
    mm = ctx.sem("mm"); vv = ctx.sem("vv"); ss = ctx.sem("ss")

    v = P.do("scalar", lambda e: e.activation(sq[:], xt[:], AF.Square),
             waits=((ld_sem, ld_val),), inc=ss)
    def statmm(e):
        for a in range(DT):
            e.matmul(ps_st[0:1, st_col:st_col + TC], ones_col[:],
                     xt[:, a * TC:(a + 1) * TC], start=(a == 0), stop=(a == DT - 1))
        for a in range(DT):
            e.matmul(ps_st[0:1, st_col + TC:st_col + 2 * TC], ones_col[:],
                     sq[:, a * TC:(a + 1) * TC], start=(a == 0), stop=(a == DT - 1))
        return e.nop()
    m1 = P.do("tensor", statmm, waits=((ss, v), (ld_sem, ld_val)), inc=mm)

    def rowmath_v(e):
        e.tensor_scalar(rowm[:], ps_st[0:1, st_col:st_col + TC], 1.0 / D, None, ALU.mult)
        e.tensor_scalar(rowr[:], ps_st[0:1, st_col + TC:st_col + 2 * TC], 1.0 / D, None, ALU.mult)
        e.tensor_tensor(sq[0:1, 0:TC], rowm[:], rowm[:], ALU.mult)
        return e.tensor_tensor(rowr[:], rowr[:], sq[0:1, 0:TC], ALU.subtract)
    v2 = P.do("vector", rowmath_v, waits=((mm, m1),), inc=vv)

    def rowmath_s(e):
        e.activation(rowr[:], rowr[:], AF.Ln, bias=eps)
        return e.activation(rowr[:], rowr[:], AF.Exp, scale=-0.5)
    s2 = P.do("scalar", rowmath_s, waits=((vv, v2),), inc=ss)

    v3 = P.do("vector",
              lambda e: e.scalar_tensor_tensor(rowm[:], rowm[:], -1.0, rowr[:],
                                               ALU.mult, ALU.mult),
              waits=((ss, s2),), inc=vv)

    def bcast(e):
        e.matmul(ps_bc[:, bc_col:bc_col + TC], ones_row[:], rowr[:],
                 start=True, stop=True)
        e.matmul(ps_bc[:, bc_col + TC:bc_col + 2 * TC], ones_row[:], rowm[:],
                 start=True, stop=True)
        return e.nop()
    m2 = P.do("tensor", bcast, waits=((vv, v3),), inc=mm)

    def norm(e):
        for a in range(DT):
            e.tensor_tensor(sq[:, a * TC:(a + 1) * TC], xt[:, a * TC:(a + 1) * TC],
                            ps_bc[:, bc_col:bc_col + TC], ALU.mult)
            last = e.tensor_tensor(xnb[:, a * TC:(a + 1) * TC],
                                   sq[:, a * TC:(a + 1) * TC],
                                   ps_bc[:, bc_col + TC:bc_col + 2 * TC], ALU.add)
        return last
    v4 = P.do("vector", norm, waits=((mm, m2),), inc=done_sem)
    return v4


def plan_ffn(P, ctx, io, uid="f"):
    FF = 4096
    FT = FF // 128
    TC = 128
    NCH = T // TC

    ones_row = ctx.sbuf("ones_row", [1, 128], F32)
    ones_col = ctx.sbuf("ones_col", [128, 1], F32)

    wup_t = ctx.sbuf(uid + "wup_t", [128, DT * FF], BF16)
    wdn_t = ctx.sbuf(uid + "wdn_t", [128, FT * D], BF16)
    raw = ctx.sbuf(uid + "raw", [128, 2 * 2048], F32)
    absw = ctx.sbuf(uid + "absw", [128, 2048], F32)
    sgnw = ctx.sbuf(uid + "sgnw", [128, 2048], F32)
    asum = ctx.sbuf(uid + "asum", [128, 32], F32)
    alpha = ctx.sbuf(uid + "alpha", [1, 2], F32)
    albc = ctx.sbuf(uid + "albc", [128, 2], F32)

    xt = ctx.sbuf(uid + "xt", [128, 2 * DT * TC], F32)
    sq = ctx.sbuf(uid + "sq", [128, DT * TC], F32)
    xnb = ctx.sbuf(uid + "xnb", [128, DT * TC], BF16)
    hgb = ctx.sbuf(uid + "hgb", [128, FT * TC], BF16)
    hsq = ctx.sbuf(uid + "hsq", [128, 2 * TC], F32)
    rowa = ctx.sbuf(uid + "rowa", [1, 4 * TC], F32)
    hrow = ctx.sbuf(uid + "hrow", [1, TC], F32)
    sbc = ctx.sbuf(uid + "sbc", [128, TC], F32)
    yout = ctx.sbuf(uid + "yout", [128, DT * TC], F32)

    ps_out = ctx.psum(uid + "ps_out", [128, DT * TC], F32)
    ps_up = ctx.psum(uid + "ps_up", [128, 1024], F32)
    ps_misc = ctx.psum(uid + "ps_misc", [128, 1024], F32)

    mm = ctx.sem("mm"); vv = ctx.sem("vv"); ss = ctx.sem("ss")
    LD = ctx.pool("ld", 6)
    STP = ctx.pool("st", 3)

    cpool = ctx.sbuf("cpool", [128, 16], F32)
    LD.issue(P, "sync", lambda e: e.dma_start(out=cpool[:], in_=io["cpool"][:]))
    LD.issue(P, "sync", lambda e: e.dma_start(out=ones_row[:], in_=io["ones_row"][:]))
    cns = LD.issue(P, "sync", lambda e: e.dma_start(out=ones_col[:], in_=io["ones_col"][:]))

    wupR = io["w_upT"].rearrange("(a p) f -> p a f", p=128)
    wdnR = io["w_downT"].rearrange("(a p) f -> p a f", p=128)

    red_ms = []
    for j in range(32):
        par = j % 2
        if j < 16:
            view = wupR[:, j // 2, (j % 2) * 2048:(j % 2) * 2048 + 2048]
        else:
            jj = j - 16
            view = wdnR[:, 2 * jj:2 * jj + 2, :]
        waits = [(vv, red_ms[j - 2])] if j >= 2 else []
        lv = LD.issue(P, "sync", lambda e, par=par, view=view: e.dma_start(
            out=(raw[:, par * 2048:(par + 1) * 2048] if view.ndim == 2
                 else sb3(raw[:, par * 2048:(par + 1) * 2048], 2)),
            in_=view),
            waits=waits)
        rv = P.do("vector", lambda e, j=j, par=par: e.tensor_reduce(
            asum[:, j:j + 1], raw[:, par * 2048:(par + 1) * 2048], AX, ALU.add,
            apply_absolute_value=True),
            waits=(lv, (cns[0], cns[1])), inc=vv)
        red_ms.append(rv)

    def alph_mm(e):
        e.matmul(ps_misc[0:1, 0:16], ones_col[:], asum[:, 0:16], start=True, stop=True)
        return e.matmul(ps_misc[0:1, 16:32], ones_col[:], asum[:, 16:32], start=True, stop=True)
    am = P.do("tensor", alph_mm, waits=((vv, red_ms[-1]),), inc=mm)
    av = P.do("vector", lambda e: e.tensor_reduce(
        alpha[0:1, 0:1], ps_misc[0:1, 0:16], AX, ALU.add), waits=((mm, am),), inc=vv)
    av2 = P.do("vector", lambda e: e.tensor_reduce(
        alpha[0:1, 1:2], ps_misc[0:1, 16:32], AX, ALU.add), inc=vv)
    asc = P.do("scalar", lambda e: e.activation(
        alpha[0:1, 0:2], alpha[0:1, 0:2], AF.Copy, scale=1.0 / (2.0 * D * FF)),
        waits=((vv, av2),), inc=ss)
    abm = P.do("tensor", lambda e: e.matmul(
        ps_misc[:, 32:34], ones_row[:], alpha[0:1, 0:2], start=True, stop=True),
        waits=((ss, asc),), inc=mm)
    abv = P.do("vector", lambda e: e.tensor_copy(albc[:, 0:2], ps_misc[:, 32:34]),
               waits=((mm, abm),), inc=vv)

    tern_s = []; tern_v = []
    for j in range(32):
        par = j % 2
        if j < 16:
            view = wupR[:, j // 2, (j % 2) * 2048:(j % 2) * 2048 + 2048]
        else:
            jj = j - 16
            view = wdnR[:, 2 * jj:2 * jj + 2, :]
        dst = wup_t if j < 16 else wdn_t
        col = (j if j < 16 else j - 16) * 2048
        acol = 0 if j < 16 else 1
        waits = [(vv, red_ms[-1])] if j < 2 else [(ss, tern_s[j - 2])]
        lv = LD.issue(P, "sync", lambda e, par=par, view=view: e.dma_start(
            out=(raw[:, par * 2048:(par + 1) * 2048] if view.ndim == 2
                 else sb3(raw[:, par * 2048:(par + 1) * 2048], 2)),
            in_=view),
            waits=waits)
        swaits = [lv]
        if j == 0:
            swaits.append((vv, abv))
        if j >= 1:
            swaits.append((vv, tern_v[j - 1]))
        def tern_sc(e, par=par):
            e.activation(absw[:], raw[:, par * 2048:(par + 1) * 2048], AF.Abs)
            return e.activation(sgnw[:], raw[:, par * 2048:(par + 1) * 2048], AF.Sign)
        sv = P.do("scalar", tern_sc, waits=swaits, inc=ss)
        tern_s.append(sv)
        def tern_ve(e, dst=dst, col=col, acol=acol):
            e.tensor_scalar(absw[:], absw[:], albc[:, acol:acol + 1], None, ALU.is_gt)
            return e.tensor_tensor(dst[:, col:col + 2048], sgnw[:], absw[:], ALU.mult)
        tv = P.do("vector", tern_ve, waits=((ss, sv),), inc=vv)
        tern_v.append(tv)
    tern_all = tern_v[-1]

    MU, SQc, B1, B2 = 0, 256, 512, 768
    store_ms = []
    ye_prev = None
    rv1_prev = None
    for c in range(NCH):
        par = c % 2
        xo = par * DT * TC
        waits = []
        if c >= 2:
            waits.append(store_ms[c - 2])
        xl = LD.issue(P, "sync", lambda e, c=c, xo=xo: e.dma_start(
            out=sb3(xt[:, xo:xo + DT * TC], DT),
            in_=rearr_dT(io["xT"], c * TC, (c + 1) * TC)),
            waits=waits)

        swaits = [xl]
        if ye_prev is not None:
            swaits.append((vv, ye_prev))
        sv = P.do("scalar", lambda e, xo=xo: e.activation(
            sq[:], xt[:, xo:xo + DT * TC], AF.Square),
            waits=swaits, inc=ss)

        def xstat(e, xo=xo):
            for a in range(DT):
                e.matmul(ps_misc[0:1, MU:MU + TC], ones_col[:],
                         xt[:, xo + a * TC:xo + (a + 1) * TC],
                         start=(a == 0), stop=(a == DT - 1))
            for a in range(DT):
                last = e.matmul(ps_misc[0:1, SQc:SQc + TC], ones_col[:],
                                sq[:, a * TC:(a + 1) * TC],
                                start=(a == 0), stop=(a == DT - 1))
            return last
        xw = [(ss, sv)]
        if rv1_prev is not None:
            xw.append((vv, rv1_prev))
        xm = P.do("tensor", xstat, waits=xw, inc=mm)

        def rowv1(e):
            e.tensor_scalar(rowa[0:1, 0:TC], ps_misc[0:1, MU:MU + TC], 1.0 / D, None, ALU.mult)
            e.tensor_scalar(rowa[0:1, TC:2 * TC], ps_misc[0:1, SQc:SQc + TC], 1.0 / D, None, ALU.mult)
            e.tensor_tensor(rowa[0:1, 2 * TC:3 * TC], rowa[0:1, 0:TC], rowa[0:1, 0:TC], ALU.mult)
            return e.tensor_tensor(rowa[0:1, TC:2 * TC], rowa[0:1, TC:2 * TC],
                                   rowa[0:1, 2 * TC:3 * TC], ALU.subtract)
        rv1 = P.do("vector", rowv1, waits=((mm, xm),), inc=vv)
        rv1_prev = rv1

        def rows1(e):
            e.activation(rowa[0:1, 2 * TC:3 * TC], rowa[0:1, TC:2 * TC], AF.Ln,
                         bias=ctx.constcol(1e-5)[0:1, :])
            e.activation(rowa[0:1, 2 * TC:3 * TC], rowa[0:1, 2 * TC:3 * TC], AF.Exp, scale=-0.5)
            return e.activation(rowa[0:1, 3 * TC:4 * TC], rowa[0:1, TC:2 * TC], AF.Sqrt)
        rs1 = P.do("scalar", rows1, waits=((vv, rv1),), inc=ss)

        def rowv2(e):
            e.tensor_tensor(rowa[0:1, TC:2 * TC], rowa[0:1, 3 * TC:4 * TC],
                            rowa[0:1, 2 * TC:3 * TC], ALU.mult)
            e.tensor_scalar(rowa[0:1, TC:2 * TC], rowa[0:1, TC:2 * TC], 1e-8, None, ALU.add)
            e.reciprocal(rowa[0:1, TC:2 * TC], rowa[0:1, TC:2 * TC])
            e.tensor_tensor(rowa[0:1, 2 * TC:3 * TC], rowa[0:1, 2 * TC:3 * TC],
                            rowa[0:1, TC:2 * TC], ALU.mult)
            return e.scalar_tensor_tensor(rowa[0:1, 3 * TC:4 * TC], rowa[0:1, 0:TC],
                                          -1.0, rowa[0:1, 2 * TC:3 * TC],
                                          ALU.mult, ALU.mult)
        rv2 = P.do("vector", rowv2, waits=((ss, rs1),), inc=vv)

        def xbcast(e):
            e.matmul(ps_misc[:, B1:B1 + TC], ones_row[:], rowa[0:1, 2 * TC:3 * TC],
                     start=True, stop=True)
            return e.matmul(ps_misc[:, B2:B2 + TC], ones_row[:], rowa[0:1, 3 * TC:4 * TC],
                            start=True, stop=True)
        xbm = P.do("tensor", xbcast, waits=((vv, rv2),), inc=mm)

        def xnorm(e, xo=xo):
            for a in range(DT):
                e.tensor_tensor(sq[:, a * TC:(a + 1) * TC],
                                xt[:, xo + a * TC:xo + (a + 1) * TC],
                                ps_misc[:, B1:B1 + TC], ALU.mult)
                last = e.tensor_tensor(xnb[:, a * TC:(a + 1) * TC],
                                       sq[:, a * TC:(a + 1) * TC],
                                       ps_misc[:, B2:B2 + TC], ALU.add)
            return last
        xn = P.do("vector", xnorm, waits=((mm, xbm),), inc=vv)

        hev = []; hqv = []; hmv = []
        for f in range(FT):
            fo = (f % 2) * 512
            w = [(vv, xn)]
            if c == 0 and f == 0:
                w.append((vv, tern_all))
            if f >= 2:
                w.append((ss, hev[f - 2]))
            def upmm(e, f=f, fo=fo):
                for a in range(DT):
                    last = e.matmul(ps_up[:, fo:fo + TC],
                                    wup_t[:, a * 4096 + f * 128:a * 4096 + f * 128 + 128],
                                    xnb[:, a * TC:(a + 1) * TC],
                                    start=(a == 0), stop=(a == DT - 1))
                return last
            um = P.do("tensor", upmm, waits=w, inc=mm)

            gw = [(mm, um)]
            ge = P.do("scalar", lambda e, fo=fo, f=f: e.activation(
                hgb[:, f * TC:(f + 1) * TC], ps_up[:, fo:fo + TC], AF.Gelu),
                waits=gw, inc=ss)
            hev.append(ge)

            hw = [(ss, ge)]
            if f >= 2:
                hw.append((mm, hmv[f - 2]))
            hq = P.do("vector", lambda e, f=f: e.tensor_tensor(
                hsq[:, (f % 2) * TC:(f % 2) * TC + TC], hgb[:, f * TC:(f + 1) * TC],
                hgb[:, f * TC:(f + 1) * TC], ALU.mult), waits=hw, inc=vv)
            hqv.append(hq)

            hm = P.do("tensor", lambda e, f=f: e.matmul(
                ps_misc[0:1, MU:MU + TC], ones_col[:],
                hsq[:, (f % 2) * TC:(f % 2) * TC + TC],
                start=(f == 0), stop=(f == FT - 1)), waits=((vv, hq),), inc=mm)
            hmv.append(hm)

        dnm = []
        for a in range(DT):
            dw = [(ss, hev[-1])]
            if a == 0 and ye_prev is not None:
                dw.append((vv, ye_prev))
            def dnfull(e, a=a):
                for f in range(FT):
                    last = e.matmul(ps_out[:, a * TC:(a + 1) * TC],
                                    wdn_t[:, f * D + a * 128:f * D + a * 128 + 128],
                                    hgb[:, f * TC:(f + 1) * TC],
                                    start=(f == 0), stop=(f == FT - 1))
                return last
            dn = P.do("tensor", dnfull, waits=dw, inc=mm)
            dnm.append(dn)

        hs = P.do("scalar", lambda e: e.activation(
            hrow[:], ps_misc[0:1, MU:MU + TC], AF.Sqrt, scale=1.0 / FF),
            waits=((mm, dnm[-1]),), inc=ss)

        def hrcp(e):
            e.tensor_scalar(hrow[:], hrow[:], 1e-8, None, ALU.add)
            return e.reciprocal(hrow[:], hrow[:])
        hv = P.do("vector", hrcp, waits=((ss, hs),), inc=vv)
        hbm = P.do("tensor", lambda e: e.matmul(
            ps_misc[:, B1:B1 + TC], ones_row[:], hrow[:],
            start=True, stop=True), waits=((vv, hv),), inc=mm)
        hbv = P.do("vector", lambda e: e.tensor_copy(
            sbc[:], ps_misc[:, B1:B1 + TC]), waits=((mm, hbm),), inc=vv)

        def yev(e, xo=xo):
            for a in range(DT):
                e.tensor_tensor(sq[:, a * TC:(a + 1) * TC],
                                ps_out[:, a * TC:(a + 1) * TC], sbc[:], ALU.mult)
                last = e.tensor_tensor(yout[:, a * TC:(a + 1) * TC],
                                       sq[:, a * TC:(a + 1) * TC],
                                       xt[:, xo + a * TC:xo + (a + 1) * TC], ALU.add)
            return last
        yw = [(vv, hbv)]
        if c >= 1:
            yw.append(store_ms[c - 1])
        ye = P.do("vector", yev, waits=yw, inc=vv)
        ye_prev = ye

        sret = STP.issue(P, "sync", lambda e, c=c: e.dma_start(
            out=rearr_dT(io["yT"], c * TC, (c + 1) * TC), in_=sb3(yout[:], DT)),
            waits=((vv, ye),))
        store_ms.append(sret)

    drain_pools(P, ctx)
    return store_ms


def plan_moe(P, ctx, io, uid="m"):
    FF = 8192
    FT = FF // 128
    NT = 512
    G = 4
    NG = FT // G

    ones_row = ctx.sbuf("ones_row", [1, 128], F32)
    ones_col = ctx.sbuf("ones_col", [128, 1], F32)
    cpool = ctx.sbuf("cpool", [128, 16], F32)

    xt = ctx.sbuf(uid + "xt", [128, DT * NT], F32)
    sq = ctx.sbuf(uid + "sq", [128, DT * NT], F32)
    xnb = ctx.sbuf(uid + "xnb", [128, DT * NT], BF16)
    w1s = ctx.sbuf(uid + "w1s", [128, 2 * DT * G * 128], BF16)
    w2s = ctx.sbuf(uid + "w2s", [128, 2 * FT * 128], BF16)
    hgb = ctx.sbuf(uid + "hgb", [128, FT * NT], BF16)
    rowa = ctx.sbuf(uid + "rowa", [1, 4 * NT], F32)
    wsr = ctx.sbuf(uid + "wsr", [1, 2 * NT], F32)
    wsbc = ctx.sbuf(uid + "wsbc", [128, NT], F32)
    yout = ctx.sbuf(uid + "yout", [128, DT * NT], F32)

    ps_h = ctx.psum(uid + "ps_h", [128, 2 * NT], F32)
    ps_y = ctx.psum(uid + "ps_y", [128, 2 * NT], F32)
    ps_misc = ctx.psum(uid + "ps_misc", [128, 2048], F32)

    mm = ctx.sem("mm"); vv = ctx.sem("vv"); ss = ctx.sem("ss")
    LD = ctx.pool("ld", 6)
    WQ = ctx.pool("wq", 4)
    STP = ctx.pool("st", 3)

    LD.issue(P, "sync", lambda e: e.dma_start(out=cpool[:], in_=io["cpool"][:]))
    LD.issue(P, "sync", lambda e: e.dma_start(out=ones_row[:], in_=io["ones_row"][:]))
    LD.issue(P, "sync", lambda e: e.dma_start(out=ones_col[:], in_=io["ones_col"][:]))
    xl = LD.issue(P, "sync", lambda e: e.dma_start(
        out=sb3(xt[:], DT), in_=rearr_dT(io["xT"], 0, NT)))
    el = LD.issue(P, "sync", lambda e: e.dma_start(
        out=wsr[:].rearrange("p (a t) -> p a t", a=2), in_=io["ewT"][:].unsqueeze(0)))

    w1R = io["w1"].rearrange("(a p) f -> p a f", p=128)
    w2R = io["w2S"]

    sv = P.do("scalar", lambda e: e.activation(sq[:], xt[:], AF.Square),
              waits=(xl,), inc=ss)
    MU, SQc, B1, B2 = 0, 512, 1024, 1536
    def xstat(e):
        for a in range(DT):
            e.matmul(ps_misc[0:1, MU:MU + NT], ones_col[:],
                     xt[:, a * NT:(a + 1) * NT], start=(a == 0), stop=(a == DT - 1))
        for a in range(DT):
            last = e.matmul(ps_misc[0:1, SQc:SQc + NT], ones_col[:],
                            sq[:, a * NT:(a + 1) * NT], start=(a == 0), stop=(a == DT - 1))
        return last
    xm = P.do("tensor", xstat, waits=((ss, sv), xl), inc=mm)

    def rowv1(e):
        e.tensor_scalar(rowa[0:1, 0:NT], ps_misc[0:1, MU:MU + NT], 1.0 / D, None, ALU.mult)
        e.tensor_scalar(rowa[0:1, NT:2 * NT], ps_misc[0:1, SQc:SQc + NT], 1.0 / D, None, ALU.mult)
        e.tensor_tensor(rowa[0:1, 2 * NT:3 * NT], rowa[0:1, 0:NT], rowa[0:1, 0:NT], ALU.mult)
        return e.tensor_tensor(rowa[0:1, NT:2 * NT], rowa[0:1, NT:2 * NT],
                               rowa[0:1, 2 * NT:3 * NT], ALU.subtract)
    rv1 = P.do("vector", rowv1, waits=((mm, xm),), inc=vv)

    def rows1(e):
        e.activation(rowa[0:1, 2 * NT:3 * NT], rowa[0:1, NT:2 * NT], AF.Ln,
                     bias=ctx.constcol(1e-5)[0:1, :])
        return e.activation(rowa[0:1, 2 * NT:3 * NT], rowa[0:1, 2 * NT:3 * NT],
                            AF.Exp, scale=-0.5)
    rs1 = P.do("scalar", rows1, waits=((vv, rv1),), inc=ss)
    rv2 = P.do("vector", lambda e: e.scalar_tensor_tensor(
        rowa[0:1, 3 * NT:4 * NT], rowa[0:1, 0:NT], -1.0,
        rowa[0:1, 2 * NT:3 * NT], ALU.mult, ALU.mult), waits=((ss, rs1),), inc=vv)

    def xbcast(e):
        e.matmul(ps_misc[:, B1:B1 + NT], ones_row[:], rowa[0:1, 2 * NT:3 * NT],
                 start=True, stop=True)
        return e.matmul(ps_misc[:, B2:B2 + NT], ones_row[:], rowa[0:1, 3 * NT:4 * NT],
                        start=True, stop=True)
    xbm = P.do("tensor", xbcast, waits=((vv, rv2),), inc=mm)

    def xnorm(e):
        for a in range(DT):
            e.tensor_tensor(sq[:, a * NT:(a + 1) * NT], xt[:, a * NT:(a + 1) * NT],
                            ps_misc[:, B1:B1 + NT], ALU.mult)
            last = e.tensor_tensor(xnb[:, a * NT:(a + 1) * NT],
                                   sq[:, a * NT:(a + 1) * NT],
                                   ps_misc[:, B2:B2 + NT], ALU.add)
        return last
    xn = P.do("vector", xnorm, waits=((mm, xbm),), inc=vv)

    GW = DT * G * 128
    gel = []
    wls = []
    for g in range(NG):
        par = g % 2
        wl = WQ.issue(P, "gpsimd", lambda e, g=g, par=par: e.dma_start(
            out=w1s[:, par * GW:(par + 1) * GW].rearrange("p (a f) -> p a f", a=DT),
            in_=w1R[:, :, g * G * 128:(g + 1) * G * 128]),
            waits=[(mm, wls[g - 2][1])] if g >= 2 else ())
        for i in range(G):
            f = g * G + i
            fo = (f % 2) * NT
            w = [(vv, xn), wl]
            if f >= 2:
                w.append((ss, gel[f - 2]))
            def upmm(e, par=par, i=i, fo=fo):
                for a in range(DT):
                    last = e.matmul(ps_h[:, fo:fo + NT],
                                    w1s[:, par * GW + a * G * 128 + i * 128:
                                        par * GW + a * G * 128 + (i + 1) * 128],
                                    xnb[:, a * NT:(a + 1) * NT],
                                    start=(a == 0), stop=(a == DT - 1))
                return last
            um = P.do("tensor", upmm, waits=w, inc=mm)
            ge = P.do("scalar", lambda e, fo=fo, f=f: e.activation(
                hgb[:, f * NT:(f + 1) * NT], ps_h[:, fo:fo + NT],
                AF.Gelu_apprx_tanh), waits=((mm, um),), inc=ss)
            gel.append(ge)
        wls.append((wl, mm.n))

    wv = P.do("vector", lambda e: e.tensor_tensor(
        rowa[0:1, 0:NT], wsr[0:1, 0:NT], wsr[0:1, NT:2 * NT], ALU.add),
        waits=(el,), inc=vv)
    wbm = P.do("tensor", lambda e: e.matmul(
        ps_misc[:, B1:B1 + NT], ones_row[:], rowa[0:1, 0:NT],
        start=True, stop=True), waits=((vv, wv), (ss, gel[-1])), inc=mm)
    wbv = P.do("vector", lambda e: e.tensor_copy(wsbc[:], ps_misc[:, B1:B1 + NT]),
               waits=((mm, wbm),), inc=vv)

    store_ms = []
    evs = []
    for a in range(DT):
        par = a % 2
        wl = WQ.issue(P, "gpsimd", lambda e, a=a, par=par: e.dma_start(
            out=w2s[:, par * FT * 128:(par + 1) * FT * 128].rearrange(
                "p (g d) -> p g d", g=FT),
            in_=w2R[:, a, :, :]),
            waits=[(mm, evs[a - 2][1])] if a >= 2 else ())
        def dnfull(e, a=a, par=par):
            for f in range(FT):
                last = e.matmul(ps_y[:, (a % 2) * NT:(a % 2) * NT + NT],
                                w2s[:, par * FT * 128 + f * 128:
                                    par * FT * 128 + (f + 1) * 128],
                                hgb[:, f * NT:(f + 1) * NT],
                                start=(f == 0), stop=(f == FT - 1))
            return last
        dw = [wl, (ss, gel[-1])]
        if a >= 2:
            dw.append((vv, evs[a - 2][0]))
        dn = P.do("tensor", dnfull, waits=dw, inc=mm)

        def yev(e, a=a):
            e.tensor_tensor(sq[:, a * NT:(a + 1) * NT],
                            ps_y[:, (a % 2) * NT:(a % 2) * NT + NT], wsbc[:], ALU.mult)
            return e.tensor_tensor(yout[:, a * NT:(a + 1) * NT],
                                   sq[:, a * NT:(a + 1) * NT],
                                   xt[:, a * NT:(a + 1) * NT], ALU.add)
        ye = P.do("vector", yev, waits=((mm, dn), (vv, wbv)), inc=vv)
        evs.append((ye, mm.n))
    sret = STP.issue(P, "sync", lambda e: e.dma_start(
        out=rearr_dT(io["yT"], 0, NT), in_=sb3(yout[:], DT)),
        waits=((vv, evs[-1][0]),))

    drain_pools(P, ctx)
    return [sret]


M1_TC = 256


def plan_mamba1(P, ctx, io, uid="b"):
    TC = M1_TC
    NCH = T // TC
    NST = 16

    ones_row = ctx.sbuf("ones_row", [1, 128], F32)
    ones_col = ctx.sbuf("ones_col", [128, 1], F32)
    ones_rb = ctx.sbuf("ones_rb", [1, 128], BF16)
    cpool = ctx.sbuf("cpool", [128, 16], F32)
    identB = ctx.sbuf("identB", [128, 128], BF16)

    ipw = ctx.sbuf(uid + "ipw", [128, DT * 2048], BF16)
    opw = ctx.sbuf(uid + "opw", [128, DT * 1024], BF16)
    xpB = ctx.sbuf(uid + "xpB", [128, DT * NST], BF16)
    xpC = ctx.sbuf(uid + "xpC", [128, DT * NST], BF16)
    xpD = ctx.sbuf(uid + "xpD", [128, DT], BF16)
    cwt = ctx.sbuf(uid + "cwt", [128, DT * 4], F32)
    cbt = ctx.sbuf(uid + "cbt", [128, DT], F32)
    dtwv = ctx.sbuf(uid + "dtwv", [128, DT], F32)
    dtbv = ctx.sbuf(uid + "dtbv", [128, DT], F32)
    amat = ctx.sbuf(uid + "amat", [128, DT * NST], F32)

    xt = ctx.sbuf(uid + "xt", [128, DT * TC], F32)
    sq = ctx.sbuf(uid + "sq", [128, DT * TC], F32)
    xnb = ctx.sbuf(uid + "xnb", [128, DT * TC], BF16)
    xin = ctx.sbuf(uid + "xin", [128, DT * (TC + 4)], F32)
    zs = ctx.sbuf(uid + "zs", [128, DT * TC], BF16)
    xcs = ctx.sbuf(uid + "xcs", [128, DT * TC], F32)
    xcb = ctx.sbuf(uid + "xcb", [128, DT * TC], BF16)
    dtt = ctx.sbuf(uid + "dtt", [128, DT * TC], F32)
    spB = ctx.sbuf(uid + "spB", [16, TC], BF16)
    spC = ctx.sbuf(uid + "spC", [16, TC], BF16)
    spfB = ctx.sbuf(uid + "spfB", [1, NST * TC], BF16)
    spfC = ctx.sbuf(uid + "spfC", [1, NST * TC], BF16)
    dtrow = ctx.sbuf(uid + "dtrow", [1, TC], F32)
    rowq = ctx.sbuf(uid + "rowq", [1, 4 * TC], F32)
    Bbc = ctx.sbuf(uid + "Bbc", [128, NST * TC], BF16)
    Cbc = ctx.sbuf(uid + "Cbc", [128, NST * TC], BF16)
    dab = ctx.sbuf(uid + "dab", [128, 2 * TC], F32)
    dxb = ctx.sbuf(uid + "dxb", [128, 2 * TC], F32)
    hbuf = ctx.sbuf(uid + "hbuf", [128, 2 * TC], F32)
    tmpb = ctx.sbuf(uid + "tmpb", [128, 2 * TC], BF16)
    state = ctx.sbuf(uid + "state", [128, 128], F32)
    ygb = ctx.sbuf(uid + "ygb", [128, DT * TC], BF16)
    yout = ctx.sbuf(uid + "yout", [128, 2 * TC], F32)

    ps_a = ctx.psum(uid + "ps_a", [128, 1024], F32)
    ps_b = ctx.psum(uid + "ps_b", [128, 1024], F32)
    ps_y = ctx.psum(uid + "ps_y", [128, 1024], F32)
    ps_o = ctx.psum(uid + "ps_o", [128, 1024], F32)

    mm = ctx.sem("mm"); vv = ctx.sem("vv"); ss = ctx.sem("ss")
    LD = ctx.pool("ld", 6)
    WG = ctx.pool("wg", 3)
    STP = ctx.pool("st", 3)
    SH = ctx.pool("sh", 2)

    LD.issue(P, "sync", lambda e: e.dma_start(out=cpool[:], in_=io["cpool"][:]))
    LD.issue(P, "sync", lambda e: e.dma_start(out=ones_row[:], in_=io["ones_row"][:]))
    LD.issue(P, "sync", lambda e: e.dma_start(out=ones_col[:], in_=io["ones_col"][:]))
    LD.issue(P, "sync", lambda e: e.dma_start(out=ones_rb[:], in_=io["ones_row_b"][:]))
    cl = LD.issue(P, "sync", lambda e: e.dma_start(out=identB[:], in_=io["identB"][:]))
    wl1 = WG.issue(P, "gpsimd", lambda e: e.dma_start(
        out=sb3(ipw[:], DT), in_=io["inprojT"].rearrange("(a p) f -> p a f", p=128)))
    wl2 = WG.issue(P, "gpsimd", lambda e: e.dma_start(
        out=sb3(opw[:], DT), in_=io["outprojT"].rearrange("(a p) f -> p a f", p=128)))
    wl3 = WG.issue(P, "gpsimd", lambda e: e.dma_start(
        out=sb3(xpB[:], DT), in_=io["xprojB"].rearrange("(a p) n -> p a n", p=128)))
    wl4 = WG.issue(P, "gpsimd", lambda e: e.dma_start(
        out=sb3(xpC[:], DT), in_=io["xprojC"].rearrange("(a p) n -> p a n", p=128)))
    wl5 = WG.issue(P, "gpsimd", lambda e: e.dma_start(
        out=xpD[:], in_=io["xprojD"][:]))
    wl6 = LD.issue(P, "sync", lambda e: e.dma_start(
        out=sb3(cwt[:], DT), in_=io["convW"].rearrange("(a p) k -> p a k", p=128)))
    wl7 = LD.issue(P, "sync", lambda e: e.dma_start(out=cbt[:], in_=io["convB"][:]))
    wl8 = LD.issue(P, "sync", lambda e: e.dma_start(out=dtwv[:], in_=io["dtW"][:]))
    wl9 = LD.issue(P, "sync", lambda e: e.dma_start(out=dtbv[:], in_=io["dtB"][:]))
    wl10 = LD.issue(P, "sync", lambda e: e.dma_start(
        out=sb3(amat[:], DT), in_=io["Alog"].rearrange("(a p) n -> p a n", p=128)))

    sA = P.do("scalar", lambda e: e.activation(amat[:], amat[:], AF.Exp),
              waits=(wl10,), inc=ss)
    vA = P.do("vector", lambda e: e.tensor_scalar(amat[:], amat[:], -1.0, None,
                                                  ALU.mult), waits=((ss, sA),), inc=vv)
    z0 = P.do("vector", lambda e: (e.memset(state[:], 0.0),
                                   e.memset(xin[:], 0.0))[-1], inc=vv)

    MU, SQc = 0, 512
    B1, B2 = 0, 512
    store_ms = []
    prev_stores = []
    ye_prev = None
    rv1_prev = None
    gate_prev = []
    scan_ms = {}
    ymm_ms = {}
    tmp_ms = {}
    for c in range(NCH):
        waits = []
        if c >= 1:
            waits.extend(prev_stores)
        xl = LD.issue(P, "sync", lambda e, c=c: e.dma_start(
            out=sb3(xt[:], DT), in_=rearr_dT(io["xT"], c * TC, (c + 1) * TC)),
            waits=waits)

        swaits = [xl]
        if ye_prev is not None:
            swaits.append((vv, ye_prev))
        sv = P.do("scalar", lambda e: e.activation(sq[:], xt[:], AF.Square),
                  waits=swaits, inc=ss)
        def xstat(e):
            for a in range(DT):
                e.matmul(ps_a[0:1, MU:MU + TC], ones_col[:],
                         xt[:, a * TC:(a + 1) * TC], start=(a == 0), stop=(a == DT - 1))
            for a in range(DT):
                last = e.matmul(ps_a[0:1, SQc:SQc + TC], ones_col[:],
                                sq[:, a * TC:(a + 1) * TC], start=(a == 0), stop=(a == DT - 1))
            return last
        xw = [(ss, sv)]
        if rv1_prev is not None:
            xw.append((vv, rv1_prev))
        xm = P.do("tensor", xstat, waits=xw, inc=mm)
        def rowv1(e):
            e.tensor_scalar(rowq[0:1, 0:TC], ps_a[0:1, MU:MU + TC], 1.0 / D, None, ALU.mult)
            e.tensor_scalar(rowq[0:1, TC:2 * TC], ps_a[0:1, SQc:SQc + TC], 1.0 / D, None, ALU.mult)
            e.tensor_tensor(rowq[0:1, 2 * TC:3 * TC], rowq[0:1, 0:TC], rowq[0:1, 0:TC], ALU.mult)
            return e.tensor_tensor(rowq[0:1, TC:2 * TC], rowq[0:1, TC:2 * TC],
                                   rowq[0:1, 2 * TC:3 * TC], ALU.subtract)
        rv1 = P.do("vector", rowv1, waits=((mm, xm),), inc=vv)
        rv1_prev = rv1
        def rows1(e):
            e.activation(rowq[0:1, 2 * TC:3 * TC], rowq[0:1, TC:2 * TC], AF.Ln,
                         bias=ctx.constcol(1e-5)[0:1, :])
            return e.activation(rowq[0:1, 2 * TC:3 * TC], rowq[0:1, 2 * TC:3 * TC],
                                AF.Exp, scale=-0.5)
        rs1 = P.do("scalar", rows1, waits=((vv, rv1),), inc=ss)
        rv2 = P.do("vector", lambda e: e.scalar_tensor_tensor(
            rowq[0:1, 3 * TC:4 * TC], rowq[0:1, 0:TC], -1.0,
            rowq[0:1, 2 * TC:3 * TC], ALU.mult, ALU.mult), waits=((ss, rs1),), inc=vv)
        def xbcast(e):
            e.matmul(ps_b[:, B1:B1 + TC], ones_row[:], rowq[0:1, 2 * TC:3 * TC],
                     start=True, stop=True)
            return e.matmul(ps_b[:, B2:B2 + TC], ones_row[:], rowq[0:1, 3 * TC:4 * TC],
                            start=True, stop=True)
        xbm = P.do("tensor", xbcast, waits=((vv, rv2),), inc=mm)
        def xnorm(e):
            for a in range(DT):
                e.tensor_tensor(sq[:, a * TC:(a + 1) * TC], xt[:, a * TC:(a + 1) * TC],
                                ps_b[:, B1:B1 + TC], ALU.mult)
                last = e.tensor_tensor(xnb[:, a * TC:(a + 1) * TC],
                                       sq[:, a * TC:(a + 1) * TC],
                                       ps_b[:, B2:B2 + TC], ALU.add)
            return last
        xn = P.do("vector", xnorm, waits=((mm, xbm),), inc=vv)

        ipm = []
        for f in range(2 * DT):
            fo = (f % 2) * 512
            w = [(vv, xn)]
            if c == 0 and f == 0:
                w += [wl1, (vv, z0)]
            if f >= 2:
                w.append((ss, ipm[f - 2]))
            elif c > 0 and gate_prev:
                w.append((vv, gate_prev[-1]))
            def ipmm(e, f=f, fo=fo):
                for a in range(DT):
                    last = e.matmul(ps_y[:, fo:fo + TC],
                                    ipw[:, a * 2048 + f * 128:a * 2048 + (f + 1) * 128],
                                    xnb[:, a * TC:(a + 1) * TC],
                                    start=(a == 0), stop=(a == DT - 1))
                return last
            um = P.do("tensor", ipmm, waits=w, inc=mm)
            if f < DT:
                ev = P.do("scalar", lambda e, f=f, fo=fo: e.activation(
                    xin[:, f * (TC + 4) + 4:f * (TC + 4) + 4 + TC],
                    ps_y[:, fo:fo + TC], AF.Copy), waits=((mm, um),), inc=ss)
            else:
                ev = P.do("scalar", lambda e, f=f, fo=fo: e.activation(
                    zs[:, (f - DT) * TC:(f - DT + 1) * TC],
                    ps_y[:, fo:fo + TC], AF.Silu), waits=((mm, um),), inc=ss)
            ipm.append(ev)

        def conv_v(e):
            for a in range(DT):
                o = a * (TC + 4)
                e.tensor_scalar(xcs[:, a * TC:(a + 1) * TC],
                                xin[:, o + 1:o + 1 + TC], cwt[:, a * 4 + 0:a * 4 + 1],
                                None, ALU.mult)
                for k in (1, 2):
                    e.scalar_tensor_tensor(xcs[:, a * TC:(a + 1) * TC],
                                           xin[:, o + 1 + k:o + 1 + k + TC],
                                           cwt[:, a * 4 + k:a * 4 + k + 1],
                                           xcs[:, a * TC:(a + 1) * TC],
                                           ALU.mult, ALU.add)
                last = e.scalar_tensor_tensor(xcs[:, a * TC:(a + 1) * TC],
                                              xin[:, o + 4:o + 4 + TC],
                                              cwt[:, a * 4 + 3:a * 4 + 4],
                                              xcs[:, a * TC:(a + 1) * TC],
                                              ALU.mult, ALU.add)
            return last
        cv = P.do("vector", conv_v, waits=((ss, ipm[DT - 1]), wl6), inc=vv)
        def halo_v(e):
            for a in range(DT):
                o = a * (TC + 4)
                last = e.tensor_copy(xin[:, o + 1:o + 4], xin[:, o + 1 + TC:o + 4 + TC])
            return last
        hv = P.do("vector", halo_v, inc=vv)
        def silu_s(e):
            for a in range(DT):
                e.activation(xcs[:, a * TC:(a + 1) * TC], xcs[:, a * TC:(a + 1) * TC],
                             AF.Silu, bias=cbt[:, a:a + 1])
                last = e.activation(xcb[:, a * TC:(a + 1) * TC],
                                    xcs[:, a * TC:(a + 1) * TC], AF.Copy)
            return last
        cs = P.do("scalar", silu_s, waits=((vv, cv), wl7), inc=ss)

        def spmm(e):
            for a in range(DT):
                e.matmul(ps_a[0:16, 0:TC], xpB[:, a * NST:(a + 1) * NST],
                         xcb[:, a * TC:(a + 1) * TC], start=(a == 0), stop=(a == DT - 1))
            for a in range(DT):
                e.matmul(ps_a[0:16, 512:512 + TC], xpC[:, a * NST:(a + 1) * NST],
                         xcb[:, a * TC:(a + 1) * TC], start=(a == 0), stop=(a == DT - 1))
            for a in range(DT):
                last = e.matmul(ps_b[0:1, B1:B1 + TC], xpD[:, a:a + 1],
                                xcb[:, a * TC:(a + 1) * TC], start=(a == 0), stop=(a == DT - 1))
            return last
        spm = P.do("tensor", spmm,
                   waits=((ss, cs), wl3, wl4, wl5, (vv, rv1), (vv, xn)), inc=mm)

        spe = P.do("vector", lambda e: (
            e.tensor_copy(spB[:], ps_a[0:16, 0:TC]),
            e.tensor_copy(spC[:], ps_a[0:16, 512:512 + TC]),
            e.tensor_copy(dtrow[:], ps_b[0:1, B1:B1 + TC]))[-1],
            waits=((mm, spm),), inc=vv)

        fb0 = SH.issue(P, "sync", lambda e: e.dma_start(out=io["scrB"][:], in_=spB[:]),
                       waits=((vv, spe),))
        fc0 = SH.issue(P, "sync", lambda e: e.dma_start(out=io["scrC"][:], in_=spC[:]),
                       waits=((vv, spe),))
        fb = SH.issue(P, "sync", lambda e: e.dma_start(
            out=spfB[:], in_=io["scrB"].rearrange("n t -> (n t)").unsqueeze(0)),
            waits=(fb0,))
        fc = SH.issue(P, "sync", lambda e: e.dma_start(
            out=spfC[:], in_=io["scrC"].rearrange("n t -> (n t)").unsqueeze(0)),
            waits=(fc0,))

        dbm = P.do("tensor", lambda e: e.matmul(
            ps_b[:, B2:B2 + TC], ones_row[:], dtrow[:], start=True, stop=True),
            waits=((vv, spe), (vv, xn)), inc=mm)
        def dtv(e):
            for a in range(DT):
                last = e.tensor_scalar(dtt[:, a * TC:(a + 1) * TC], ps_b[:, B2:B2 + TC],
                                       dtwv[:, a:a + 1], dtbv[:, a:a + 1],
                                       ALU.mult, ALU.add)
            return last
        dv = P.do("vector", dtv, waits=((mm, dbm), wl8, wl9), inc=vv)
        def softp(e):
            for a in range(DT):
                sl = slice(a * TC, (a + 1) * TC)
                o = a * (TC + 4)
                e.activation(sq[:, sl], dtt[:, sl], AF.Abs)
                e.activation(sq[:, sl], sq[:, sl], AF.Exp, scale=-1.0)
                e.activation(sq[:, sl], sq[:, sl], AF.Ln,
                             bias=ctx.constcol(1.0))
                last = e.activation(xin[:, o + 4:o + 4 + TC], dtt[:, sl], AF.Relu)
            return last
        sp2 = P.do("scalar", softp, waits=((vv, dv), (vv, hv)), inc=ss)
        def dtfin(e):
            for a in range(DT):
                sl = slice(a * TC, (a + 1) * TC)
                o = a * (TC + 4)
                e.tensor_tensor(dtt[:, sl], sq[:, sl], xin[:, o + 4:o + 4 + TC], ALU.add)
                last = e.tensor_tensor(xcs[:, sl], dtt[:, sl], xcs[:, sl], ALU.mult)
            return last
        gv = P.do("vector", dtfin, waits=((ss, sp2),), inc=vv)

        bc_ms = []
        for n in range(NST):
            po = (n % 2) * 512
            w = [fb, fc, cl, (vv, dv)]
            if n >= 2:
                w.append((vv, bc_ms[n - 2]))
            def bcmm(e, n=n, po=po):
                e.matmul(ps_a[:, po:po + TC], ones_rb[:],
                         spfB[0:1, n * TC:(n + 1) * TC], start=True, stop=True)
                return e.matmul(ps_b[:, po:po + TC], ones_rb[:],
                                spfC[0:1, n * TC:(n + 1) * TC], start=True, stop=True)
            bm = P.do("tensor", bcmm, waits=w, inc=mm)
            bv = P.do("vector", lambda e, n=n, po=po: (
                e.tensor_copy(Bbc[:, n * TC:(n + 1) * TC], ps_a[:, po:po + TC]),
                e.tensor_copy(Cbc[:, n * TC:(n + 1) * TC], ps_b[:, po:po + TC]))[-1],
                waits=((mm, bm),), inc=vv)
            bc_ms.append(bv)

        new_gate = []
        for a in range(DT):
            for n in range(NST):
                idx = a * NST + n
                p = idx % 2
                po = p * TC
                sl = slice(a * TC, (a + 1) * TC)
                ew = [(vv, gv), (vv, bc_ms[-1])]
                if (c, idx) in scan_ms or True:
                    prev = scan_ms.get((c, idx - 2)) or (scan_ms.get((c - 1, 126 + idx)) if idx < 2 else None)
                    if prev:
                        ew.append((vv, prev))
                se = P.do("scalar", lambda e, sl=sl, po=po, a=a, n=n: e.activation(
                    dab[:, po:po + TC], dtt[:, sl], AF.Exp,
                    scale=amat[:, a * NST + n:a * NST + n + 1]), waits=ew, inc=ss)
                def scanv(e, sl=sl, po=po, a=a, n=n, idx=idx):
                    e.tensor_tensor(dxb[:, po:po + TC], xcs[:, sl],
                                    Bbc[:, n * TC:(n + 1) * TC], ALU.mult)
                    e.tensor_tensor_scan(hbuf[:, po:po + TC], dab[:, po:po + TC],
                                         dxb[:, po:po + TC],
                                         state[:, idx:idx + 1], ALU.mult, ALU.add)
                    e.tensor_copy(state[:, idx:idx + 1], hbuf[:, po + TC - 1:po + TC])
                    return e.tensor_tensor(tmpb[:, po:po + TC], hbuf[:, po:po + TC],
                                           Cbc[:, n * TC:(n + 1) * TC], ALU.mult)
                tw = [(ss, se)]
                pm = ymm_ms.get((c, idx - 2)) or (ymm_ms.get((c - 1, 126 + idx)) if idx < 2 else None)
                if pm:
                    tw.append((mm, pm))
                sc = P.do("vector", scanv, waits=tw, inc=vv)
                scan_ms[(c, idx)] = sc
                yw = [(vv, sc)]
                if n == 0:
                    if a >= 2:
                        yw.append((vv, new_gate[a - 2]))
                    elif gate_prev:
                        yw.append((vv, gate_prev[a - 2]))
                    yw.append((ss, ipm[-1]))
                ym = P.do("tensor", lambda e, a=a, n=n, po=po: e.matmul(
                    ps_y[:, (a % 2) * 512:(a % 2) * 512 + TC], identB[:],
                    tmpb[:, po:po + TC], start=(n == 0), stop=(n == NST - 1)),
                    waits=yw, inc=mm)
                ymm_ms[(c, idx)] = ym
            ga = P.do("vector", lambda e, a=a: e.tensor_tensor(
                ygb[:, a * TC:(a + 1) * TC], ps_y[:, (a % 2) * 512:(a % 2) * 512 + TC],
                zs[:, a * TC:(a + 1) * TC], ALU.mult),
                waits=((mm, ymm_ms[(c, a * NST + NST - 1)]),), inc=vv)
            new_gate.append(ga)
        gate_prev = new_gate

        op_ms = []
        ostores = []
        for dto in range(DT):
            w = [(vv, new_gate[-1])]
            if c == 0 and dto == 0:
                w.append(wl2)
            if dto >= 2:
                w.append((vv, op_ms[dto - 2]))
            def opmm(e, dto=dto):
                for a in range(DT):
                    last = e.matmul(ps_o[:, (dto % 2) * 512:(dto % 2) * 512 + TC],
                                    opw[:, a * 1024 + dto * 128:a * 1024 + (dto + 1) * 128],
                                    ygb[:, a * TC:(a + 1) * TC],
                                    start=(a == 0), stop=(a == DT - 1))
                return last
            om = P.do("tensor", opmm, waits=w, inc=mm)
            ow = [(mm, om)]
            if dto >= 2:
                ow.append(ostores[dto - 2])
            oe = P.do("vector", lambda e, dto=dto: e.tensor_tensor(
                yout[:, (dto % 2) * TC:(dto % 2) * TC + TC],
                ps_o[:, (dto % 2) * 512:(dto % 2) * 512 + TC],
                xt[:, dto * TC:(dto + 1) * TC], ALU.add),
                waits=ow, inc=vv)
            op_ms.append(oe)
            osr = STP.issue(P, "sync", lambda e, c=c, dto=dto: e.dma_start(
                out=io["yT"][dto * 128:(dto + 1) * 128, c * TC:(c + 1) * TC],
                in_=yout[:, (dto % 2) * TC:(dto % 2) * TC + TC]),
                waits=((vv, oe),))
            ostores.append(osr)
        ye = op_ms[-1]
        ye_prev = ye
        store_ms.append(ostores[-1])
        prev_stores = [ostores[-3], ostores[-2], ostores[-1]]

    drain_pools(P, ctx)
    return store_ms


def plan_attn_a(P, ctx, io, uid="a"):
    TC = 512
    NCH = T // TC
    NH = 8
    SCALE = 1.0 / 8.0
    EG = 4

    ones_row = ctx.sbuf("ones_row", [1, 128], F32)
    ones_col = ctx.sbuf("ones_col", [128, 1], F32)
    cpool = ctx.sbuf("cpool", [128, 16], F32)
    halfones = ctx.sbuf(uid + "halfones", [128, 2], F32)

    mq = ctx.sbuf(uid + "mq", [128, DT * 512], BF16)
    mk = ctx.sbuf(uid + "mk", [128, DT * 512], BF16)
    mv = ctx.sbuf(uid + "mv", [128, DT * 1024], BF16)
    xt = ctx.sbuf(uid + "xt", [128, DT * TC], F32)
    sq = ctx.sbuf(uid + "sq", [128, DT * TC], F32)
    xnb = ctx.sbuf(uid + "xnb", [128, DT * T], BF16)
    rowq = ctx.sbuf(uid + "rowq", [1, 4 * TC], F32)
    QB = ctx.sbuf(uid + "QB", [128, 4 * 512], BF16)
    KB = ctx.sbuf(uid + "KB", [128, 4 * 512], BF16)
    QBo = ctx.sbuf(uid + "QBo", [64, 4 * 512], BF16)
    KBo = ctx.sbuf(uid + "KBo", [64, 4 * 512], BF16)
    VB = ctx.sbuf(uid + "VB", [128, 4 * 1024], BF16)
    Ebuf = ctx.sbuf(uid + "Ebuf", [128, EG * T], BF16)
    zpart = ctx.sbuf(uid + "zpart", [128, 16], F32)
    vn = ctx.sbuf(uid + "vn", [128, EG * 128], BF16)
    bscr = ctx.sbuf(uid + "bscr", [1, 16], F32)
    bcol = ctx.sbuf(uid + "bcol", [128, 1], F32)
    acc = xt

    ps_s = ctx.psum(uid + "ps_s", [128, 2 * 512], F32)
    ps_o = ctx.psum(uid + "ps_o", [128, 2 * 512], F32)
    ps_m = ctx.psum(uid + "ps_m", [128, 1024], F32)
    ps_p = ctx.psum(uid + "ps_p", [128, 2 * 512], F32)

    mm = ctx.sem("mm"); vv = ctx.sem("vv"); ss = ctx.sem("ss")
    LD = ctx.pool("ld", 6)
    WG = ctx.pool("wg", 3)
    STP = ctx.pool("st", 3)

    LD.issue(P, "sync", lambda e: e.dma_start(out=cpool[:], in_=io["cpool"][:]))
    LD.issue(P, "sync", lambda e: e.dma_start(out=ones_row[:], in_=io["ones_row"][:]))
    LD.issue(P, "sync", lambda e: e.dma_start(out=ones_col[:], in_=io["ones_col"][:]))
    hl = LD.issue(P, "sync", lambda e: e.dma_start(out=halfones[:], in_=io["halfones"][:]))
    wq = WG.issue(P, "gpsimd", lambda e: e.dma_start(
        out=sb3(mq[:], DT), in_=io["Mq"].rearrange("(a p) f -> p a f", p=128)))
    wk = WG.issue(P, "gpsimd", lambda e: e.dma_start(
        out=sb3(mk[:], DT), in_=io["Mk"].rearrange("(a p) f -> p a f", p=128)))
    wv = WG.issue(P, "gpsimd", lambda e: e.dma_start(
        out=sb3(mv[:], DT), in_=io["Mv"].rearrange("(a p) f -> p a f", p=128)))

    rv1_prev = None
    xn_ms = []
    for c in range(NCH):
        xl = LD.issue(P, "sync", lambda e, c=c: e.dma_start(
            out=sb3(xt[:], DT), in_=rearr_dT(io["xT"], c * TC, (c + 1) * TC)),
            waits=[(vv, xn_ms[-1])] if xn_ms else ())
        sv = P.do("scalar", lambda e: e.activation(sq[:], xt[:], AF.Square),
                  waits=(xl,) + (((vv, xn_ms[-1]),) if xn_ms else ()), inc=ss)
        def xstat(e):
            for a in range(DT):
                e.matmul(ps_m[0:1, 0:TC], ones_col[:],
                         xt[:, a * TC:(a + 1) * TC], start=(a == 0), stop=(a == DT - 1))
            for a in range(DT):
                last = e.matmul(ps_m[0:1, 512:512 + TC], ones_col[:],
                                sq[:, a * TC:(a + 1) * TC], start=(a == 0), stop=(a == DT - 1))
            return last
        xw = [(ss, sv)]
        if rv1_prev is not None:
            xw.append((vv, rv1_prev))
        xm = P.do("tensor", xstat, waits=xw, inc=mm)
        def rowv1(e):
            e.tensor_scalar(rowq[0:1, 0:TC], ps_m[0:1, 0:TC], 1.0 / D, None, ALU.mult)
            e.tensor_scalar(rowq[0:1, TC:2 * TC], ps_m[0:1, 512:512 + TC], 1.0 / D, None, ALU.mult)
            e.tensor_tensor(rowq[0:1, 2 * TC:3 * TC], rowq[0:1, 0:TC], rowq[0:1, 0:TC], ALU.mult)
            return e.tensor_tensor(rowq[0:1, TC:2 * TC], rowq[0:1, TC:2 * TC],
                                   rowq[0:1, 2 * TC:3 * TC], ALU.subtract)
        rv1 = P.do("vector", rowv1, waits=((mm, xm),), inc=vv)
        rv1_prev = rv1
        def rows1(e):
            e.activation(rowq[0:1, 2 * TC:3 * TC], rowq[0:1, TC:2 * TC], AF.Ln,
                         bias=ctx.constcol(1e-5)[0:1, :])
            return e.activation(rowq[0:1, 2 * TC:3 * TC], rowq[0:1, 2 * TC:3 * TC],
                                AF.Exp, scale=-0.5)
        rs1 = P.do("scalar", rows1, waits=((vv, rv1),), inc=ss)
        rv2 = P.do("vector", lambda e: e.scalar_tensor_tensor(
            rowq[0:1, 3 * TC:4 * TC], rowq[0:1, 0:TC], -1.0,
            rowq[0:1, 2 * TC:3 * TC], ALU.mult, ALU.mult), waits=((ss, rs1),), inc=vv)
        def xbcast(e):
            e.matmul(ps_m[:, 0:TC], ones_row[:], rowq[0:1, 2 * TC:3 * TC],
                     start=True, stop=True)
            return e.matmul(ps_m[:, 512:512 + TC], ones_row[:], rowq[0:1, 3 * TC:4 * TC],
                            start=True, stop=True)
        xbm = P.do("tensor", xbcast, waits=((vv, rv2),), inc=mm)
        def xnorm(e, c=c):
            for a in range(DT):
                e.tensor_tensor(sq[:, a * TC:(a + 1) * TC], xt[:, a * TC:(a + 1) * TC],
                                ps_m[:, 0:TC], ALU.mult)
                last = e.tensor_tensor(xnb[:, a * T + c * TC:a * T + (c + 1) * TC],
                                       sq[:, a * TC:(a + 1) * TC],
                                       ps_m[:, 512:512 + TC], ALU.add)
            return last
        xn = P.do("vector", xnorm, waits=((mm, xbm),), inc=vv)
        xn_ms.append(xn)
    ln_done = xn_ms[-1]

    prev_head_store = []
    prev_pv_mm = None
    prev_vev_last = None
    prev_ra_last = None
    for h in range(NH):
        hb = h * TC

        pev = []
        for t in range(4):
            po = (t % 2) * 512
            w = [(vv, ln_done), wq]
            if t >= 2:
                w.append((ss, pev[t - 2]))
            else:
                if prev_pv_mm is not None:
                    w.append((mm, prev_pv_mm))
                if prev_vev_last is not None:
                    w.append((ss, prev_vev_last))
                if prev_ra_last is not None:
                    w.append((vv, prev_ra_last))
            def qmm(e, t=t, po=po, hb=hb):
                for a in range(DT):
                    last = e.matmul(ps_p[:, po:po + 512],
                                    mq[:, a * 512 + t * 128:a * 512 + (t + 1) * 128],
                                    xnb[:, a * T + hb:a * T + hb + TC],
                                    start=(a == 0), stop=(a == DT - 1))
                return last
            qm = P.do("tensor", qmm, waits=w, inc=mm)
            qe = P.do("scalar", lambda e, t=t, po=po: e.activation(
                QB[:, t * 512:(t + 1) * 512], ps_p[:, po:po + 512], AF.Copy,
                scale=SCALE), waits=((mm, qm),), inc=ss)
            pev.append(qe)
        for t in range(4):
            po = (t % 2) * 512
            w = [wk, (ss, pev[2 + t])]
            def kmm(e, t=t, po=po, hb=hb):
                for a in range(DT):
                    last = e.matmul(ps_p[:, po:po + 512],
                                    mk[:, a * 512 + t * 128:a * 512 + (t + 1) * 128],
                                    xnb[:, a * T + hb:a * T + hb + TC],
                                    start=(a == 0), stop=(a == DT - 1))
                return last
            km = P.do("tensor", kmm, waits=w, inc=mm)
            ke = P.do("scalar", lambda e, t=t, po=po: e.activation(
                KB[:, t * 512:(t + 1) * 512], ps_p[:, po:po + 512], AF.Copy),
                waits=((mm, km),), inc=ss)
            pev.append(ke)
        for idx in range(8):
            t, nchunk = idx // 2, idx % 2
            po = (idx % 2) * 512
            w = [wv, (ss, pev[6 + idx])]
            def vmm(e, t=t, nchunk=nchunk, po=po, hb=hb):
                for a in range(DT):
                    last = e.matmul(
                        ps_p[:, po:po + 512],
                        xnb[:, a * T + hb + t * 128:a * T + hb + (t + 1) * 128],
                        mv[:, a * 1024 + nchunk * 512:a * 1024 + (nchunk + 1) * 512],
                        start=(a == 0), stop=(a == DT - 1))
                return last
            vm = P.do("tensor", vmm, waits=w, inc=mm)
            ve = P.do("scalar", lambda e, t=t, nchunk=nchunk, po=po: e.activation(
                VB[:, t * 1024 + nchunk * 512:t * 1024 + (nchunk + 1) * 512],
                ps_p[:, po:po + 512], AF.Copy), waits=((mm, vm),), inc=ss)
            pev.append(ve)
        prev_vev_last = pev[-1]
        qo = STP.issue(P, "sync", lambda e: e.dma_start(
            out=QBo[:], in_=QB[64:128, :]), waits=((ss, pev[3]),))
        ko = STP.issue(P, "sync", lambda e: e.dma_start(
            out=KBo[:], in_=KB[64:128, :]), waits=((ss, pev[7]),))

        mxv = []
        for wi, buf in ((0, QB), (1, KB)):
            sqw = [(ss, pev[-1])]
            if wi == 1:
                sqw.append((vv, mxv[3]))
            sq2 = P.do("scalar", lambda e, buf=buf: e.activation(
                sq[:, 0:2048], buf[:], AF.Square), waits=sqw, inc=ss)
            for t in range(4):
                nm = P.do("tensor", lambda e, t=t: e.matmul(
                    ps_m[0:2, (t % 2) * 512:(t % 2) * 512 + 512], halfones[:],
                    sq[:, t * 512:(t + 1) * 512], start=True, stop=True),
                    waits=((ss, sq2), hl) + (((vv, mxv[-2]),) if len(mxv) >= 2 else ()),
                    inc=mm)
                rv = P.do("vector", lambda e, t=t, wi=wi: e.tensor_reduce(
                    zpart[0:2, wi * 4 + t:wi * 4 + t + 1],
                    ps_m[0:2, (t % 2) * 512:(t % 2) * 512 + 512], AX, ALU.max),
                    waits=((mm, nm),), inc=vv)
                mxv.append(rv)
        cm = P.do("vector", lambda e: (
            e.tensor_reduce(zpart[0:2, 8:9], zpart[0:2, 0:4], AX, ALU.max),
            e.tensor_reduce(zpart[0:2, 9:10], zpart[0:2, 4:8], AX, ALU.max))[-1],
            waits=((vv, mxv[-1]),), inc=vv)
        db1 = STP.issue(P, "sync", lambda e: e.dma_start(
            out=io["scrA"][:], in_=zpart[0:2, 8:10]), waits=((vv, cm),))
        db2 = STP.issue(P, "sync", lambda e: e.dma_start(
            out=bscr[0:1, 0:4], in_=io["scrA"].rearrange("p o -> (p o)").unsqueeze(0)),
            waits=(db1,))
        bv1 = P.do("vector", lambda e: e.tensor_tensor(
            bscr[0:1, 4:6], bscr[0:1, 0:2], bscr[0:1, 2:4], ALU.max),
            waits=(db2,), inc=vv)
        bs1 = P.do("scalar", lambda e: e.activation(
            bscr[0:1, 6:8], bscr[0:1, 4:6], AF.Sqrt), waits=((vv, bv1),), inc=ss)
        bv2 = P.do("vector", lambda e: e.scalar_tensor_tensor(
            bscr[0:1, 8:9], bscr[0:1, 6:7], -1.0, bscr[0:1, 7:8],
            ALU.mult, ALU.mult), waits=((ss, bs1),), inc=vv)
        bb1 = P.do("tensor", lambda e: e.matmul(
            ps_m[:, 0:1], ones_row[:], bscr[0:1, 8:9], start=True, stop=True),
            waits=((vv, bv2),), inc=mm)
        bb2 = P.do("vector", lambda e: e.tensor_copy(bcol[:], ps_m[:, 0:1]),
                   waits=((mm, bb1),), inc=vv)
        if "dbga_bcol" in io:
            STP.issue(P, "sync", lambda e, h=h: e.dma_start(
                out=io["dbga_bcol"][h], in_=bcol[:]), waits=((vv, bb2),))
            STP.issue(P, "sync", lambda e, h=h: e.dma_start(
                out=io["dbga_zp"][h], in_=zpart[0:2, 0:16]), waits=((vv, bb2),))
            STP.issue(P, "sync", lambda e, h=h: e.dma_start(
                out=io["dbga_qb"][h], in_=QB[:]), waits=((ss, pev[3]),))
            STP.issue(P, "sync", lambda e, h=h: e.dma_start(
                out=io["dbga_kb"][h], in_=KB[:]), waits=((ss, pev[7]),))

        exp_ms = {}
        pa_ms = []
        zr_prev = None
        last_pm_group = None
        for g in range(8):
            vn_ms = []
            for jj in range(EG):
                jt = g * EG + jj
                r_j, ut_j = jt // 4, jt % 4
                for ic in range(8):
                    po = (ic % 2) * 512
                    w = []
                    if (jt, ic - 2) in exp_ms:
                        w.append((ss, exp_ms[(jt, ic - 2)]))
                    elif (jt - 1, 6 + ic) in exp_ms:
                        w.append((ss, exp_ms[(jt - 1, 6 + ic)]))
                    if jj == 0 and ic == 0 and last_pm_group is not None:
                        w.append((mm, last_pm_group))
                    def smm(e, r_j=r_j, ut_j=ut_j, ic=ic, po=po):
                        ksrc = KB if r_j % 2 == 0 else KBo
                        kb_sl = ksrc[0:64,
                                     (r_j // 2) * 512 + ut_j * 128:
                                     (r_j // 2) * 512 + (ut_j + 1) * 128]
                        qsrc = QB if ic % 2 == 0 else QBo
                        qb_sl = qsrc[0:64, (ic // 2) * 512:(ic // 2) * 512 + 512]
                        return e.matmul(ps_s[:, po:po + 512], kb_sl, qb_sl,
                                        start=True, stop=True)
                    if jt == 0 and ic == 0:
                        w += [qo, ko]
                    sm = P.do("tensor", smm, waits=w, inc=mm)
                    ew = [(mm, sm), (vv, bb2)]
                    if ic == 0 and zr_prev is not None:
                        ew.append((vv, zr_prev))
                    ex = P.do("scalar", lambda e, jj=jj, ic=ic, po=po: e.activation(
                        Ebuf[:, jj * T + ic * 512:jj * T + (ic + 1) * 512],
                        ps_s[:, po:po + 512], AF.Exp, bias=bcol[:],
                        accum_out=zpart[:, ic:ic + 1]),
                        waits=ew, inc=ss)
                    exp_ms[(jt, ic)] = ex
                zr1 = P.do("vector", lambda e, jj=jj: e.tensor_reduce(
                    zpart[:, 8 + jj:9 + jj], zpart[:, 0:8], AX, ALU.add),
                    waits=((ss, exp_ms[(jt, 7)]),), inc=vv)
                zs1 = P.do("scalar", lambda e, jj=jj: (
                    e.activation(zpart[:, 8 + jj:9 + jj], zpart[:, 8 + jj:9 + jj],
                                 AF.Ln),
                    e.activation(zpart[:, 8 + jj:9 + jj], zpart[:, 8 + jj:9 + jj],
                                 AF.Exp, scale=-1.0))[-1],
                    waits=((vv, zr1),), inc=ss)
                def zvn(e, jj=jj, r_j=r_j, ut_j=ut_j):
                    vo = ut_j * 1024 + r_j * 128
                    return e.tensor_scalar(vn[:, jj * 128:(jj + 1) * 128],
                                           VB[:, vo:vo + 128],
                                           zpart[:, 8 + jj:9 + jj], None, ALU.mult)
                zr = P.do("vector", zvn, waits=((ss, zs1),), inc=vv)
                zr_prev = zr
                vn_ms.append(zr)
            for ic in range(8):
                po = (ic % 2) * 512
                def pvmm(e, ic=ic, po=po):
                    for jj in range(EG):
                        last = e.matmul(ps_o[:, po:po + 512],
                                        vn[:, jj * 128:(jj + 1) * 128],
                                        Ebuf[:, jj * T + ic * 512:jj * T + (ic + 1) * 512],
                                        start=(jj == 0), stop=(jj == EG - 1))
                    return last
                w = [(vv, vn_ms[-1])]
                if len(pa_ms) >= 2:
                    w.append((vv, pa_ms[-2]))
                pm = P.do("tensor", pvmm, waits=w, inc=mm)
                if ic == 7:
                    last_pm_group = mm.n
                paw = [(mm, pm)]
                if g == 0 and prev_head_store:
                    paw.append(prev_head_store[ic])
                pa = P.do("vector", lambda e, ic=ic, po=po, g=g:
                          (e.tensor_copy(acc[:, ic * 512:(ic + 1) * 512],
                                         ps_o[:, po:po + 512]) if g == 0 else
                           e.tensor_tensor(acc[:, ic * 512:(ic + 1) * 512],
                                           acc[:, ic * 512:(ic + 1) * 512],
                                           ps_o[:, po:po + 512], ALU.add)),
                          waits=paw, inc=vv)
                pa_ms.append(pa)
        prev_pv_mm = mm.n

        ra_last = None
        phs = list(prev_head_store)
        for ic in range(8):
            rlw = []
            if phs:
                rlw.append(phs[ic])
            rl = LD.issue(P, "sync", lambda e, ic=ic, hb=hb: e.dma_start(
                out=sq[:, ic * 512:ic * 512 + 512],
                in_=io["xT"][ic * 128:(ic + 1) * 128, hb:hb + TC]), waits=rlw)
            ra = P.do("vector", lambda e, ic=ic: e.tensor_tensor(
                acc[:, ic * 512:(ic + 1) * 512], acc[:, ic * 512:(ic + 1) * 512],
                sq[:, ic * 512:ic * 512 + 512], ALU.add),
                waits=(rl, (vv, pa_ms[-1])), inc=vv)
            sr = STP.issue(P, "sync", lambda e, ic=ic, hb=hb: e.dma_start(
                out=io["yT"][ic * 128:(ic + 1) * 128, hb:hb + TC],
                in_=acc[:, ic * 512:(ic + 1) * 512]), waits=((vv, ra),))
            ra_last = ra
            if len(prev_head_store) <= ic:
                prev_head_store.append(sr)
            else:
                prev_head_store[ic] = sr
        prev_ra_last = ra_last

    drain_pools(P, ctx)
    return prev_head_store


def plan_mla(P, ctx, io, uid="l"):
    TC = 512
    NCH = T // TC
    NHD = 16
    SCL = 1.0 / math.sqrt(96.0)

    ones_row = ctx.sbuf("ones_row", [1, 128], F32)
    ones_col = ctx.sbuf("ones_col", [128, 1], F32)
    cpool = ctx.sbuf("cpool", [128, 16], F32)

    wkv = ctx.sbuf(uid + "wkv", [128, DT * 128], BF16)
    wkup = ctx.sbuf(uid + "wkup", [128, 1024], BF16)
    wvup = ctx.sbuf(uid + "wvup", [128, 1024], BF16)
    wqh = ctx.sbuf(uid + "wqh", [128, 2 * 512], BF16)
    wqr = ctx.sbuf(uid + "wqr", [128, DT * 32], BF16)
    wkr = ctx.sbuf(uid + "wkr", [128, 32], BF16)
    wod = ctx.sbuf(uid + "wod", [64, 2 * 2048], BF16)
    cosT = ctx.sbuf(uid + "cosT", [16, T], F32)
    sinT = ctx.sbuf(uid + "sinT", [16, T], F32)
    strip = ctx.sbuf(uid + "strip", [128, 1024], BF16)

    xt = ctx.sbuf(uid + "xt", [128, DT * TC], F32)
    sq = ctx.sbuf(uid + "sq", [128, DT * TC], F32)
    xnb = ctx.sbuf(uid + "xnb", [128, DT * T], BF16)
    rowq = ctx.sbuf(uid + "rowq", [1, 4 * TC], F32)
    ckv = ctx.sbuf(uid + "ckv", [128, T], BF16)
    ta = ctx.sbuf(uid + "ta", [16, 512], F32)
    tb = ctx.sbuf(uid + "tb", [16, 512], F32)
    tc_ = ctx.sbuf(uid + "tc", [16, 512], F32)
    kqch = ctx.sbuf(uid + "kqch", [16, 4 * 512], BF16)
    kcat = ctx.sbuf(uid + "kcat", [96, T], BF16)
    qcat = ctx.sbuf(uid + "qcat", [96, T], BF16)
    vaug = ctx.sbuf(uid + "vaug", [128, 32 * 65], BF16)
    ebuf = ctx.sbuf(uid + "ebuf", [128, 2 * 512], BF16)
    zbc = ctx.sbuf(uid + "zbc", [64, 512], F32)
    anst = ctx.sbuf(uid + "anst", [64, 2 * 512], BF16)
    anL = xt[0:64, :].bitcast(BF16)
    nrm = ctx.sbuf(uid + "nrm", [1, 512], F32)
    bscr = ctx.sbuf(uid + "bscr", [1, 16], F32)
    bcol = ctx.sbuf(uid + "bcol", [128, 1], F32)
    yout = ctx.sbuf(uid + "yout", [128, 2 * 512], F32)

    ps_p = ctx.psum(uid + "ps_p", [128, 2 * 512], F32)
    ps_s = ctx.psum(uid + "ps_s", [128, 2 * 512], F32)
    ps_at = ctx.psum(uid + "ps_at", [128, 2 * 512], F32)
    ps_m = ctx.psum(uid + "ps_m", [128, 1024], F32)

    mm = ctx.sem("mm"); vv = ctx.sem("vv"); ss = ctx.sem("ss")
    LD = ctx.pool("ld", 6)
    WG = ctx.pool("wg", 3)
    STP = ctx.pool("st", 3)
    SH = ctx.pool("sh", 2)

    LD.issue(P, "sync", lambda e: e.dma_start(out=cpool[:], in_=io["cpool"][:]))
    LD.issue(P, "sync", lambda e: e.dma_start(out=ones_row[:], in_=io["ones_row"][:]))
    LD.issue(P, "sync", lambda e: e.dma_start(out=ones_col[:], in_=io["ones_col"][:]))
    w1 = WG.issue(P, "gpsimd", lambda e: e.dma_start(
        out=sb3(wkv[:], DT), in_=io["Wkv"].rearrange("(a p) f -> p a f", p=128)))
    w2 = WG.issue(P, "gpsimd", lambda e: e.dma_start(out=wkup[:], in_=io["Wkup"][:]))
    w3 = WG.issue(P, "gpsimd", lambda e: e.dma_start(out=wvup[:], in_=io["Wvup"][:]))
    w5 = WG.issue(P, "gpsimd", lambda e: e.dma_start(
        out=sb3(wqr[:], DT), in_=io["Wqr"].rearrange("(a p) f -> p a f", p=128)))
    w6 = WG.issue(P, "gpsimd", lambda e: e.dma_start(out=wkr[:], in_=io["Wkr"][:]))
    c1 = LD.issue(P, "sync", lambda e: e.dma_start(out=cosT[:], in_=io["cosT"][:]))
    c2 = LD.issue(P, "sync", lambda e: e.dma_start(out=sinT[:], in_=io["sinT"][:]))
    c3 = LD.issue(P, "sync", lambda e: e.dma_start(out=strip[:], in_=io["strip"][:]))

    rv1_prev = None
    xn_ms = []
    for c in range(NCH):
        xl = LD.issue(P, "sync", lambda e, c=c: e.dma_start(
            out=sb3(xt[:], DT), in_=rearr_dT(io["xT"], c * TC, (c + 1) * TC)),
            waits=[(vv, xn_ms[-1])] if xn_ms else ())
        sv = P.do("scalar", lambda e: e.activation(sq[:], xt[:], AF.Square),
                  waits=(xl,) + (((vv, xn_ms[-1]),) if xn_ms else ()), inc=ss)
        def xstat(e):
            for a in range(DT):
                e.matmul(ps_m[0:1, 0:TC], ones_col[:],
                         xt[:, a * TC:(a + 1) * TC], start=(a == 0), stop=(a == DT - 1))
            for a in range(DT):
                last = e.matmul(ps_m[0:1, 512:512 + TC], ones_col[:],
                                sq[:, a * TC:(a + 1) * TC], start=(a == 0), stop=(a == DT - 1))
            return last
        xw = [(ss, sv)]
        if rv1_prev is not None:
            xw.append((vv, rv1_prev))
        xm = P.do("tensor", xstat, waits=xw, inc=mm)
        def rowv1(e):
            e.tensor_scalar(rowq[0:1, 0:TC], ps_m[0:1, 0:TC], 1.0 / D, None, ALU.mult)
            e.tensor_scalar(rowq[0:1, TC:2 * TC], ps_m[0:1, 512:512 + TC], 1.0 / D, None, ALU.mult)
            e.tensor_tensor(rowq[0:1, 2 * TC:3 * TC], rowq[0:1, 0:TC], rowq[0:1, 0:TC], ALU.mult)
            return e.tensor_tensor(rowq[0:1, TC:2 * TC], rowq[0:1, TC:2 * TC],
                                   rowq[0:1, 2 * TC:3 * TC], ALU.subtract)
        rv1 = P.do("vector", rowv1, waits=((mm, xm),), inc=vv)
        rv1_prev = rv1
        def rows1(e):
            e.activation(rowq[0:1, 2 * TC:3 * TC], rowq[0:1, TC:2 * TC], AF.Ln,
                         bias=ctx.constcol(1e-5)[0:1, :])
            return e.activation(rowq[0:1, 2 * TC:3 * TC], rowq[0:1, 2 * TC:3 * TC],
                                AF.Exp, scale=-0.5)
        rs1 = P.do("scalar", rows1, waits=((vv, rv1),), inc=ss)
        rv2 = P.do("vector", lambda e: e.scalar_tensor_tensor(
            rowq[0:1, 3 * TC:4 * TC], rowq[0:1, 0:TC], -1.0,
            rowq[0:1, 2 * TC:3 * TC], ALU.mult, ALU.mult), waits=((ss, rs1),), inc=vv)
        def xbcast(e):
            e.matmul(ps_m[:, 0:TC], ones_row[:], rowq[0:1, 2 * TC:3 * TC],
                     start=True, stop=True)
            return e.matmul(ps_m[:, 512:512 + TC], ones_row[:], rowq[0:1, 3 * TC:4 * TC],
                            start=True, stop=True)
        xbm = P.do("tensor", xbcast, waits=((vv, rv2),), inc=mm)
        def xnorm(e, c=c):
            for a in range(DT):
                e.tensor_tensor(sq[:, a * TC:(a + 1) * TC], xt[:, a * TC:(a + 1) * TC],
                                ps_m[:, 0:TC], ALU.mult)
                last = e.tensor_tensor(xnb[:, a * T + c * TC:a * T + (c + 1) * TC],
                                       sq[:, a * TC:(a + 1) * TC],
                                       ps_m[:, 512:512 + TC], ALU.add)
            return last
        xn = P.do("vector", xnorm, waits=((mm, xbm),), inc=vv)
        xn_ms.append(xn)
    ln_done = xn_ms[-1]

    ck_ms = []
    for c in range(NCH):
        po = (c % 2) * 512
        w = [(vv, ln_done), w1]
        if c >= 2:
            w.append((ss, ck_ms[c - 2]))
        def ckmm(e, c=c, po=po):
            for a in range(DT):
                last = e.matmul(ps_p[:, po:po + 512],
                                wkv[:, a * 128:(a + 1) * 128],
                                xnb[:, a * T + c * TC:a * T + (c + 1) * TC],
                                start=(a == 0), stop=(a == DT - 1))
            return last
        cm = P.do("tensor", ckmm, waits=w, inc=mm)
        ce = P.do("scalar", lambda e, c=c, po=po: e.activation(
            ckv[:, c * TC:(c + 1) * TC], ps_p[:, po:po + 512], AF.Copy),
            waits=((mm, cm),), inc=ss)
        ck_ms.append(ce)
    ckv_done = ck_ms[-1]

    rp_ms = []
    for c in range(NCH):
        po = (c % 2) * 512
        w = [(ss, ckv_done), w6, w5, c1, c2]
        if c >= 1:
            w.extend(rp_ms[c - 1][2])
        def rmm(e, c=c, po=po):
            e.matmul(ps_s[0:16, po:po + 512], wkr[:, 0:16],
                     ckv[:, c * TC:(c + 1) * TC], start=True, stop=True)
            e.matmul(ps_at[0:16, po:po + 512], wkr[:, 16:32],
                     ckv[:, c * TC:(c + 1) * TC], start=True, stop=True)
            for a in range(DT):
                e.matmul(ps_m[0:16, 0:512], wqr[:, a * 32:a * 32 + 16],
                         xnb[:, a * T + c * TC:a * T + (c + 1) * TC],
                         start=(a == 0), stop=(a == DT - 1))
            for a in range(DT):
                last = e.matmul(ps_m[0:16, 512:1024], wqr[:, a * 32 + 16:a * 32 + 32],
                                xnb[:, a * T + c * TC:a * T + (c + 1) * TC],
                                start=(a == 0), stop=(a == DT - 1))
            return last
        rm = P.do("tensor", rmm, waits=w, inc=mm)
        def rot(e, c=c, po=po):
            cs = cosT[:, c * TC:(c + 1) * TC]
            sn = sinT[:, c * TC:(c + 1) * TC]
            e.tensor_tensor(ta[:], ps_s[0:16, po:po + 512], cs, ALU.mult)
            e.tensor_tensor(tb[:], ps_at[0:16, po:po + 512], sn, ALU.mult)
            e.tensor_tensor(kqch[:, 0:512], ta[:], tb[:], ALU.subtract)
            e.tensor_tensor(ta[:], ps_at[0:16, po:po + 512], cs, ALU.mult)
            e.tensor_tensor(tb[:], ps_s[0:16, po:po + 512], sn, ALU.mult)
            e.tensor_tensor(kqch[:, 512:1024], ta[:], tb[:], ALU.add)
            e.tensor_tensor(ta[:], ps_m[0:16, 0:512], cs, ALU.mult)
            e.tensor_tensor(tb[:], ps_m[0:16, 512:1024], sn, ALU.mult)
            e.tensor_tensor(tc_[:], ta[:], tb[:], ALU.subtract)
            e.tensor_scalar(kqch[:, 1024:1536], tc_[:], SCL, None, ALU.mult)
            e.tensor_tensor(ta[:], ps_m[0:16, 512:1024], cs, ALU.mult)
            e.tensor_tensor(tb[:], ps_m[0:16, 0:512], sn, ALU.mult)
            e.tensor_tensor(tc_[:], ta[:], tb[:], ALU.add)
            return e.tensor_scalar(kqch[:, 1536:2048], tc_[:], SCL, None, ALU.mult)
        rv = P.do("vector", rot, waits=((mm, rm),), inc=vv)
        sds = []
        for wpl in range(4):
            sd = SH.issue(P, "sync", lambda e, c=c, wpl=wpl: e.dma_start(
                out=io["scrR"][wpl, :, c, :],
                in_=kqch[:, wpl * 512:(wpl + 1) * 512]),
                waits=((vv, rv),))
            sds.append(sd)
        rp_ms.append((rv, sds[-1], sds))
    allrp = [sd for ent in rp_ms for sd in ent[2]]
    kc = SH.issue(P, "sync", lambda e: e.dma_start(
        out=kcat[64:96, :],
        in_=io["scrR"][0:2, :, :, :].rearrange("w p c t -> (w p) (c t)")),
        waits=tuple(allrp))
    qc_ = SH.issue(P, "sync", lambda e: e.dma_start(
        out=qcat[64:96, :],
        in_=io["scrR"][2:4, :, :, :].rearrange("w p c t -> (w p) (c t)")),
        waits=(kc,))

    if "dbgR" in io:
        STP.issue(P, "sync", lambda e: e.dma_start(
            out=io["dbgR"][0, :, :], in_=kcat[64:96, :].bitcast(BF16)), waits=(kc,))
        STP.issue(P, "sync", lambda e: e.dma_start(
            out=io["dbgR"][1, :, :], in_=qcat[64:96, :].bitcast(BF16)), waits=(qc_,))

    st_ms = []
    head_end_mm = None
    prev_last_evac = None
    an_ms = {}
    wq_prev = None
    for h in range(NHD):
        hpar = h % 2
        wqld = WG.issue(P, "gpsimd", lambda e, h=h, hpar=hpar: e.dma_start(
            out=wqh[:, hpar * 512:(hpar + 1) * 512],
            in_=io["WqS"][:, h, :]),
            waits=[(mm, wq_prev)] if wq_prev is not None else ())
        kev = []
        for c in range(NCH):
            po = (c % 2) * 512
            w = [(ss, ckv_done), w2]
            if c >= 2:
                w.append((ss, kev[c - 2]))
            else:
                if head_end_mm is not None:
                    w.append((mm, head_end_mm))
                if prev_last_evac is not None:
                    w.append((ss, prev_last_evac))
            km = P.do("tensor", lambda e, c=c, po=po, h=h: e.matmul(
                ps_p[0:64, po:po + 512], wkup[:, h * 64:(h + 1) * 64],
                ckv[:, c * TC:(c + 1) * TC], start=True, stop=True),
                waits=w, inc=mm)
            ke = P.do("scalar", lambda e, c=c, po=po: e.activation(
                kcat[0:64, c * TC:(c + 1) * TC], ps_p[0:64, po:po + 512], AF.Copy),
                waits=((mm, km), kc, qc_), inc=ss)
            kev.append(ke)
        for c in range(NCH):
            po = (c % 2) * 512
            w = [wqld, (ss, kev[6 + c] if c >= 2 else kev[-1])]
            def qmm(e, c=c, po=po, hpar=hpar):
                for a in range(DT):
                    last = e.matmul(ps_p[0:64, po:po + 512],
                                    wqh[:, hpar * 512 + a * 64:hpar * 512 + (a + 1) * 64],
                                    xnb[:, a * T + c * TC:a * T + (c + 1) * TC],
                                    start=(a == 0), stop=(a == DT - 1))
                return last
            qm = P.do("tensor", qmm, waits=w, inc=mm)
            qe = P.do("scalar", lambda e, c=c, po=po: e.activation(
                qcat[0:64, c * TC:(c + 1) * TC], ps_p[0:64, po:po + 512], AF.Copy,
                scale=SCL), waits=((mm, qm),), inc=ss)
            kev.append(qe)
        wq_prev = mm.n
        vz = P.do("vector", lambda e: e.memset(vaug[:], 1.0),
                  waits=((mm, head_end_mm),) if head_end_mm is not None else (),
                  inc=vv)
        for tt in range(32):
            po = (tt % 2) * 512
            w = [w3, (vv, vz)]
            if tt >= 2:
                w.append((ss, kev[14 + tt]))
            else:
                w.append((ss, kev[15]))
            vm = P.do("tensor", lambda e, tt=tt, po=po, h=h: e.matmul(
                ps_p[0:128, po:po + 64], ckv[:, tt * 128:(tt + 1) * 128],
                wvup[:, h * 64:(h + 1) * 64], start=True, stop=True),
                waits=w, inc=mm)
            vev = P.do("scalar", lambda e, tt=tt, po=po: e.activation(
                vaug[:, tt * 65:tt * 65 + 64], ps_p[0:128, po:po + 64], AF.Copy),
                waits=((mm, vm),), inc=ss)
            kev.append(vev)
        prev_last_evac = kev[-1]

        sq2 = P.do("scalar", lambda e: e.activation(
            sq[0:96, 0:T], kcat[:, :], AF.Square),
            waits=((ss, kev[15]), kc, (vv, xn_ms[-1])), inc=ss)
        nm_ms = []
        for c in range(NCH):
            nm = P.do("tensor", lambda e, c=c: e.matmul(
                ps_m[0:1, (c % 2) * 512:(c % 2) * 512 + 512], ones_col[0:96, :],
                sq[0:96, c * TC:(c + 1) * TC], start=True, stop=True),
                waits=((ss, sq2),) + (((vv, nm_ms[c - 2]),) if c >= 2 else ()), inc=mm)
            rv = P.do("vector", lambda e, c=c: e.tensor_reduce(
                bscr[0:1, c:c + 1],
                ps_m[0:1, (c % 2) * 512:(c % 2) * 512 + 512], AX, ALU.max),
                waits=((mm, nm),), inc=vv)
            nm_ms.append(rv)
        kn = P.do("vector", lambda e: e.tensor_reduce(
            bscr[0:1, 8:9], bscr[0:1, 0:8], AX, ALU.max), waits=((vv, nm_ms[-1]),), inc=vv)
        sq3 = P.do("scalar", lambda e: e.activation(
            sq[0:96, 0:T], qcat[:, :], AF.Square), waits=((vv, kn), qc_), inc=ss)
        nm_ms2 = []
        for c in range(NCH):
            nm = P.do("tensor", lambda e, c=c: e.matmul(
                ps_m[0:1, (c % 2) * 512:(c % 2) * 512 + 512], ones_col[0:96, :],
                sq[0:96, c * TC:(c + 1) * TC], start=True, stop=True),
                waits=((ss, sq3),) + (((vv, nm_ms2[c - 2]),) if c >= 2 else ()), inc=mm)
            rv = P.do("vector", lambda e, c=c: e.tensor_reduce(
                bscr[0:1, c:c + 1],
                ps_m[0:1, (c % 2) * 512:(c % 2) * 512 + 512], AX, ALU.max),
                waits=((mm, nm),), inc=vv)
            nm_ms2.append(rv)
        qn = P.do("vector", lambda e: e.tensor_reduce(
            bscr[0:1, 9:10], bscr[0:1, 0:8], AX, ALU.max), waits=((vv, nm_ms2[-1]),), inc=vv)
        bs = P.do("scalar", lambda e: e.activation(
            bscr[0:1, 10:12], bscr[0:1, 8:10], AF.Sqrt), waits=((vv, qn),), inc=ss)
        bv = P.do("vector", lambda e: (
            e.tensor_tensor(bscr[0:1, 12:13], bscr[0:1, 10:11], bscr[0:1, 11:12], ALU.mult),
            e.tensor_scalar(bscr[0:1, 12:13], bscr[0:1, 12:13], -1.0, None, ALU.mult))[-1],
            waits=((ss, bs),), inc=vv)
        bb = P.do("tensor", lambda e: e.matmul(
            ps_m[:, 0:1], ones_row[:], bscr[0:1, 12:13], start=True, stop=True),
            waits=((vv, bv),), inc=mm)
        bc2 = P.do("vector", lambda e: e.tensor_copy(bcol[:], ps_m[:, 0:1]),
                   waits=((mm, bb),), inc=vv)

        exp_prev = {}
        pv_ms = {}
        for qcn in range(NCH):
            ktmax = 4 * qcn + 4
            pvn = None
            for kt in range(ktmax):
                po = (kt % 2) * 512
                w = [(vv, bc2)]
                if (qcn, kt - 2) in exp_prev:
                    w.append((ss, exp_prev[(qcn, kt - 2)]))
                elif qcn >= 1 and kt < 2:
                    w.append((ss, exp_prev[(qcn - 1, 4 * qcn + kt - 2)]))
                sm = P.do("tensor", lambda e, kt=kt, qcn=qcn, po=po: e.matmul(
                    ps_s[:, po:po + 512], kcat[:, kt * 128:(kt + 1) * 128],
                    qcat[:, qcn * TC:(qcn + 1) * TC], start=True, stop=True),
                    waits=w, inc=mm)
                ew = [(mm, sm)]
                if (qcn, kt - 2) in pv_ms:
                    ew.append((mm, pv_ms[(qcn, kt - 2)]))
                elif qcn >= 1 and kt < 2:
                    ew.append((mm, pv_ms[(qcn - 1, 4 * qcn + kt - 2)]))
                if "dbgKQ" in io and h == 0 and qcn == 7 and kt == 28:
                    STP.issue(P, "sync", lambda e: e.dma_start(
                        out=io["dbgKQ"][0, :, :], in_=kcat[0:64, 3584:4096]),
                        waits=((mm, sm),))
                    STP.issue(P, "sync", lambda e: e.dma_start(
                        out=io["dbgKQ"][1, :, :], in_=qcat[0:64, 3584:4096]),
                        waits=((mm, sm),))
                if "dbg" in io and h == 0 and qcn == 7 and kt in (28, 30):
                    sdump = P.do("vector", lambda e, po=po: e.tensor_copy(
                        sq[:, 0:512], ps_s[:, po:po + 512]), waits=((mm, sm),), inc=vv)
                    sdd = STP.issue(P, "sync", lambda e, kt=kt: e.dma_start(
                        out=io["dbg5"][(kt - 28) // 2, :, :], in_=sq[:, 0:512]),
                        waits=((vv, sdump),))
                    ew.append(sdd)
                ex = P.do("scalar", lambda e, kt=kt, po=po: e.activation(
                    ebuf[:, po:po + 512], ps_s[:, po:po + 512], AF.Exp,
                    bias=bcol[:]), waits=ew, inc=ss)
                exp_prev[(qcn, kt)] = ex
                lastdep = (ss, ex)
                if kt >= 4 * qcn:
                    off = 512 - (kt - 4 * qcn) * 128
                    mk = P.do("vector", lambda e, po=po, off=off: e.tensor_tensor(
                        ebuf[:, po:po + 512], ebuf[:, po:po + 512],
                        strip[:, off:off + 512], ALU.mult),
                        waits=((ss, ex), c3), inc=vv)
                    lastdep = (vv, mk)
                if "dbg" in io and h == 0 and qcn == 7 and kt == 28:
                    STP.issue(P, "sync", lambda e: e.dma_start(
                        out=io["dbg4"][:, :], in_=vaug[:, 28 * 65:32 * 65]),
                        waits=(lastdep,))
                if "dbg" in io and h == 0 and qcn == 7 and kt >= 28:
                    dsd2 = STP.issue(P, "sync", lambda e, kt=kt, po=po: e.dma_start(
                        out=io["dbg2"][kt - 28, :, :], in_=ebuf[:, po:po + 512]),
                        waits=(lastdep,))
                    lastdep = dsd2
                pw = [lastdep]
                if kt == 0 and (h, qcn - 2) in an_ms:
                    pw.append((vv, an_ms[(h, qcn - 2)][0]))
                elif kt == 0 and h >= 1 and qcn < 2:
                    pw.append((vv, an_ms[(h - 1, NCH - 2 + qcn)][0]))
                pv = P.do("tensor", lambda e, kt=kt, po=po, qcn=qcn, ktmax=ktmax:
                          e.matmul(
                    ps_at[0:65, (qcn % 2) * 512:(qcn % 2) * 512 + 512],
                    vaug[:, kt * 65:(kt + 1) * 65], ebuf[:, po:po + 512],
                    start=(kt == 0), stop=(kt == ktmax - 1)),
                    waits=pw, inc=mm)
                pv_ms[(qcn, kt)] = pv
                pvn = pv
            if "dbg" in io and h == 0:
                zraw = P.do("vector", lambda e, qcn=qcn: e.tensor_copy(
                    ta[0:1, 0:512], ps_at[64:65, (qcn % 2) * 512:(qcn % 2) * 512 + 512]),
                    waits=((mm, pvn),), inc=vv)
                STP.issue(P, "sync", lambda e, qcn=qcn: e.dma_start(
                    out=io["dbg3"][qcn, :, :], in_=ta[0:1, 0:512]), waits=((vv, zraw),))
            zr = P.do("vector", lambda e, qcn=qcn: e.reciprocal(
                nrm[0:1, 0:512], ps_at[64:65, (qcn % 2) * 512:(qcn % 2) * 512 + 512]),
                waits=((mm, pvn),), inc=vv)
            zb = P.do("tensor", lambda e: e.matmul(
                ps_m[0:64, 0:512], ones_row[:, 0:64], nrm[0:1, 0:512],
                start=True, stop=True), waits=((vv, zr),), inc=mm)
            ze = P.do("vector", lambda e: e.tensor_copy(zbc[:], ps_m[0:64, 0:512]),
                      waits=((mm, zb),), inc=vv)
            if "dbg" in io and h == 0:
                zd0 = STP.issue(P, "sync", lambda e, qcn=qcn: e.dma_start(
                    out=io["dbg"][qcn, 0, 1:2, :], in_=nrm[0:1, 0:512]),
                    waits=((vv, zr),))
                zd1 = STP.issue(P, "sync", lambda e, qcn=qcn: e.dma_start(
                    out=io["dbg"][qcn, 0, 0:1, :], in_=zbc[0:1, :]),
                    waits=((vv, ze), zd0))
                dbv = P.do("vector", lambda e, qcn=qcn: e.tensor_copy(
                    zbc[:], ps_at[0:64, (qcn % 2) * 512:(qcn % 2) * 512 + 512]),
                    waits=((vv, ze), zd1), inc=vv)
                dsd = STP.issue(P, "sync", lambda e, qcn=qcn: e.dma_start(
                    out=io["dbg"][qcn, 1, :, :], in_=zbc[:]), waits=((vv, dbv),))
                zrec = P.do("vector", lambda e, qcn=qcn: e.tensor_copy(
                    zbc[:], ps_m[0:64, 0:512]), waits=(dsd,), inc=vv)
            anw = []
            if (h, qcn - 2) in an_ms:
                anw.append(an_ms[(h, qcn - 2)][1])
            elif h >= 1 and qcn < 2:
                anw.append(an_ms[(h - 1, NCH - 2 + qcn)][1])
            if "dbg" in io and h == 0:
                anw.append((vv, zrec))
            an = P.do("vector", lambda e, qcn=qcn: e.tensor_tensor(
                anst[:, (qcn % 2) * 512:(qcn % 2) * 512 + 512],
                ps_at[0:64, (qcn % 2) * 512:(qcn % 2) * 512 + 512], zbc[:], ALU.mult),
                waits=anw, inc=vv)
            ansd = STP.issue(P, "sync", lambda e, h=h, qcn=qcn: e.dma_start(
                out=io["anD"][h, qcn, :, :],
                in_=anst[:, (qcn % 2) * 512:(qcn % 2) * 512 + 512]),
                waits=((vv, an),))
            an_ms[(h, qcn)] = (an, ansd)
        head_end_mm = mm.n

    last_an_dma = an_ms[(NHD - 1, NCH - 1)][1]
    ev_ms = []
    om_by_c = {}
    for c in range(NCH):
        anlw = [last_an_dma]
        if c >= 1:
            anlw.append((mm, om_by_c[c - 1]))
        anl = LD.issue(P, "sync", lambda e, c=c: e.dma_start(
            out=anL.rearrange("p (h t) -> p h t", h=NHD),
            in_=io["anD"][:, c, :, :].rearrange("h p t -> p h t")), waits=anlw)
        for dto in range(DT):
            i = c * DT + dto
            po = (i % 2) * 512
            dpar = dto % 2
            wol = WG.issue(P, "gpsimd", lambda e, c=c, dto=dto, dpar=dpar: e.dma_start(
                out=wod[:, dpar * 2048:(dpar + 1) * 2048].rearrange(
                    "p (h j) -> p h j", h=NHD),
                in_=io["WoS2"][:, dto, :, :]),
                waits=[(mm, ev_ms[-2][1])] if len(ev_ms) >= 2 else ())
            w = [anl, wol]
            def womm(e, c=c, dto=dto, po=po, dpar=dpar):
                for h in range(NHD):
                    last = e.matmul(ps_p[:, po:po + 512],
                                    wod[:, dpar * 2048 + h * 128:dpar * 2048 + (h + 1) * 128],
                                    anL[:, h * 512:(h + 1) * 512],
                                    start=(h == 0), stop=(h == NHD - 1))
                return last
            if len(ev_ms) >= 2:
                w.append((vv, ev_ms[-2][0]))
            om = P.do("tensor", womm, waits=w, inc=mm)
            om_by_c[c] = mm.n
            rlw = [last_an_dma]
            if len(ev_ms) >= 8:
                rlw.append((vv, ev_ms[-8][0]))
            rl = LD.issue(P, "sync", lambda e, c=c, dto=dto: e.dma_start(
                out=sq[:, dto * 512:dto * 512 + 512],
                in_=io["xT"][dto * 128:(dto + 1) * 128, c * TC:(c + 1) * TC]),
                waits=rlw)
            ow = [(mm, om), rl]
            if len(st_ms) >= 2:
                ow.append(st_ms[-2])
            oe = P.do("vector", lambda e, po=po, dto=dto: e.tensor_tensor(
                yout[:, po:po + 512], ps_p[:, po:po + 512],
                sq[:, dto * 512:dto * 512 + 512], ALU.add),
                waits=ow, inc=vv)
            ev_ms.append((oe, mm.n))
            sr = STP.issue(P, "sync", lambda e, c=c, dto=dto, po=po: e.dma_start(
                out=io["yT"][dto * 128:(dto + 1) * 128, c * TC:(c + 1) * TC],
                in_=yout[:, po:po + 512]), waits=((vv, oe),))
            st_ms.append(sr)

    drain_pools(P, ctx)
    return st_ms


def plan_mamba2(P, ctx, io, uid="s", nlayers=4, nchunks=None, stages=99):
    QC = 128
    NCH = T // QC if nchunks is None else nchunks
    NH2 = 32
    DIP = 4256

    ones_row = ctx.sbuf("ones_row", [1, 128], F32)
    ones_col = ctx.sbuf("ones_col", [128, 1], F32)
    onescb = ctx.sbuf(uid + "onescb", [128, 1], BF16)
    orb = ctx.sbuf("ones_rb", [1, 128], BF16)
    cpool = ctx.sbuf("cpool", [128, 16], F32)
    identB = ctx.sbuf("identB", [128, 128], BF16)
    identF = ctx.sbuf(uid + "identF", [128, 128], F32)
    maskB = ctx.sbuf(uid + "maskB", [128, 512], BF16)

    inw = ctx.sbuf(uid + "inw", [128, DT * DIP], BF16)
    owS = ctx.sbuf(uid + "owS", [128, 16 * 1024], BF16)
    cwX = ctx.sbuf(uid + "cwX", [128, 16 * 4], F32)
    cbX = ctx.sbuf(uid + "cbX", [128, 16], F32)
    cwBC = ctx.sbuf(uid + "cwBC", [64, 2 * 4], F32)
    cbBC = ctx.sbuf(uid + "cbBC", [64, 2], F32)
    lnw = ctx.sbuf(uid + "lnw", [128, DT], F32)
    lnb = ctx.sbuf(uid + "lnb", [128, DT], F32)
    dtb2 = ctx.sbuf(uid + "dtb2", [32, 1], F32)
    amat2 = ctx.sbuf(uid + "amat2", [32, 1], F32)
    dpS = ctx.sbuf(uid + "dpS", [128, 32], F32)
    nwS = ctx.sbuf(uid + "nwS", [128, 32], F32)

    xt = ctx.sbuf(uid + "xt", [128, DT * QC], F32)
    sq = ctx.sbuf(uid + "sq", [128, DT * QC], F32)
    xnb = ctx.sbuf(uid + "xnb", [128, DT * QC], BF16)
    rowq = ctx.sbuf(uid + "rowq", [1, 4 * QC], F32)
    zsil = ctx.sbuf(uid + "zsil", [128, 16 * QC], BF16)
    xse = ctx.sbuf(uid + "xse", [128, 16 * (QC + 4)], BF16)
    bce = ctx.sbuf(uid + "bce", [64, 2 * (QC + 4)], F32)
    xcs = ctx.sbuf(uid + "xcs", [128, 16 * QC], BF16)
    xcb = xcs
    Bcb = ctx.sbuf(uid + "Bcb", [64, 2 * QC], BF16)
    dtr = ctx.sbuf(uid + "dtr", [32, 5 * QC], F32)
    nlamT = ctx.sbuf(uid + "nlamT", [128, 32], F32)
    dtT = ctx.sbuf(uid + "dtT", [128, 32], F32)
    wall = ctx.sbuf(uid + "wall", [128, 32], F32)
    gv = ctx.sbuf(uid + "gv", [128, 32], F32)
    lamF = ctx.sbuf(uid + "lamF", [1, NH2 * QC], F32)
    lsc = ctx.sbuf(uid + "lsc", [32, QC], F32)
    onesF = ctx.sbuf(uid + "onesF", [32, QC], F32)
    lamef = ctx.sbuf(uid + "lamef", [1, NH2 * QC], BF16)
    geb = ctx.sbuf(uid + "geb", [128, 4 * QC], BF16)
    Gt = ctx.sbuf(uid + "Gt", [128, QC], BF16)
    Bst = ctx.sbuf(uid + "Bst", [128, 64], BF16)
    Xst = ctx.sbuf(uid + "Xst", [128, 2 * 128], BF16)
    Xd = ctx.sbuf(uid + "Xd", [128, 2 * 64], BF16)
    Xw = ctx.sbuf(uid + "Xw", [128, 2 * 64], BF16)
    Hbuf = ctx.sbuf(uid + "Hbuf", [128, 16 * 64], F32)
    Hbb = ctx.sbuf(uid + "Hbb", [64, 2 * 64], BF16)
    CEb = ctx.sbuf(uid + "CEb", [64, 2 * QC], BF16)
    ybuf = ctx.sbuf(uid + "ybuf", [128, 16 * QC], BF16)
    ygn = ctx.sbuf(uid + "ygn", [128, 16 * QC], BF16)
    ysq = ygn
    yout = ctx.sbuf(uid + "yout", [128, 2 * QC], F32)

    ps_a = ctx.psum(uid + "ps_a", [128, 1024], F32)
    ps_d = ctx.psum(uid + "ps_d", [128, 2048], F32)
    ps_y = ctx.psum(uid + "ps_y", [64, 512], F32)
    ps_t = ctx.psum(uid + "ps_t", [128, 512], F32)

    mm = ctx.sem("mm"); vv = ctx.sem("vv"); ss = ctx.sem("ss")
    LD = ctx.pool("ld", 6)
    WG = ctx.pool("wg", 3)
    STP = ctx.pool("st", 3)

    LD.issue(P, "sync", lambda e: e.dma_start(out=cpool[:], in_=io["cpool"][:]))
    LD.issue(P, "sync", lambda e: e.dma_start(out=ones_row[:], in_=io["ones_row"][:]))
    LD.issue(P, "sync", lambda e: e.dma_start(out=ones_col[:], in_=io["ones_col"][:]))
    LD.issue(P, "sync", lambda e: e.dma_start(out=onescb[:], in_=io["ones_col_b"][:]))
    LD.issue(P, "sync", lambda e: e.dma_start(out=orb[:], in_=io["ones_row_b"][:]))
    cid = LD.issue(P, "sync", lambda e: e.dma_start(out=identB[:], in_=io["identB"][:]))
    cif = LD.issue(P, "sync", lambda e: e.dma_start(out=identF[:], in_=io["identF"][:]))
    cmk = LD.issue(P, "sync", lambda e: e.dma_start(out=maskB[:], in_=io["maskrep"][:]))
    ycp = LD.issue(P, "sync", lambda e: e.dma_start(out=io["yT"][:], in_=io["xT"][:]))

    prev_layer_store = []
    for li in range(nlayers):
        wlin = WG.issue(P, "gpsimd", lambda e, li=li: e.dma_start(
            out=sb3(inw[:], DT),
            in_=io["in_w"][li].rearrange("(a p) f -> p a f", p=128)),
            waits=tuple(prev_layer_store))
        wlo = WG.issue(P, "gpsimd", lambda e, li=li: e.dma_start(
            out=owS[:].rearrange("p (h f) -> p h f", h=16), in_=io["outS"][li]),
            waits=tuple(prev_layer_store))
        wl3 = LD.issue(P, "sync", lambda e, li=li: e.dma_start(
            out=sb3(cwX[:], 16), in_=io["convX"][li]))
        wl4 = LD.issue(P, "sync", lambda e, li=li: e.dma_start(
            out=cbX[:], in_=io["convbX"][li]))
        wl5 = LD.issue(P, "sync", lambda e, li=li: e.dma_start(
            out=sb3(cwBC[:], 2), in_=io["convBC"][li]))
        wl6 = LD.issue(P, "sync", lambda e, li=li: e.dma_start(
            out=cbBC[:], in_=io["convbBC"][li]))
        wl7 = LD.issue(P, "sync", lambda e, li=li: e.dma_start(
            out=lnw[:], in_=io["lnwS"][li]))
        wl8 = LD.issue(P, "sync", lambda e, li=li: e.dma_start(
            out=lnb[:], in_=io["lnbS"][li]))
        wl9 = LD.issue(P, "sync", lambda e, li=li: e.dma_start(
            out=dtb2[:], in_=io["dtbS"][li]))
        wl10 = LD.issue(P, "sync", lambda e, li=li: e.dma_start(
            out=amat2[:], in_=io["AlogS"][li]))
        wl11 = LD.issue(P, "sync", lambda e, li=li: e.dma_start(
            out=dpS[:], in_=io["DpS"][li]))
        wl12 = LD.issue(P, "sync", lambda e, li=li: e.dma_start(
            out=nwS[:], in_=io["nwSS"][li]))
        wls = [wlin, wlo, wl3, wl4, wl5, wl6, wl7, wl8, wl9, wl10, wl11, wl12,
               cid, cif, cmk]
        if li == 0 and "_pro_stores" in io:
            wls = wls + list(io["_pro_stores"][-3:])

        sA = P.do("scalar", lambda e: e.activation(amat2[:], amat2[:], AF.Exp),
                  waits=(wl10,) + tuple(prev_layer_store), inc=ss)
        vA = P.do("vector", lambda e: e.tensor_scalar(amat2[:], amat2[:], -1.0, None,
                                                      ALU.mult), waits=((ss, sA),), inc=vv)
        z0 = P.do("vector", lambda e: (e.memset(Hbuf[:], 0.0),
                                       e.memset(xse[:], 0.0),
                                       e.memset(onesF[:], 1.0),
                                       e.memset(bce[:], 0.0))[-1],
                  waits=tuple(prev_layer_store), inc=vv)
        if "dbg_amat" in io:
            STP.issue(P, "sync", lambda e, li=li: e.dma_start(
                out=io["dbg_amat"][li], in_=amat2[:]), waits=((vv, vA),))

        store_ms = []
        ye_prev = None
        rv1_prev = None
        hup_last = None
        prev_geb_users = None
        prev_dt_users = None
        ostores = []
        for c in range(NCH):
            waits = list(wls) + ([store_ms[c - 2]] if c >= 2 else [])
            if ye_prev is not None:
                waits.append((vv, ye_prev))
            xl = LD.issue(P, "sync", lambda e, c=c: e.dma_start(
                out=sb3(xt[:], DT), in_=rearr_dT(io["innerT"], c * QC, (c + 1) * QC)),
                waits=waits)

            swaits = [xl]
            if ye_prev is not None:
                swaits.append((vv, ye_prev))
            sv = P.do("scalar", lambda e: e.activation(sq[:], xt[:], AF.Square),
                      waits=swaits, inc=ss)
            def xstat(e):
                for a in range(DT):
                    e.matmul(ps_a[0:1, 0:QC], ones_col[:],
                             xt[:, a * QC:(a + 1) * QC], start=(a == 0), stop=(a == DT - 1))
                for a in range(DT):
                    last = e.matmul(ps_a[0:1, 512:512 + QC], ones_col[:],
                                    sq[:, a * QC:(a + 1) * QC], start=(a == 0), stop=(a == DT - 1))
                return last
            xw = [(ss, sv)]
            if rv1_prev is not None:
                xw.append((vv, rv1_prev))
            xm = P.do("tensor", xstat, waits=xw, inc=mm)
            def rowv1(e):
                e.tensor_scalar(rowq[0:1, 0:QC], ps_a[0:1, 0:QC], 1.0 / D, None, ALU.mult)
                e.tensor_scalar(rowq[0:1, QC:2 * QC], ps_a[0:1, 512:512 + QC], 1.0 / D, None, ALU.mult)
                e.tensor_tensor(rowq[0:1, 2 * QC:3 * QC], rowq[0:1, 0:QC], rowq[0:1, 0:QC], ALU.mult)
                return e.tensor_tensor(rowq[0:1, QC:2 * QC], rowq[0:1, QC:2 * QC],
                                       rowq[0:1, 2 * QC:3 * QC], ALU.subtract)
            rv1 = P.do("vector", rowv1, waits=((mm, xm),), inc=vv)
            rv1_prev = rv1
            def rows1(e):
                e.activation(rowq[0:1, 2 * QC:3 * QC], rowq[0:1, QC:2 * QC], AF.Ln,
                             bias=ctx.constcol(1e-5)[0:1, :])
                return e.activation(rowq[0:1, 2 * QC:3 * QC], rowq[0:1, 2 * QC:3 * QC],
                                    AF.Exp, scale=-0.5)
            rs1 = P.do("scalar", rows1, waits=((vv, rv1),), inc=ss)
            rv2 = P.do("vector", lambda e: e.scalar_tensor_tensor(
                rowq[0:1, 3 * QC:4 * QC], rowq[0:1, 0:QC], -1.0,
                rowq[0:1, 2 * QC:3 * QC], ALU.mult, ALU.mult), waits=((ss, rs1),), inc=vv)
            def xbcast(e):
                e.matmul(ps_a[:, 0:QC], ones_row[:], rowq[0:1, 2 * QC:3 * QC],
                         start=True, stop=True)
                return e.matmul(ps_a[:, 512:512 + QC], ones_row[:], rowq[0:1, 3 * QC:4 * QC],
                                start=True, stop=True)
            xbm = P.do("tensor", xbcast, waits=((vv, rv2),), inc=mm)
            def xnorm(e):
                for a in range(DT):
                    e.tensor_tensor(sq[:, a * QC:(a + 1) * QC], xt[:, a * QC:(a + 1) * QC],
                                    ps_a[:, 0:QC], ALU.mult)
                    e.tensor_tensor(sq[:, a * QC:(a + 1) * QC], sq[:, a * QC:(a + 1) * QC],
                                    ps_a[:, 512:512 + QC], ALU.add)
                    last = e.tensor_scalar(xnb[:, a * QC:(a + 1) * QC],
                                           sq[:, a * QC:(a + 1) * QC],
                                           lnw[:, a:a + 1], lnb[:, a:a + 1],
                                           ALU.mult, ALU.add)
                return last
            xn = P.do("vector", xnorm, waits=((mm, xbm), wl7, wl8), inc=vv)

            if stages < 2:
                continue
            ipm = []
            for f in range(32):
                fo = (f % 2) * 512
                w = [(vv, xn)]
                if c == 0 and f == 0:
                    w += [wlin, (vv, z0)]
                if f >= 2:
                    w.append((ss, ipm[f - 2]))
                def ipmm(e, f=f, fo=fo):
                    for a in range(DT):
                        last = e.matmul(ps_a[:, fo:fo + QC],
                                        inw[:, a * DIP + f * 128:a * DIP + (f + 1) * 128],
                                        xnb[:, a * QC:(a + 1) * QC],
                                        start=(a == 0), stop=(a == DT - 1))
                    return last
                um = P.do("tensor", ipmm, waits=w, inc=mm)
                if f < 16:
                    ev = P.do("scalar", lambda e, f=f, fo=fo: e.activation(
                        zsil[:, f * QC:(f + 1) * QC], ps_a[:, fo:fo + QC], AF.Silu),
                        waits=((mm, um),), inc=ss)
                else:
                    fx = f - 16
                    ev = P.do("scalar", lambda e, fx=fx, fo=fo: e.activation(
                        xse[:, fx * (QC + 4) + 4:fx * (QC + 4) + 4 + QC],
                        ps_a[:, fo:fo + QC], AF.Copy), waits=((mm, um),), inc=ss)
                ipm.append(ev)
            def bcd_mm(e):
                for a in range(DT):
                    e.matmul(ps_a[0:64, 0:QC], inw[:, a * DIP + 4096:a * DIP + 4160],
                             xnb[:, a * QC:(a + 1) * QC], start=(a == 0), stop=(a == DT - 1))
                for a in range(DT):
                    e.matmul(ps_a[0:64, 512:512 + QC], inw[:, a * DIP + 4160:a * DIP + 4224],
                             xnb[:, a * QC:(a + 1) * QC], start=(a == 0), stop=(a == DT - 1))
                for a in range(DT):
                    last = e.matmul(ps_t[0:32, 0:QC], inw[:, a * DIP + 4224:a * DIP + 4256],
                                    xnb[:, a * QC:(a + 1) * QC], start=(a == 0), stop=(a == DT - 1))
                return last
            bw = [(ss, ipm[-1]), (vv, rv1)]
            if hup_last is not None:
                bw.append((vv, hup_last))
            bm = P.do("tensor", bcd_mm, waits=bw, inc=mm)
            bev = P.do("scalar", lambda e: (
                e.activation(bce[:, 4:4 + QC], ps_a[0:64, 0:QC], AF.Copy),
                e.activation(bce[:, (QC + 4) + 4:(QC + 4) + 4 + QC],
                             ps_a[0:64, 512:512 + QC], AF.Copy))[-1],
                waits=((mm, bm),), inc=ss)

            if stages < 3:
                continue
            dw = [(mm, bm), wl9, (vv, vA)]
            if prev_dt_users is not None:
                dw.append(prev_dt_users)
            dv1 = P.do("vector", lambda e: e.tensor_scalar(
                dtr[:, 0:QC], ps_t[0:32, 0:QC], dtb2[:, 0:1], None, ALU.add),
                waits=dw, inc=vv)
            def dtsp(e):
                e.activation(dtr[:, QC:2 * QC], dtr[:, 0:QC], AF.Abs)
                e.activation(dtr[:, QC:2 * QC], dtr[:, QC:2 * QC], AF.Exp, scale=-1.0)
                e.activation(dtr[:, QC:2 * QC], dtr[:, QC:2 * QC], AF.Ln,
                             bias=ctx.constcol(1.0)[0:32, :])
                return e.activation(dtr[:, 3 * QC:4 * QC], dtr[:, 0:QC], AF.Relu)
            sp1 = P.do("scalar", dtsp, waits=((vv, dv1),), inc=ss)
            def dtfin(e):
                e.tensor_tensor(dtr[:, 0:QC], dtr[:, QC:2 * QC], dtr[:, 3 * QC:4 * QC], ALU.add)
                e.tensor_scalar(dtr[:, QC:2 * QC], dtr[:, 0:QC], amat2[:, 0:1], None, ALU.mult)
                e.tensor_tensor_scan(lsc[:], onesF[:],
                                     dtr[:, QC:2 * QC], 0.0, ALU.mult, ALU.add)
                e.tensor_copy(dtr[:, 2 * QC:3 * QC], lsc[:])
                return e.tensor_scalar(dtr[:, 3 * QC:4 * QC], dtr[:, 2 * QC:3 * QC],
                                       -1.0, None, ALU.mult)
            dv2 = P.do("vector", dtfin, waits=((ss, sp1),), inc=vv)
            sp2 = P.do("scalar", lambda e: e.activation(
                dtr[:, 4 * QC:5 * QC], dtr[:, 2 * QC:3 * QC], AF.Exp),
                waits=((vv, dv2),), inc=ss)
            if "dbg_lam" in io:
                STP.issue(P, "sync", lambda e, c=c: e.dma_start(
                    out=io["dbg_lam"][c, :, 0:128], in_=lsc[:]), waits=((vv, dv2),))
                STP.issue(P, "sync", lambda e, c=c: e.dma_start(
                    out=io["dbg_lam"][c, :, 128:256], in_=dtr[:, QC:2 * QC]),
                    waits=((vv, dv2),))

            trm = P.do("tensor", lambda e: (
                e.transpose(ps_t[:, 256:256 + 32], dtr[:, 3 * QC:4 * QC],
                            identF[0:32, 0:32]),
                e.transpose(ps_t[:, 288:288 + 32], dtr[:, 0:QC], identF[0:32, 0:32]),
                e.transpose(ps_t[0:1, 320:352], dtr[:, 4 * QC - 1:4 * QC],
                            identF[0:32, 0:32]))[-1],
                waits=((vv, dv2), cif), inc=mm)
            tre = P.do("vector", lambda e: (
                e.tensor_copy(nlamT[:], ps_t[:, 256:256 + 32]),
                e.tensor_copy(dtT[:], ps_t[:, 288:288 + 32]),
                e.tensor_copy(rowq[0:1, 0:32], ps_t[0:1, 320:352]))[-1],
                waits=((mm, trm),), inc=vv)
            lb = P.do("tensor", lambda e: e.matmul(
                ps_t[:, 352:352 + 32], ones_row[:], rowq[0:1, 0:32],
                start=True, stop=True), waits=((vv, tre),), inc=mm)
            ge1 = P.do("scalar", lambda e: e.activation(
                rowq[0:1, 32:64], rowq[0:1, 0:32], AF.Exp, scale=-1.0),
                waits=((vv, tre),), inc=ss)
            gbm = P.do("tensor", lambda e: e.matmul(
                ps_t[:, 384:384 + 32], ones_row[:], rowq[0:1, 32:64],
                start=True, stop=True), waits=((ss, ge1),), inc=mm)
            def wallv(e):
                e.tensor_copy(gv[:, 0:32], ps_t[:, 384:384 + 32])
                return e.tensor_tensor(wall[:], nlamT[:], ps_t[:, 352:352 + 32],
                                       ALU.subtract)
            wv1 = P.do("vector", wallv, waits=((mm, gbm), (mm, lb)), inc=vv)
            wv2 = P.do("scalar", lambda e: e.activation(wall[:], wall[:], AF.Exp),
                       waits=((vv, wv1),), inc=ss)
            wv3 = P.do("vector", lambda e: e.tensor_tensor(wall[:], wall[:], dtT[:],
                                                           ALU.mult),
                       waits=((ss, wv2),), inc=vv)

            if stages < 4:
                continue
            def conv_x(e):
                for a in range(16):
                    o = a * (QC + 4)
                    e.tensor_scalar(xcs[:, a * QC:(a + 1) * QC],
                                    xse[:, o + 1:o + 1 + QC], cwX[:, a * 4:a * 4 + 1],
                                    None, ALU.mult)
                    for k in (1, 2):
                        e.scalar_tensor_tensor(xcs[:, a * QC:(a + 1) * QC],
                                               xse[:, o + 1 + k:o + 1 + k + QC],
                                               cwX[:, a * 4 + k:a * 4 + k + 1],
                                               xcs[:, a * QC:(a + 1) * QC], ALU.mult, ALU.add)
                    e.scalar_tensor_tensor(xcs[:, a * QC:(a + 1) * QC],
                                           xse[:, o + 4:o + 4 + QC],
                                           cwX[:, a * 4 + 3:a * 4 + 4],
                                           xcs[:, a * QC:(a + 1) * QC], ALU.mult, ALU.add)
                for j in range(2):
                    o = j * (QC + 4)
                    e.tensor_scalar(Bcb[:, j * QC:(j + 1) * QC],
                                    bce[:, o + 1:o + 1 + QC], cwBC[:, j * 4:j * 4 + 1],
                                    None, ALU.mult)
                    for k in (1, 2):
                        e.scalar_tensor_tensor(Bcb[:, j * QC:(j + 1) * QC],
                                               bce[:, o + 1 + k:o + 1 + k + QC],
                                               cwBC[:, j * 4 + k:j * 4 + k + 1],
                                               Bcb[:, j * QC:(j + 1) * QC], ALU.mult, ALU.add)
                    last = e.scalar_tensor_tensor(Bcb[:, j * QC:(j + 1) * QC],
                                                  bce[:, o + 4:o + 4 + QC],
                                                  cwBC[:, j * 4 + 3:j * 4 + 4],
                                                  Bcb[:, j * QC:(j + 1) * QC], ALU.mult, ALU.add)
                return last
            cv = P.do("vector", conv_x, waits=((ss, ipm[-1]), (ss, bev), wl3, wl5), inc=vv)
            def halo_v(e):
                for a in range(16):
                    o = a * (QC + 4)
                    e.tensor_copy(xse[:, o + 1:o + 4], xse[:, o + 1 + QC:o + 4 + QC])
                for j in range(2):
                    o = j * (QC + 4)
                    last = e.tensor_copy(bce[:, o + 1:o + 4], bce[:, o + 1 + QC:o + 4 + QC])
                return last
            hv = P.do("vector", halo_v, inc=vv)
            def silu_s(e):
                for a in range(16):
                    e.activation(xcs[:, a * QC:(a + 1) * QC], xcs[:, a * QC:(a + 1) * QC],
                                 AF.Silu, bias=cbX[:, a:a + 1])
                for j in range(2):
                    last = e.activation(Bcb[:, j * QC:(j + 1) * QC],
                                        Bcb[:, j * QC:(j + 1) * QC],
                                        AF.Silu, bias=cbBC[:, j:j + 1])
                return last
            cs = P.do("scalar", silu_s, waits=((vv, cv), wl4, wl6), inc=ss)

            if stages < 5:
                continue
            f1 = WG.issue(P, "gpsimd", lambda e: e.dma_start(
                out=io["scrL2"][0, :, :], in_=dtr[:, 2 * QC:3 * QC]),
                waits=((vv, dv2),))
            f3 = WG.issue(P, "gpsimd", lambda e: e.dma_start(
                out=io["scrLe"][:, :], in_=dtr[:, 4 * QC:5 * QC]),
                waits=((ss, sp2),))
            lw = [(vv, z0), f1]
            if prev_geb_users is not None:
                lw.append(prev_geb_users)
            f1b = WG.issue(P, "gpsimd", lambda e: e.dma_start(
                out=lamF[:], in_=io["scrL2"][0].rearrange("p t -> (p t)").unsqueeze(0)),
                waits=tuple(lw))
            f3b = WG.issue(P, "gpsimd", lambda e: e.dma_start(
                out=lamef[:], in_=io["scrLe"].rearrange("p t -> (p t)").unsqueeze(0)),
                waits=(f3,))

            if stages < 6:
                continue
            gw = [(ss, cs), (vv, wv1)]
            if prev_geb_users is not None:
                gw.append(prev_geb_users)
            gm = P.do("tensor", lambda e: e.matmul(
                ps_t[:, 128:128 + QC], Bcb[0:64, 0:QC], Bcb[0:64, QC:2 * QC],
                start=True, stop=True), waits=gw, inc=mm)
            gev = P.do("scalar", lambda e: e.activation(
                Gt[:], ps_t[:, 128:128 + QC], AF.Copy), waits=((mm, gm),), inc=ss)

            if stages < 6.5:
                continue
            btm = P.do("tensor", lambda e: e.transpose(
                ps_t[:, 448:448 + 32].bitcast(BF16), Bcb[0:64, 0:QC],
                identB[0:64, 0:64]),
                waits=((ss, cs), (vv, wv1), (ss, gev), cid), inc=mm)
            bte = P.do("vector", lambda e: e.tensor_copy(
                Bst[:, 0:64], ps_t[:, 448:448 + 32].bitcast(BF16)),
                waits=((mm, btm),), inc=vv)

            if stages < 7:
                continue
            xev_ms = {}
            yg_ms = {}
            dh_ms = {}
            exp_ms = {}
            ce_ms = {}
            hups = []
            ygl = None
            for h in range(NH2):
                hp = h % 2
                hq = h % 4
                bank = hq
                dcol = (h % 16) * 128
                dw_ = [f1b, cmk, cid]
                if h >= 1:
                    dw_.append((ss, exp_ms[h - 1]))
                def d2mm(e, h=h, dcol=dcol):
                    e.matmul(ps_d[:, dcol:dcol + QC], ones_row[:],
                             lamF[0:1, h * QC:(h + 1) * QC],
                             start=True, stop=False)
                    return e.matmul(ps_d[:, dcol:dcol + QC], identB[:],
                                    maskB[:, (h % 4) * 128:(h % 4) * 128 + 128],
                                    start=False, stop=True)
                dm2 = P.do("tensor", d2mm, waits=dw_, inc=mm)
                ew_ = [(mm, dm2), (vv, tre), (ss, gev)]
                if h >= 4:
                    ew_.append((vv, yg_ms[h - 4]))
                ex = P.do("scalar", lambda e, h=h, hq=hq, dcol=dcol: e.activation(
                    geb[:, hq * QC:(hq + 1) * QC], ps_d[:, dcol:dcol + QC],
                    AF.Exp, bias=nlamT[:, h:h + 1]), waits=ew_, inc=ss)
                exp_ms[h] = ex
                gemv = P.do("vector", lambda e, hq=hq: e.tensor_tensor(
                    geb[:, hq * QC:(hq + 1) * QC], Gt[:],
                    geb[:, hq * QC:(hq + 1) * QC], ALU.mult),
                    waits=((ss, ex),), inc=vv)
                lmw = [f3b]
                if h >= 4:
                    lmw.append((vv, yg_ms[h - 4]))
                if h >= 1:
                    lmw.append((vv, ce_ms[h - 1]))
                if h < 4:
                    lmw.append((ss, bev))
                lem = P.do("tensor", lambda e, h=h, hq=hq: e.matmul(
                    ps_a[0:64, hq * 128:(hq + 1) * 128], orb[:, 0:64],
                    lamef[0:1, h * QC:(h + 1) * QC], start=True, stop=True),
                    waits=lmw, inc=mm)
                cw_ = [(mm, lem), (ss, cs)]
                if h >= 2:
                    cw_.append((vv, yg_ms[h - 2]))
                cev = P.do("vector", lambda e, h=h, hp=hp, hq=hq: e.tensor_tensor(
                    CEb[:, hp * QC:(hp + 1) * QC], Bcb[0:64, QC:2 * QC],
                    ps_a[0:64, hq * 128:(hq + 1) * 128], ALU.mult),
                    waits=cw_, inc=vv)
                ce_ms[h] = cev
                if h % 2 == 0:
                    xw_ = [(ss, cs), cid, (vv, tre)]
                    if h == 0:
                        xw_.append((vv, bte))
                    if h >= 2:
                        xw_.append((vv, xev_ms[h - 2]))
                    if h >= 1:
                        xw_.append((vv, hups[h - 1]))
                    xtm = P.do("tensor", lambda e, h=h: e.transpose(
                        ps_t[:, 256:256 + 64].bitcast(BF16),
                        xcb[:, (h // 2) * QC:(h // 2) * QC + QC],
                        identB[:]), waits=xw_, inc=mm)
                    xev = P.do("vector", lambda e: e.tensor_copy(
                        Xst[:, 0:128],
                        ps_t[:, 256:256 + 64].bitcast(BF16)),
                        waits=((mm, xtm),), inc=vv)
                    xev_ms[h] = xev
                    xev_ms[h + 1] = xev
                xdv = P.do("vector", lambda e, h=h, hp=hp: (
                    e.tensor_scalar(Xd[:, hp * 64:(hp + 1) * 64],
                                    Xst[:, (h % 2) * 64:(h % 2) * 64 + 64],
                                    dtT[:, h:h + 1], None, ALU.mult),
                    e.tensor_scalar(Xw[:, hp * 64:(hp + 1) * 64],
                                    Xst[:, (h % 2) * 64:(h % 2) * 64 + 64],
                                    wall[:, h:h + 1], None, ALU.mult))[-1],
                    waits=((vv, wv3), (vv, xev_ms[h])), inc=vv)
                hbw = [(vv, z0)]
                if hup_last is not None and h < 2:
                    hbw.append((vv, hup_last))
                if h >= 2:
                    hbw.append((vv, hups[h - 2]))
                hbc = P.do("vector", lambda e, h=h, hp=hp: e.tensor_copy(
                    Hbb[:, hp * 64:(hp + 1) * 64],
                    Hbuf[(h % 2) * 64:(h % 2) * 64 + 64,
                         (h // 2) * 64:(h // 2) * 64 + 64]),
                    waits=hbw, inc=vv)
                yw_ = [(vv, xdv), (vv, gemv), (vv, cev), (vv, hbc)]
                if h >= 4:
                    yw_.append((vv, yg_ms[h - 4]))
                if h >= 1:
                    yw_.append((vv, yg_ms[h - 1]))
                def ymm(e, h=h, hp=hp, hq=hq):
                    col = hq * 128
                    e.matmul(ps_y[:, col:col + QC], Xd[:, hp * 64:(hp + 1) * 64],
                             geb[:, hq * QC:(hq + 1) * QC], start=True, stop=False)
                    return e.matmul(ps_y[:, col:col + QC], Hbb[:, hp * 64:(hp + 1) * 64],
                                    CEb[:, hp * QC:(hp + 1) * QC], start=False, stop=True)
                ym = P.do("tensor", ymm, waits=yw_, inc=mm)
                dhw = [(vv, xdv), (vv, bte), (vv, xev)]
                if h >= 1:
                    dhw.append((vv, hups[h - 1]))
                dhm = P.do("tensor", lambda e, h=h, hp=hp: e.matmul(
                    ps_t[0:64, 384 + hp * 64:384 + hp * 64 + 64],
                    Bst[:, 0:64], Xw[:, hp * 64:(hp + 1) * 64],
                    start=True, stop=True), waits=dhw, inc=mm)
                hup = P.do("vector", lambda e, h=h, hp=hp: e.scalar_tensor_tensor(
                    Hbuf[(h % 2) * 64:(h % 2) * 64 + 64,
                         (h // 2) * 64:(h // 2) * 64 + 64],
                    Hbuf[(h % 2) * 64:(h % 2) * 64 + 64,
                         (h // 2) * 64:(h // 2) * 64 + 64],
                    gv[(h % 2) * 64:(h % 2) * 64 + 64, h:h + 1],
                    ps_t[0:64, 384 + hp * 64:384 + hp * 64 + 64],
                    ALU.mult, ALU.add), waits=((mm, dhm), (vv, hbc)), inc=vv)
                hups.append(hup)
                def yev(e, h=h, hq=hq):
                    col = hq * 128
                    ysl = (slice((h % 2) * 64, (h % 2) * 64 + 64),
                           slice((h // 2) * QC, (h // 2) * QC + QC))
                    e.scalar_tensor_tensor(ybuf[ysl[0], ysl[1]],
                                           xcs[(h % 2) * 64:(h % 2) * 64 + 64,
                                               (h // 2) * QC:(h // 2) * QC + QC],
                                           dpS[(h % 2) * 64:(h % 2) * 64 + 64, h:h + 1],
                                           ps_y[:, col:col + QC],
                                           ALU.mult, ALU.add)
                    e.tensor_tensor(ybuf[ysl[0], ysl[1]], ybuf[ysl[0], ysl[1]],
                                    zsil[(h % 2) * 64:(h % 2) * 64 + 64,
                                         (h // 2) * QC:(h // 2) * QC + QC], ALU.mult)
                    return e.tensor_tensor(ysq[ysl[0], ysl[1]],
                                           ybuf[ysl[0], ysl[1]],
                                           ybuf[ysl[0], ysl[1]], ALU.mult)
                yg = P.do("vector", yev, waits=((mm, ym), wl11), inc=vv)
                yg_ms[h] = yg
                ygl = yg
            hup_last = hups[-1]
            prev_geb_users = (mm, mm.n)
            prev_dt_users = (vv, ygl)

            if stages < 8:
                continue
            def ysum(e):
                for a in range(16):
                    last = e.matmul(ps_a[0:1, 0:QC], onescb[:],
                                    ysq[:, a * QC:(a + 1) * QC],
                                    start=(a == 0), stop=(a == 15))
                return last
            ysm = P.do("tensor", ysum, waits=((vv, ygl),), inc=mm)
            yr1 = P.do("scalar", lambda e: e.activation(
                rowq[0:1, 0:QC], ps_a[0:1, 0:QC], AF.Ln,
                bias=ctx.constcol(1e-5)[0:1, :], scale=1.0 / 2048.0),
                waits=((mm, ysm),), inc=ss)
            yr2 = P.do("scalar", lambda e: e.activation(
                rowq[0:1, 0:QC], rowq[0:1, 0:QC], AF.Exp, scale=-0.5), inc=ss)
            ybm = P.do("tensor", lambda e: e.matmul(
                ps_t[:, 384:384 + QC], ones_row[:], rowq[0:1, 0:QC],
                start=True, stop=True), waits=((ss, yr2), (vv, hups[-1]),
                                               (vv, hups[-2])), inc=mm)
            ygv = P.do("vector", lambda e: (
                [e.scalar_tensor_tensor(
                    ygn[(h % 2) * 64:(h % 2) * 64 + 64,
                        (h // 2) * QC:(h // 2) * QC + QC],
                    ybuf[(h % 2) * 64:(h % 2) * 64 + 64,
                         (h // 2) * QC:(h // 2) * QC + QC],
                    nwS[(h % 2) * 64:(h % 2) * 64 + 64, h:h + 1],
                    ps_t[(h % 2) * 64:(h % 2) * 64 + 64, 384:384 + QC],
                    ALU.mult, ALU.mult)
                 for h in range(NH2)])[-1], waits=((mm, ybm), wl12), inc=vv)

            if "dbg_xnb" in io:
                STP.issue(P, "sync", lambda e, c=c: e.dma_start(
                    out=io["dbg_xnb"][c], in_=xnb[:]), waits=((vv, ygl),))
                STP.issue(P, "sync", lambda e, c=c: e.dma_start(
                    out=io["dbg_xcs"][c], in_=xcs[:]), waits=((vv, ygl),))
                STP.issue(P, "sync", lambda e, c=c: e.dma_start(
                    out=io["dbg_zsil"][c], in_=zsil[:]), waits=((vv, ygl),))
                STP.issue(P, "sync", lambda e, c=c: e.dma_start(
                    out=io["dbg_dtr"][c], in_=dtr[:]), waits=((vv, ygl),))
                STP.issue(P, "sync", lambda e, c=c: e.dma_start(
                    out=io["dbg_ybuf"][c], in_=ybuf[:]), waits=((vv, ygv),))
                STP.issue(P, "sync", lambda e, c=c: e.dma_start(
                    out=io["dbg_ygn"][c], in_=ygn[:]), waits=((vv, ygv),))
                STP.issue(P, "sync", lambda e, c=c: e.dma_start(
                    out=io["dbg_hbuf"][c], in_=Hbuf[:]), waits=((vv, hups[-1]), (vv, hups[-2])))
                STP.issue(P, "sync", lambda e, c=c: e.dma_start(
                    out=io["dbg_bcb"][c], in_=Bcb[:]), waits=((vv, ygl),))
                STP.issue(P, "sync", lambda e, c=c: e.dma_start(
                    out=io["dbg_geb"][c], in_=geb[:]), waits=((vv, ygl),))

            if stages < 9:
                continue
            op_ms = []
            for dto in range(DT):
                fo = (dto % 2) * 512
                yo = (dto % 2) * QC
                w = [(vv, ygv)]
                if c == 0 and dto == 0:
                    w.append(wlo)
                if dto >= 2:
                    w.append((vv, op_ms[dto - 2]))
                def opmm(e, dto=dto, fo=fo):
                    for a in range(16):
                        last = e.matmul(
                            ps_a[:, fo:fo + QC],
                            owS[:, a * 1024 + dto * 128:a * 1024 + (dto + 1) * 128],
                            ygn[:, a * QC:(a + 1) * QC],
                            start=(a == 0), stop=(a == 15))
                    return last
                om = P.do("tensor", opmm, waits=w, inc=mm)
                ow_ = [(mm, om)]
                if len(ostores) >= 2:
                    ow_.append(ostores[-2])
                def oev(e, dto=dto, fo=fo, yo=yo):
                    return e.tensor_tensor(yout[:, yo:yo + QC], ps_a[:, fo:fo + QC],
                                           xt[:, dto * QC:(dto + 1) * QC], ALU.add)
                oe = P.do("vector", oev, waits=ow_, inc=vv)
                op_ms.append(oe)
                if li == nlayers - 1:
                    osr = WG.issue(P, "gpsimd", lambda e, c=c, dto=dto, yo=yo:
                                   e.dma_start(
                        out=io["yT"][dto * 128:(dto + 1) * 128, c * QC:(c + 1) * QC],
                        in_=yout[:, yo:yo + QC], accum_op=ALU.add),
                        waits=((vv, oe), ycp))
                else:
                    osr = STP.issue(P, "sync", lambda e, c=c, dto=dto, yo=yo:
                                    e.dma_start(
                        out=io["innerT"][dto * 128:(dto + 1) * 128, c * QC:(c + 1) * QC],
                        in_=yout[:, yo:yo + QC]), waits=((vv, oe),))
                ostores.append(osr)
            ye_prev = op_ms[-1]
            store_ms.append(ostores[-1])
        prev_layer_store = [ostores[-3], ostores[-2], ostores[-1]] if stages >= 9 else []

    drain_pools(P, ctx)
    return prev_layer_store


def plan_m2_prologue(P, ctx, io):
    TC = 128
    NCH = T // TC
    ones_row = ctx.sbuf("ones_row", [1, 128], F32)
    ones_col = ctx.sbuf("ones_col", [128, 1], F32)
    cpool = ctx.sbuf("cpool", [128, 16], F32)
    xt = ctx.sbuf("sxt", [128, DT * TC], F32)
    sq = ctx.sbuf("ssq", [128, DT * TC], F32)
    xo = ctx.sbuf("pxo", [128, DT * TC], F32)
    rowq = ctx.sbuf("srowq", [1, 4 * TC], F32)
    ps_m = ctx.psum("sps_a", [128, 1024], F32)
    mm = ctx.sem("mm"); vv = ctx.sem("vv"); ss = ctx.sem("ss")
    LD = ctx.pool("ld", 6)
    STP = ctx.pool("st", 3)
    LD.issue(P, "sync", lambda e: e.dma_start(out=cpool[:], in_=io["cpool"][:]))
    LD.issue(P, "sync", lambda e: e.dma_start(out=ones_row[:], in_=io["ones_row"][:]))
    LD.issue(P, "sync", lambda e: e.dma_start(out=ones_col[:], in_=io["ones_col"][:]))
    rv1_prev = None
    stores = []
    xn_prev = None
    for c in range(NCH):
        xl = LD.issue(P, "sync", lambda e, c=c: e.dma_start(
            out=sb3(xt[:], DT), in_=rearr_dT(io["xT"], c * TC, (c + 1) * TC)),
            waits=[stores[c - 2]] if c >= 2 else [])
        sv = P.do("scalar", lambda e: e.activation(sq[:], xt[:], AF.Square),
                  waits=(xl,) + (((vv, xn_prev),) if xn_prev else ()), inc=ss)
        def xstat(e):
            for a in range(DT):
                e.matmul(ps_m[0:1, 0:TC], ones_col[:],
                         xt[:, a * TC:(a + 1) * TC], start=(a == 0), stop=(a == DT - 1))
            for a in range(DT):
                last = e.matmul(ps_m[0:1, 512:512 + TC], ones_col[:],
                                sq[:, a * TC:(a + 1) * TC], start=(a == 0), stop=(a == DT - 1))
            return last
        xw = [(ss, sv)]
        if rv1_prev is not None:
            xw.append((vv, rv1_prev))
        xm = P.do("tensor", xstat, waits=xw, inc=mm)
        def rowv1(e):
            e.tensor_scalar(rowq[0:1, 0:TC], ps_m[0:1, 0:TC], 1.0 / D, None, ALU.mult)
            e.tensor_scalar(rowq[0:1, TC:2 * TC], ps_m[0:1, 512:512 + TC], 1.0 / D, None, ALU.mult)
            e.tensor_tensor(rowq[0:1, 2 * TC:3 * TC], rowq[0:1, 0:TC], rowq[0:1, 0:TC], ALU.mult)
            return e.tensor_tensor(rowq[0:1, TC:2 * TC], rowq[0:1, TC:2 * TC],
                                   rowq[0:1, 2 * TC:3 * TC], ALU.subtract)
        rv1 = P.do("vector", rowv1, waits=((mm, xm),), inc=vv)
        rv1_prev = rv1
        def rows1(e):
            e.activation(rowq[0:1, 2 * TC:3 * TC], rowq[0:1, TC:2 * TC], AF.Ln,
                         bias=ctx.constcol(1e-5)[0:1, :])
            return e.activation(rowq[0:1, 2 * TC:3 * TC], rowq[0:1, 2 * TC:3 * TC],
                                AF.Exp, scale=-0.5)
        rs1 = P.do("scalar", rows1, waits=((vv, rv1),), inc=ss)
        rv2 = P.do("vector", lambda e: e.scalar_tensor_tensor(
            rowq[0:1, 3 * TC:4 * TC], rowq[0:1, 0:TC], -1.0,
            rowq[0:1, 2 * TC:3 * TC], ALU.mult, ALU.mult), waits=((ss, rs1),), inc=vv)
        def xbcast(e):
            e.matmul(ps_m[:, 0:TC], ones_row[:], rowq[0:1, 2 * TC:3 * TC],
                     start=True, stop=True)
            return e.matmul(ps_m[:, 512:512 + TC], ones_row[:], rowq[0:1, 3 * TC:4 * TC],
                            start=True, stop=True)
        xbm = P.do("tensor", xbcast, waits=((vv, rv2),), inc=mm)
        def xnorm(e):
            for a in range(DT):
                e.tensor_tensor(sq[:, a * TC:(a + 1) * TC], xt[:, a * TC:(a + 1) * TC],
                                ps_m[:, 0:TC], ALU.mult)
                last = e.tensor_tensor(xo[:, a * TC:(a + 1) * TC],
                                       sq[:, a * TC:(a + 1) * TC],
                                       ps_m[:, 512:512 + TC], ALU.add)
            return last
        xnw = [(mm, xbm)]
        if xn_prev is not None:
            xnw.append((vv, xn_prev))
        if c >= 1:
            xnw.append(stores[c - 1])
        xn = P.do("vector", xnorm, waits=xnw, inc=vv)
        xn_prev = xn
        sr = STP.issue(P, "sync", lambda e, c=c: e.dma_start(
            out=rearr_dT(io["innerT"], c * TC, (c + 1) * TC), in_=sb3(xo[:], DT)),
            waits=((vv, xn),))
        stores.append(sr)
    return stores


def _consts():
    return {
        "ones_row": np.ones((1, 128), np.float32),
        "ones_col": np.ones((128, 1), np.float32),
    }


def run_ffn_standalone(xT_cores, w_up, w_down):
    ctx = Ctx()
    nc = ctx.nc
    io = {
        "xT": nc.dram_tensor("xT", [D, T], F32, kind="ExternalInput").ap(),
        "w_upT": nc.dram_tensor("w_upT", [D, 4096], F32, kind="ExternalInput").ap(),
        "w_downT": nc.dram_tensor("w_downT", [4096, D], F32, kind="ExternalInput").ap(),
        "ones_row": nc.dram_tensor("ones_row_d", [1, 128], F32, kind="ExternalInput").ap(),
        "cpool": nc.dram_tensor("cpool_d", [128, 16], F32, kind="ExternalInput").ap(),
        "ones_col": nc.dram_tensor("ones_col_d", [128, 1], F32, kind="ExternalInput").ap(),
        "yT": nc.dram_tensor("yT", [D, T], F32, kind="ExternalOutput").ap(),
    }
    P = Plan(ctx)
    plan_ffn(P, ctx, io)
    P.emit()
    cs = _consts()
    wupT = np.ascontiguousarray(w_up.T)
    wdnT = np.ascontiguousarray(w_down.T)
    cp = ctx.const_values()
    in_maps = [
        {"xT": xT_cores[k], "w_upT": wupT, "w_downT": wdnT,
         "ones_row_d": cs["ones_row"], "ones_col_d": cs["ones_col"],
         "cpool_d": cp}
        for k in range(len(xT_cores))
    ]
    res = run_bass_kernel_spmd(nc, in_maps, core_ids=list(range(len(xT_cores))))
    out = [res.results[k]["yT"] for k in range(len(xT_cores))]
    ctx.close()
    return out


BF16NP = ml_dtypes.bfloat16


def _base_consts():
    return {
        "ones_row_d": np.ones((1, 128), np.float32),
        "ones_col_d": np.ones((128, 1), np.float32),
    }


def _base_io(nc):
    return {
        "ones_row": nc.dram_tensor("ones_row_d", [1, 128], F32, kind="ExternalInput").ap(),
        "ones_col": nc.dram_tensor("ones_col_d", [128, 1], F32, kind="ExternalInput").ap(),
        "cpool": nc.dram_tensor("cpool_d", [128, 16], F32, kind="ExternalInput").ap(),
    }


def _launch(plan_fns, io_build, maps_build, n_cores=NB):
    ctx = Ctx()
    nc = ctx.nc
    io = _base_io(nc)
    io.update(io_build(nc))
    P = Plan(ctx)
    for fn in plan_fns:
        fn(P, ctx, io)
    P.emit()
    common = _base_consts()
    common["cpool_d"] = ctx.const_values()
    in_maps = []
    for k in range(n_cores):
        m = dict(common)
        m.update(maps_build(k))
        in_maps.append(m)
    res = run_bass_kernel_spmd(nc, in_maps, core_ids=list(range(n_cores)))
    out = [np.asarray(res.results[k]["yT"]) for k in range(n_cores)]
    ctx.close()
    return out


def _m2_launch(xT_cores, inputs, nlayers=4, nchunks=None, skip_m2=False, stages=99,
               dbg=False):
    f32 = np.float32
    c = np.ascontiguousarray
    NLm2 = 4
    m2_inw = np.asarray(inputs["L1_attn_a_in_w"], f32)
    m2_cw = np.asarray(inputs["L1_attn_a_conv_w"], f32)
    m2_cb = np.asarray(inputs["L1_attn_a_conv_b"], f32)
    m2_dtb = np.asarray(inputs["L1_attn_a_dt_bias"], f32)
    m2_alog = np.asarray(inputs["L1_attn_a_A_log"], f32)
    m2_dp = np.asarray(inputs["L1_attn_a_Dp"], f32)
    m2_nw = np.asarray(inputs["L1_attn_a_norm_w"], f32)
    m2_ow = np.asarray(inputs["L1_attn_a_out_w"], f32)
    m2_lnw = np.asarray(inputs["L1_attn_a_ln_w"], f32)
    m2_lnb = np.asarray(inputs["L1_attn_a_ln_b"], f32)
    identB_h = np.eye(128).astype(BF16NP)
    mask01 = np.zeros((128, 128), f32)
    for p in range(128):
        mask01[p, :] = np.where(p <= np.arange(128), 0.0, -1e30)
    maskrep_h = np.tile(mask01, (1, 4)).astype(BF16NP)

    def io_m2(nc):
        d = {
            "xT": nc.dram_tensor("xT", [1024, T], F32, kind="ExternalInput").ap(),
            "innerT": nc.dram_tensor("innerT", [1024, T], F32, kind="Internal").ap(),
            "identB": nc.dram_tensor("identB", [128, 128], BF16, kind="ExternalInput").ap(),
            "identF": nc.dram_tensor("identF", [128, 128], F32, kind="ExternalInput").ap(),
            "maskrep": nc.dram_tensor("maskrep", [128, 512], BF16, kind="ExternalInput").ap(),
            "ones_row_b": nc.dram_tensor("ones_row_b", [1, 128], BF16, kind="ExternalInput").ap(),
            "ones_col_b": nc.dram_tensor("ones_col_b", [128, 1], BF16, kind="ExternalInput").ap(),
            "scrL2": nc.dram_tensor("scrL2", [2, 32, 128], F32, kind="Internal").ap(),
            "scrLe": nc.dram_tensor("scrLe", [32, 128], BF16, kind="Internal").ap(),
            "yT": nc.dram_tensor("yT", [1024, T], F32, kind="ExternalOutput").ap(),
        }
        d["in_w"] = [nc.dram_tensor(f"in_w{i}", [1024, 4256], F32, kind="ExternalInput").ap() for i in range(NLm2)]
        d["outS"] = [nc.dram_tensor(f"outS{i}", [128, 16, 1024], F32, kind="ExternalInput").ap() for i in range(NLm2)]
        d["convX"] = [nc.dram_tensor(f"convX{i}", [128, 16, 4], F32, kind="ExternalInput").ap() for i in range(NLm2)]
        d["convbX"] = [nc.dram_tensor(f"convbX{i}", [128, 16], F32, kind="ExternalInput").ap() for i in range(NLm2)]
        d["convBC"] = [nc.dram_tensor(f"convBC{i}", [64, 2, 4], F32, kind="ExternalInput").ap() for i in range(NLm2)]
        d["convbBC"] = [nc.dram_tensor(f"convbBC{i}", [64, 2], F32, kind="ExternalInput").ap() for i in range(NLm2)]
        d["lnwS"] = [nc.dram_tensor(f"lnwS{i}", [128, 8], F32, kind="ExternalInput").ap() for i in range(NLm2)]
        d["lnbS"] = [nc.dram_tensor(f"lnbS{i}", [128, 8], F32, kind="ExternalInput").ap() for i in range(NLm2)]
        d["dtbS"] = [nc.dram_tensor(f"dtbS{i}", [32, 1], F32, kind="ExternalInput").ap() for i in range(NLm2)]
        d["AlogS"] = [nc.dram_tensor(f"AlogS{i}", [32, 1], F32, kind="ExternalInput").ap() for i in range(NLm2)]
        d["DpS"] = [nc.dram_tensor(f"DpS{i}", [128, 32], F32, kind="ExternalInput").ap() for i in range(NLm2)]
        d["nwSS"] = [nc.dram_tensor(f"nwSS{i}", [128, 32], F32, kind="ExternalInput").ap() for i in range(NLm2)]
        if dbg:
            NCH0 = T // 128
            d["dbg_xnb"] = nc.dram_tensor("dbg_xnb", [NCH0, 128, 8 * 128], BF16, kind="ExternalOutput").ap()
            d["dbg_xcs"] = nc.dram_tensor("dbg_xcs", [NCH0, 128, 16 * 128], BF16, kind="ExternalOutput").ap()
            d["dbg_zsil"] = nc.dram_tensor("dbg_zsil", [NCH0, 128, 16 * 128], BF16, kind="ExternalOutput").ap()
            d["dbg_dtr"] = nc.dram_tensor("dbg_dtr", [NCH0, 32, 5 * 128], F32, kind="ExternalOutput").ap()
            d["dbg_ybuf"] = nc.dram_tensor("dbg_ybuf", [NCH0, 128, 16 * 128], BF16, kind="ExternalOutput").ap()
            d["dbg_ygn"] = nc.dram_tensor("dbg_ygn", [NCH0, 128, 16 * 128], BF16, kind="ExternalOutput").ap()
            d["dbg_hbuf"] = nc.dram_tensor("dbg_hbuf", [NCH0, 128, 16 * 64], F32, kind="ExternalOutput").ap()
            d["dbg_bcb"] = nc.dram_tensor("dbg_bcb", [NCH0, 64, 2 * 128], BF16, kind="ExternalOutput").ap()
            d["dbg_amat"] = nc.dram_tensor("dbg_amat", [4, 32, 1], F32, kind="ExternalOutput").ap()
            d["dbg_lam"] = nc.dram_tensor("dbg_lam", [NCH0, 32, 256], F32, kind="ExternalOutput").ap()
            d["dbg_geb"] = nc.dram_tensor("dbg_geb", [NCH0, 128, 4 * 128], BF16, kind="ExternalOutput").ap()
        return d

    def plans_m2(P, ctx, io):
        io["_pro_stores"] = plan_m2_prologue(P, ctx, io)
        if not skip_m2:
            plan_mamba2(P, ctx, io, nlayers=nlayers, nchunks=nchunks, stages=stages)

    m2_w = {
        "identB": identB_h, "identF": np.eye(128, dtype=f32),
        "maskrep": maskrep_h,
        "ones_row_b": np.ones((1, 128), BF16NP),
        "ones_col_b": np.ones((128, 1), BF16NP),
    }
    for i in range(NLm2):
        m2_w[f"in_w{i}"] = m2_inw[i]
        oS = np.zeros((128, 16, 1024), f32)
        for h in range(32):
            oS[(h % 2) * 64:(h % 2) * 64 + 64, h // 2, :] = m2_ow[i, h * 64:(h + 1) * 64, :]
        m2_w[f"outS{i}"] = oS
        m2_w[f"convX{i}"] = c(m2_cw[i, :, :2048].reshape(4, 16, 128).transpose(2, 1, 0))
        m2_w[f"convbX{i}"] = c(m2_cb[i, :2048].reshape(16, 128).T)
        m2_w[f"convBC{i}"] = c(m2_cw[i, :, 2048:].reshape(4, 2, 64).transpose(2, 1, 0))
        m2_w[f"convbBC{i}"] = c(m2_cb[i, 2048:].reshape(2, 64).T)
        m2_w[f"lnwS{i}"] = c(m2_lnw[i].reshape(8, 128).T)
        m2_w[f"lnbS{i}"] = c(m2_lnb[i].reshape(8, 128).T)
        m2_w[f"dtbS{i}"] = c(m2_dtb[i][:, None])
        m2_w[f"AlogS{i}"] = c(m2_alog[i][:, None])
        m2_w[f"DpS{i}"] = c(np.broadcast_to(m2_dp[i][None, :], (128, 32)))
        m2_w[f"nwSS{i}"] = c(np.tile(m2_nw[i].reshape(32, 64).T, (2, 1)))
    if dbg:
        ctx = Ctx()
        nc = ctx.nc
        io = _base_io(nc)
        io.update(io_m2(nc))
        P = Plan(ctx)
        plans_m2(P, ctx, io)
        P.emit()
        common = _base_consts()
        common["cpool_d"] = ctx.const_values()
        in_maps = [dict(common, **m2_w, xT=np.ascontiguousarray(xT_cores[k]))
                   for k in range(len(xT_cores))]
        res = run_bass_kernel_spmd(nc, in_maps, core_ids=list(range(len(xT_cores))))
        out = [dict(res.results[k]) for k in range(len(xT_cores))]
        ctx.close()
        return out
    return _launch([plans_m2], io_m2,
                   lambda k: dict(m2_w, xT=np.ascontiguousarray(xT_cores[k])),
                   n_cores=len(xT_cores))


def run_m2_standalone(xT_cores, inputs, nlayers=4, nchunks=None, skip_m2=False, stages=99,
                      dbg=False):
    return _m2_launch(xT_cores, inputs, nlayers=nlayers, nchunks=nchunks, skip_m2=skip_m2,
                      stages=stages, dbg=dbg)


def kernel(**inputs):
    f32 = np.float32
    c = np.ascontiguousarray
    x0 = np.asarray(inputs["L0_attn_a_x"], f32)

    Mq = np.asarray(inputs["L0_attn_a_Mq"], f32)
    Mk = np.asarray(inputs["L0_attn_a_Mk"], f32)
    Mv = np.asarray(inputs["L0_attn_a_Mv"], f32)
    ho = np.zeros((128, 2), f32); ho[0:64, 0] = 1; ho[64:128, 1] = 1

    def io_aa(nc):
        return {
            "xT": nc.dram_tensor("xT", [D, T], F32, kind="ExternalInput").ap(),
            "Mq": nc.dram_tensor("Mq", [1024, 512], F32, kind="ExternalInput").ap(),
            "Mk": nc.dram_tensor("Mk", [1024, 512], F32, kind="ExternalInput").ap(),
            "Mv": nc.dram_tensor("Mv", [1024, 1024], F32, kind="ExternalInput").ap(),
            "halfones": nc.dram_tensor("halfones", [128, 2], F32, kind="ExternalInput").ap(),
            "scrA": nc.dram_tensor("scrA", [2, 2], F32, kind="Internal").ap(),
            "yT": nc.dram_tensor("yT", [D, T], F32, kind="ExternalOutput").ap(),
        }
    y1 = _launch([plan_attn_a], io_aa,
                 lambda k: {"xT": c(x0[k].T), "Mq": Mq, "Mk": Mk, "Mv": Mv,
                            "halfones": ho})

    ew = np.asarray(inputs["L0_ffn_expert_weights"], f32)
    w1 = np.asarray(inputs["L0_ffn_w1"], f32)
    w2 = np.asarray(inputs["L0_ffn_w2"], f32)
    w2S = c(w2.reshape(64, 128, 8, 128).transpose(1, 2, 0, 3))
    ewT = c(ew.T)

    def io_moe(nc):
        return {
            "xT": nc.dram_tensor("xT", [1024, 512], F32, kind="ExternalInput").ap(),
            "w1": nc.dram_tensor("w1", [1024, 8192], F32, kind="ExternalInput").ap(),
            "w2S": nc.dram_tensor("w2S", [128, 8, 64, 128], F32, kind="ExternalInput").ap(),
            "ewT": nc.dram_tensor("ewT", [2, 512], F32, kind="ExternalInput").ap(),
            "yT": nc.dram_tensor("yT", [1024, 512], F32, kind="ExternalOutput").ap(),
        }

    def run_moe(xb0T):
        outs = _launch([plan_moe], io_moe,
                       lambda k: {"xT": c(xb0T[:, k * 512:(k + 1) * 512]),
                                  "w1": w1, "w2S": w2S,
                                  "ewT": c(ewT[:, k * 512:(k + 1) * 512])})
        return np.concatenate(outs, axis=1)

    y2 = list(y1)
    y2[0] = run_moe(y1[0])

    ipw = np.asarray(inputs["L0_attn_b_in_proj_w"], f32)
    cw = np.asarray(inputs["L0_attn_b_conv_w"], f32)
    cb = np.asarray(inputs["L0_attn_b_conv_b"], f32)
    xpw = np.asarray(inputs["L0_attn_b_x_proj_w"], f32)
    dtw = np.asarray(inputs["L0_attn_b_dt_proj_w"], f32)
    dtb = np.asarray(inputs["L0_attn_b_dt_proj_b"], f32)
    alog = np.asarray(inputs["L0_attn_b_A_log"], f32)
    opw = np.asarray(inputs["L0_attn_b_out_proj_w"], f32)
    identB_h = np.eye(128).astype(BF16NP)

    def io_m1(nc):
        return {
            "xT": nc.dram_tensor("xT", [1024, T], F32, kind="ExternalInput").ap(),
            "inprojT": nc.dram_tensor("inprojT", [1024, 2048], F32, kind="ExternalInput").ap(),
            "convW": nc.dram_tensor("convW", [1024, 4], F32, kind="ExternalInput").ap(),
            "convB": nc.dram_tensor("convB", [128, 8], F32, kind="ExternalInput").ap(),
            "xprojB": nc.dram_tensor("xprojB", [1024, 16], F32, kind="ExternalInput").ap(),
            "xprojC": nc.dram_tensor("xprojC", [1024, 16], F32, kind="ExternalInput").ap(),
            "xprojD": nc.dram_tensor("xprojD", [128, 8], F32, kind="ExternalInput").ap(),
            "dtW": nc.dram_tensor("dtW", [128, 8], F32, kind="ExternalInput").ap(),
            "dtB": nc.dram_tensor("dtB", [128, 8], F32, kind="ExternalInput").ap(),
            "Alog": nc.dram_tensor("Alog", [1024, 16], F32, kind="ExternalInput").ap(),
            "outprojT": nc.dram_tensor("outprojT", [1024, 1024], F32, kind="ExternalInput").ap(),
            "identB": nc.dram_tensor("identB", [128, 128], BF16, kind="ExternalInput").ap(),
            "ones_row_b": nc.dram_tensor("ones_row_b", [1, 128], BF16, kind="ExternalInput").ap(),
            "scrB": nc.dram_tensor("scrB", [16, M1_TC], BF16, kind="Internal").ap(),
            "scrC": nc.dram_tensor("scrC", [16, M1_TC], BF16, kind="Internal").ap(),
            "yT": nc.dram_tensor("yT", [1024, T], F32, kind="ExternalOutput").ap(),
        }
    m1_w = {
        "inprojT": c(ipw.T), "convW": c(cw[:, 0, :]),
        "convB": c(cb.reshape(8, 128).T),
        "xprojB": c(xpw[:16].T), "xprojC": c(xpw[16:32].T),
        "xprojD": c(xpw[32].reshape(8, 128).T),
        "dtW": c(dtw[:, 0].reshape(8, 128).T), "dtB": c(dtb.reshape(8, 128).T),
        "Alog": alog, "outprojT": c(opw.T),
        "identB": identB_h, "ones_row_b": np.ones((1, 128), BF16NP),
    }
    y3 = _launch([plan_mamba1], io_m1,
                 lambda k: dict(m1_w, xT=c(y2[k])))

    y4 = list(y3)
    y4[0] = run_moe(y3[0])

    y5 = _m2_launch([c(y4[k]) for k in range(NB)], inputs)

    w_up = np.asarray(inputs["L1_ffn_w_up"], f32)
    w_down = np.asarray(inputs["L1_ffn_w_down"], f32)
    wupT = c(w_up.T)
    wdnT = c(w_down.T)

    def io_ffn(nc):
        return {
            "xT": nc.dram_tensor("xT", [D, T], F32, kind="ExternalInput").ap(),
            "w_upT": nc.dram_tensor("w_upT", [D, 4096], F32, kind="ExternalInput").ap(),
            "w_downT": nc.dram_tensor("w_downT", [4096, D], F32, kind="ExternalInput").ap(),
            "yT": nc.dram_tensor("yT", [D, T], F32, kind="ExternalOutput").ap(),
        }

    def run_ffn(xs):
        return _launch([plan_ffn], io_ffn,
                       lambda k: {"xT": c(xs[k]), "w_upT": wupT, "w_downT": wdnT})

    y6 = run_ffn(y5)

    Wkv = np.asarray(inputs["L1_attn_b_W_kv_down"], f32)
    Wkup = np.asarray(inputs["L1_attn_b_W_k_up"], f32)
    Wvup = np.asarray(inputs["L1_attn_b_W_v_up"], f32)
    Wq = np.asarray(inputs["L1_attn_b_W_q"], f32)
    Wqr = np.asarray(inputs["L1_attn_b_W_q_rope"], f32)
    Wkr = np.asarray(inputs["L1_attn_b_W_k_rope"], f32)
    Wo = np.asarray(inputs["L1_attn_b_W_o"], f32)
    inv_freq = 1.0 / 10000.0 ** (np.arange(0, 32, 2) / 32.0)
    freqs = np.outer(np.arange(T), inv_freq).astype(f32)
    cosT_h = c(np.cos(freqs).T)
    sinT_h = c(np.sin(freqs).T)
    stripm = np.zeros((128, 1024), BF16NP)
    for p in range(128):
        stripm[p, :] = (p <= np.arange(1024) - 512)
    WqS_h = np.zeros((128, 16, 512), f32)
    for h in range(16):
        for a in range(8):
            WqS_h[:, h, a * 64:(a + 1) * 64] = Wq[a * 128:(a + 1) * 128, h * 64:(h + 1) * 64]
    WoS2_h = np.zeros((64, 8, 16, 128), f32)
    for h in range(16):
        for dto in range(8):
            WoS2_h[:, dto, h, :] = Wo[h * 64:(h + 1) * 64, dto * 128:(dto + 1) * 128]
    NCHm = T // 512

    def io_mla(nc):
        return {
            "xT": nc.dram_tensor("xT", [1024, T], F32, kind="ExternalInput").ap(),
            "Wkv": nc.dram_tensor("Wkv", [1024, 128], F32, kind="ExternalInput").ap(),
            "Wkup": nc.dram_tensor("Wkup", [128, 1024], F32, kind="ExternalInput").ap(),
            "Wvup": nc.dram_tensor("Wvup", [128, 1024], F32, kind="ExternalInput").ap(),
            "WqS": nc.dram_tensor("WqS", [128, 16, 512], F32, kind="ExternalInput").ap(),
            "Wqr": nc.dram_tensor("Wqr", [1024, 32], F32, kind="ExternalInput").ap(),
            "Wkr": nc.dram_tensor("Wkr", [128, 32], F32, kind="ExternalInput").ap(),
            "WoS2": nc.dram_tensor("WoS2", [64, 8, 16, 128], F32, kind="ExternalInput").ap(),
            "cosT": nc.dram_tensor("cosT", [16, T], F32, kind="ExternalInput").ap(),
            "sinT": nc.dram_tensor("sinT", [16, T], F32, kind="ExternalInput").ap(),
            "strip": nc.dram_tensor("strip", [128, 1024], BF16, kind="ExternalInput").ap(),
            "scrR": nc.dram_tensor("scrR", [4, 16, NCHm, 512], BF16, kind="Internal").ap(),
            "anD": nc.dram_tensor("anD", [16, NCHm, 64, 512], BF16, kind="Internal").ap(),
            "yT": nc.dram_tensor("yT", [1024, T], F32, kind="ExternalOutput").ap(),
        }
    mla_w = {"Wkv": Wkv, "Wkup": Wkup, "Wvup": Wvup, "WqS": WqS_h, "Wqr": Wqr,
             "Wkr": Wkr, "WoS2": WoS2_h, "cosT": cosT_h, "sinT": sinT_h,
             "strip": stripm}
    y7 = _launch([plan_mla], io_mla,
                 lambda k: dict(mla_w, xT=c(y6[k])))

    y8 = run_ffn(y7)

    return np.stack([y8[k].T for k in range(NB)]).astype(np.float32)


# version 04 (final)
# speedup vs baseline: 1.0005x; 1.0005x over previous
import math
import contextlib
import numpy as np
import ml_dtypes

import concourse.bass as bass
import concourse.mybir as mybir
from concourse.bass_utils import run_bass_kernel_spmd
from concourse.alu_op_type import AluOpType as ALU

F32 = mybir.dt.float32
BF16 = mybir.dt.bfloat16
AF = mybir.ActivationFunctionType
AX = mybir.AxisListType.X

D = 1024
T = 4096
NB = 8
DT = D // 128

ENGINES = ("sync", "gpsimd", "tensor", "vector", "scalar")


class Sem:
    __slots__ = ("h", "n", "name")

    def __init__(self, handle, name):
        self.h = handle
        self.n = 0
        self.name = name

    def inc(self, k=1):
        self.n += k
        return self.n


class DmaPool:

    def __init__(self, ctx, name, n=4):
        self.sems = [ctx.sem(f"{name}{i}") for i in range(n)]
        self.i = 0

    def issue(self, P, eng, fn, waits=()):
        s = self.sems[self.i % len(self.sems)]
        self.i += 1
        pre = ((s, s.n),)
        P.do(eng, fn, waits=tuple(waits) + pre, inc=s, incby=16)
        return (s, s.n)


class Ctx:

    def __init__(self, detect_races=True):
        self.nc = bass.Bass("TRN2", target_bir_lowering=False,
                            detect_race_conditions=detect_races)
        self.stack = contextlib.ExitStack()
        self.sems = {}
        self._bufs = {}
        self.sbuf_bytes = 0

    def sem(self, name):
        if name not in self.sems:
            self.sems[name] = Sem(
                self.stack.enter_context(self.nc.semaphore(name)), name)
        return self.sems[name]

    def pool(self, name, n=4):
        if not hasattr(self, "_pools"):
            self._pools = {}
        if name not in self._pools:
            self._pools[name] = DmaPool(self, name, n)
        return self._pools[name]

    def sbuf(self, name, shape, dtype):
        key = ("sb", name)
        if key not in self._bufs:
            self._bufs[key] = self.stack.enter_context(
                self.nc.sbuf_tensor("sb_" + name, list(shape), dtype))
            self.sbuf_bytes += shape[0] * shape[1] * (2 if dtype == BF16 else 4)
        return self._bufs[key]

    def psum(self, name, shape, dtype=F32):
        key = ("ps", name)
        if key not in self._bufs:
            self._bufs[key] = self.stack.enter_context(
                self.nc.psum_tensor(name, list(shape), dtype))
        return self._bufs[key]

    def constcol(self, value):
        if not hasattr(self, "_consts"):
            self._consts = {}
            self.cpool = self.sbuf("cpool", [128, 16], F32)
        if value not in self._consts:
            assert len(self._consts) < 16
            self._consts[value] = len(self._consts)
        i = self._consts[value]
        return self.cpool[:, i:i + 1]

    def const_values(self):
        vals = np.zeros((128, 16), np.float32)
        if hasattr(self, "_consts"):
            for v, i in self._consts.items():
                vals[:, i] = v
        return vals

    def close(self):
        self.stack.close()


class Plan:

    def __init__(self, ctx):
        self.ctx = ctx
        self.q = {e: [] for e in ENGINES}

    def do(self, eng, fn, waits=(), inc=None, incby=1):
        waits = tuple((s, v) for (s, v) in waits)

        def thunk(e, fn=fn, waits=waits, inc=inc, incby=incby):
            for s, v in waits:
                e.wait_ge(s.h, v)
            instr = fn(e)
            if inc is not None:
                if instr is None:
                    instr = e.nop()
                instr.then_inc(inc.h, incby)
        self.q[eng].append(thunk)
        if inc is not None:
            inc.inc(incby)
        return inc.n if inc is not None else None

    def wait(self, eng, sem, val):
        self.do(eng, lambda e: None, waits=((sem, val),))

    def emit(self):
        nc = self.ctx.nc
        with nc.Block() as block:
            @block.sync
            def _(e):
                for fn in self.q["sync"]:
                    fn(e)

            @block.gpsimd
            def _(e):
                for fn in self.q["gpsimd"]:
                    fn(e)

            @block.tensor
            def _(e):
                for fn in self.q["tensor"]:
                    fn(e)

            @block.vector
            def _(e):
                for fn in self.q["vector"]:
                    fn(e)

            @block.scalar
            def _(e):
                for fn in self.q["scalar"]:
                    fn(e)


def drain_pools(P, ctx, eng="gpsimd"):
    if hasattr(ctx, "_pools"):
        for pool in ctx._pools.values():
            for s in pool.sems:
                if s.n:
                    P.wait(eng, s, s.n)


def rearr_dT(ap, c0, c1):
    return ap[:, c0:c1].rearrange("(a p) t -> p a t", p=128)


def sb3(ap, n):
    return ap.rearrange("p (a t) -> p a t", a=n)



def plan_plain_ln(P, ctx, *, xt, TC, sq, ps_st, st_col, rowm, rowr, ps_bc,
                  bc_col, xnb, ld_sem, ld_val, done_sem,
                  ones_row, ones_col, eps=1e-5, pref=""):
    mm = ctx.sem("mm"); vv = ctx.sem("vv"); ss = ctx.sem("ss")

    v = P.do("scalar", lambda e: e.activation(sq[:], xt[:], AF.Square),
             waits=((ld_sem, ld_val),), inc=ss)
    def statmm(e):
        for a in range(DT):
            e.matmul(ps_st[0:1, st_col:st_col + TC], ones_col[:],
                     xt[:, a * TC:(a + 1) * TC], start=(a == 0), stop=(a == DT - 1))
        for a in range(DT):
            e.matmul(ps_st[0:1, st_col + TC:st_col + 2 * TC], ones_col[:],
                     sq[:, a * TC:(a + 1) * TC], start=(a == 0), stop=(a == DT - 1))
        return e.nop()
    m1 = P.do("tensor", statmm, waits=((ss, v), (ld_sem, ld_val)), inc=mm)

    def rowmath_v(e):
        e.tensor_scalar(rowm[:], ps_st[0:1, st_col:st_col + TC], 1.0 / D, None, ALU.mult)
        e.tensor_scalar(rowr[:], ps_st[0:1, st_col + TC:st_col + 2 * TC], 1.0 / D, None, ALU.mult)
        e.tensor_tensor(sq[0:1, 0:TC], rowm[:], rowm[:], ALU.mult)
        return e.tensor_tensor(rowr[:], rowr[:], sq[0:1, 0:TC], ALU.subtract)
    v2 = P.do("vector", rowmath_v, waits=((mm, m1),), inc=vv)

    def rowmath_s(e):
        e.activation(rowr[:], rowr[:], AF.Ln, bias=eps)
        return e.activation(rowr[:], rowr[:], AF.Exp, scale=-0.5)
    s2 = P.do("scalar", rowmath_s, waits=((vv, v2),), inc=ss)

    v3 = P.do("vector",
              lambda e: e.scalar_tensor_tensor(rowm[:], rowm[:], -1.0, rowr[:],
                                               ALU.mult, ALU.mult),
              waits=((ss, s2),), inc=vv)

    def bcast(e):
        e.matmul(ps_bc[:, bc_col:bc_col + TC], ones_row[:], rowr[:],
                 start=True, stop=True)
        e.matmul(ps_bc[:, bc_col + TC:bc_col + 2 * TC], ones_row[:], rowm[:],
                 start=True, stop=True)
        return e.nop()
    m2 = P.do("tensor", bcast, waits=((vv, v3),), inc=mm)

    def norm(e):
        for a in range(DT):
            e.tensor_tensor(sq[:, a * TC:(a + 1) * TC], xt[:, a * TC:(a + 1) * TC],
                            ps_bc[:, bc_col:bc_col + TC], ALU.mult)
            last = e.tensor_tensor(xnb[:, a * TC:(a + 1) * TC],
                                   sq[:, a * TC:(a + 1) * TC],
                                   ps_bc[:, bc_col + TC:bc_col + 2 * TC], ALU.add)
        return last
    v4 = P.do("vector", norm, waits=((mm, m2),), inc=done_sem)
    return v4


def plan_ffn(P, ctx, io, uid="f"):
    FF = 4096
    FT = FF // 128
    TC = 128
    NCH = T // TC

    ones_row = ctx.sbuf("ones_row", [1, 128], F32)
    ones_col = ctx.sbuf("ones_col", [128, 1], F32)

    wup_t = ctx.sbuf(uid + "wup_t", [128, DT * FF], BF16)
    wdn_t = ctx.sbuf(uid + "wdn_t", [128, FT * D], BF16)
    raw = ctx.sbuf(uid + "raw", [128, 2 * 2048], F32)
    absw = ctx.sbuf(uid + "absw", [128, 2048], F32)
    sgnw = ctx.sbuf(uid + "sgnw", [128, 2048], F32)
    asum = ctx.sbuf(uid + "asum", [128, 32], F32)
    alpha = ctx.sbuf(uid + "alpha", [1, 2], F32)
    albc = ctx.sbuf(uid + "albc", [128, 2], F32)

    xt = ctx.sbuf(uid + "xt", [128, 2 * DT * TC], F32)
    sq = ctx.sbuf(uid + "sq", [128, DT * TC], F32)
    xnb = ctx.sbuf(uid + "xnb", [128, DT * TC], BF16)
    hgb = ctx.sbuf(uid + "hgb", [128, FT * TC], BF16)
    hsq = ctx.sbuf(uid + "hsq", [128, 2 * TC], F32)
    rowa = ctx.sbuf(uid + "rowa", [1, 4 * TC], F32)
    hrow = ctx.sbuf(uid + "hrow", [1, TC], F32)
    sbc = ctx.sbuf(uid + "sbc", [128, TC], F32)
    yout = ctx.sbuf(uid + "yout", [128, DT * TC], F32)

    ps_out = ctx.psum(uid + "ps_out", [128, DT * TC], F32)
    ps_up = ctx.psum(uid + "ps_up", [128, 1024], F32)
    ps_misc = ctx.psum(uid + "ps_misc", [128, 1024], F32)

    mm = ctx.sem("mm"); vv = ctx.sem("vv"); ss = ctx.sem("ss")
    LD = ctx.pool("ld", 6)
    STP = ctx.pool("st", 3)

    cpool = ctx.sbuf("cpool", [128, 16], F32)
    LD.issue(P, "sync", lambda e: e.dma_start(out=cpool[:], in_=io["cpool"][:]))
    LD.issue(P, "sync", lambda e: e.dma_start(out=ones_row[:], in_=io["ones_row"][:]))
    cns = LD.issue(P, "sync", lambda e: e.dma_start(out=ones_col[:], in_=io["ones_col"][:]))

    wupR = io["w_upT"].rearrange("(a p) f -> p a f", p=128)
    wdnR = io["w_downT"].rearrange("(a p) f -> p a f", p=128)

    red_ms = []
    for j in range(32):
        par = j % 2
        if j < 16:
            view = wupR[:, j // 2, (j % 2) * 2048:(j % 2) * 2048 + 2048]
        else:
            jj = j - 16
            view = wdnR[:, 2 * jj:2 * jj + 2, :]
        waits = [(vv, red_ms[j - 2])] if j >= 2 else []
        lv = LD.issue(P, "sync", lambda e, par=par, view=view: e.dma_start(
            out=(raw[:, par * 2048:(par + 1) * 2048] if view.ndim == 2
                 else sb3(raw[:, par * 2048:(par + 1) * 2048], 2)),
            in_=view),
            waits=waits)
        rv = P.do("vector", lambda e, j=j, par=par: e.tensor_reduce(
            asum[:, j:j + 1], raw[:, par * 2048:(par + 1) * 2048], AX, ALU.add,
            apply_absolute_value=True),
            waits=(lv, (cns[0], cns[1])), inc=vv)
        red_ms.append(rv)

    def alph_mm(e):
        e.matmul(ps_misc[0:1, 0:16], ones_col[:], asum[:, 0:16], start=True, stop=True)
        return e.matmul(ps_misc[0:1, 16:32], ones_col[:], asum[:, 16:32], start=True, stop=True)
    am = P.do("tensor", alph_mm, waits=((vv, red_ms[-1]),), inc=mm)
    av = P.do("vector", lambda e: e.tensor_reduce(
        alpha[0:1, 0:1], ps_misc[0:1, 0:16], AX, ALU.add), waits=((mm, am),), inc=vv)
    av2 = P.do("vector", lambda e: e.tensor_reduce(
        alpha[0:1, 1:2], ps_misc[0:1, 16:32], AX, ALU.add), inc=vv)
    asc = P.do("scalar", lambda e: e.activation(
        alpha[0:1, 0:2], alpha[0:1, 0:2], AF.Copy, scale=1.0 / (2.0 * D * FF)),
        waits=((vv, av2),), inc=ss)
    abm = P.do("tensor", lambda e: e.matmul(
        ps_misc[:, 32:34], ones_row[:], alpha[0:1, 0:2], start=True, stop=True),
        waits=((ss, asc),), inc=mm)
    abv = P.do("vector", lambda e: e.tensor_copy(albc[:, 0:2], ps_misc[:, 32:34]),
               waits=((mm, abm),), inc=vv)

    tern_s = []; tern_v = []
    for j in range(32):
        par = j % 2
        if j < 16:
            view = wupR[:, j // 2, (j % 2) * 2048:(j % 2) * 2048 + 2048]
        else:
            jj = j - 16
            view = wdnR[:, 2 * jj:2 * jj + 2, :]
        dst = wup_t if j < 16 else wdn_t
        col = (j if j < 16 else j - 16) * 2048
        acol = 0 if j < 16 else 1
        waits = [(vv, red_ms[-1])] if j < 2 else [(ss, tern_s[j - 2])]
        lv = LD.issue(P, "sync", lambda e, par=par, view=view: e.dma_start(
            out=(raw[:, par * 2048:(par + 1) * 2048] if view.ndim == 2
                 else sb3(raw[:, par * 2048:(par + 1) * 2048], 2)),
            in_=view),
            waits=waits)
        swaits = [lv]
        if j == 0:
            swaits.append((vv, abv))
        if j >= 1:
            swaits.append((vv, tern_v[j - 1]))
        def tern_sc(e, par=par):
            e.activation(absw[:], raw[:, par * 2048:(par + 1) * 2048], AF.Abs)
            return e.activation(sgnw[:], raw[:, par * 2048:(par + 1) * 2048], AF.Sign)
        sv = P.do("scalar", tern_sc, waits=swaits, inc=ss)
        tern_s.append(sv)
        def tern_ve(e, dst=dst, col=col, acol=acol):
            e.tensor_scalar(absw[:], absw[:], albc[:, acol:acol + 1], None, ALU.is_gt)
            return e.tensor_tensor(dst[:, col:col + 2048], sgnw[:], absw[:], ALU.mult)
        tv = P.do("vector", tern_ve, waits=((ss, sv),), inc=vv)
        tern_v.append(tv)
    tern_all = tern_v[-1]

    MU, SQc, B1, B2 = 0, 256, 512, 768
    store_ms = []
    ye_prev = None
    rv1_prev = None
    for c in range(NCH):
        par = c % 2
        xo = par * DT * TC
        waits = []
        if c >= 2:
            waits.append(store_ms[c - 2])
        xl = LD.issue(P, "sync", lambda e, c=c, xo=xo: e.dma_start(
            out=sb3(xt[:, xo:xo + DT * TC], DT),
            in_=rearr_dT(io["xT"], c * TC, (c + 1) * TC)),
            waits=waits)

        swaits = [xl]
        if ye_prev is not None:
            swaits.append((vv, ye_prev))
        sv = P.do("scalar", lambda e, xo=xo: e.activation(
            sq[:], xt[:, xo:xo + DT * TC], AF.Square),
            waits=swaits, inc=ss)

        def xstat(e, xo=xo):
            for a in range(DT):
                e.matmul(ps_misc[0:1, MU:MU + TC], ones_col[:],
                         xt[:, xo + a * TC:xo + (a + 1) * TC],
                         start=(a == 0), stop=(a == DT - 1))
            for a in range(DT):
                last = e.matmul(ps_misc[0:1, SQc:SQc + TC], ones_col[:],
                                sq[:, a * TC:(a + 1) * TC],
                                start=(a == 0), stop=(a == DT - 1))
            return last
        xw = [(ss, sv)]
        if rv1_prev is not None:
            xw.append((vv, rv1_prev))
        xm = P.do("tensor", xstat, waits=xw, inc=mm)

        def rowv1(e):
            e.tensor_scalar(rowa[0:1, 0:TC], ps_misc[0:1, MU:MU + TC], 1.0 / D, None, ALU.mult)
            e.tensor_scalar(rowa[0:1, TC:2 * TC], ps_misc[0:1, SQc:SQc + TC], 1.0 / D, None, ALU.mult)
            e.tensor_tensor(rowa[0:1, 2 * TC:3 * TC], rowa[0:1, 0:TC], rowa[0:1, 0:TC], ALU.mult)
            return e.tensor_tensor(rowa[0:1, TC:2 * TC], rowa[0:1, TC:2 * TC],
                                   rowa[0:1, 2 * TC:3 * TC], ALU.subtract)
        rv1 = P.do("vector", rowv1, waits=((mm, xm),), inc=vv)
        rv1_prev = rv1

        def rows1(e):
            e.activation(rowa[0:1, 2 * TC:3 * TC], rowa[0:1, TC:2 * TC], AF.Ln,
                         bias=ctx.constcol(1e-5)[0:1, :])
            e.activation(rowa[0:1, 2 * TC:3 * TC], rowa[0:1, 2 * TC:3 * TC], AF.Exp, scale=-0.5)
            return e.activation(rowa[0:1, 3 * TC:4 * TC], rowa[0:1, TC:2 * TC], AF.Sqrt)
        rs1 = P.do("scalar", rows1, waits=((vv, rv1),), inc=ss)

        def rowv2(e):
            e.tensor_tensor(rowa[0:1, TC:2 * TC], rowa[0:1, 3 * TC:4 * TC],
                            rowa[0:1, 2 * TC:3 * TC], ALU.mult)
            e.tensor_scalar(rowa[0:1, TC:2 * TC], rowa[0:1, TC:2 * TC], 1e-8, None, ALU.add)
            e.reciprocal(rowa[0:1, TC:2 * TC], rowa[0:1, TC:2 * TC])
            e.tensor_tensor(rowa[0:1, 2 * TC:3 * TC], rowa[0:1, 2 * TC:3 * TC],
                            rowa[0:1, TC:2 * TC], ALU.mult)
            return e.scalar_tensor_tensor(rowa[0:1, 3 * TC:4 * TC], rowa[0:1, 0:TC],
                                          -1.0, rowa[0:1, 2 * TC:3 * TC],
                                          ALU.mult, ALU.mult)
        rv2 = P.do("vector", rowv2, waits=((ss, rs1),), inc=vv)

        def xbcast(e):
            e.matmul(ps_misc[:, B1:B1 + TC], ones_row[:], rowa[0:1, 2 * TC:3 * TC],
                     start=True, stop=True)
            return e.matmul(ps_misc[:, B2:B2 + TC], ones_row[:], rowa[0:1, 3 * TC:4 * TC],
                            start=True, stop=True)
        xbm = P.do("tensor", xbcast, waits=((vv, rv2),), inc=mm)

        def xnorm(e, xo=xo):
            for a in range(DT):
                e.tensor_tensor(sq[:, a * TC:(a + 1) * TC],
                                xt[:, xo + a * TC:xo + (a + 1) * TC],
                                ps_misc[:, B1:B1 + TC], ALU.mult)
                last = e.tensor_tensor(xnb[:, a * TC:(a + 1) * TC],
                                       sq[:, a * TC:(a + 1) * TC],
                                       ps_misc[:, B2:B2 + TC], ALU.add)
            return last
        xn = P.do("vector", xnorm, waits=((mm, xbm),), inc=vv)

        hev = []; hqv = []; hmv = []
        for f in range(FT):
            fo = (f % 2) * 512
            w = [(vv, xn)]
            if c == 0 and f == 0:
                w.append((vv, tern_all))
            if f >= 2:
                w.append((ss, hev[f - 2]))
            def upmm(e, f=f, fo=fo):
                for a in range(DT):
                    last = e.matmul(ps_up[:, fo:fo + TC],
                                    wup_t[:, a * 4096 + f * 128:a * 4096 + f * 128 + 128],
                                    xnb[:, a * TC:(a + 1) * TC],
                                    start=(a == 0), stop=(a == DT - 1))
                return last
            um = P.do("tensor", upmm, waits=w, inc=mm)

            gw = [(mm, um)]
            ge = P.do("scalar", lambda e, fo=fo, f=f: e.activation(
                hgb[:, f * TC:(f + 1) * TC], ps_up[:, fo:fo + TC], AF.Gelu),
                waits=gw, inc=ss)
            hev.append(ge)

            hw = [(ss, ge)]
            if f >= 2:
                hw.append((mm, hmv[f - 2]))
            hq = P.do("vector", lambda e, f=f: e.tensor_tensor(
                hsq[:, (f % 2) * TC:(f % 2) * TC + TC], hgb[:, f * TC:(f + 1) * TC],
                hgb[:, f * TC:(f + 1) * TC], ALU.mult), waits=hw, inc=vv)
            hqv.append(hq)

            hm = P.do("tensor", lambda e, f=f: e.matmul(
                ps_misc[0:1, MU:MU + TC], ones_col[:],
                hsq[:, (f % 2) * TC:(f % 2) * TC + TC],
                start=(f == 0), stop=(f == FT - 1)), waits=((vv, hq),), inc=mm)
            hmv.append(hm)

        dnm = []
        for a in range(DT):
            dw = [(ss, hev[-1])]
            if a == 0 and ye_prev is not None:
                dw.append((vv, ye_prev))
            def dnfull(e, a=a):
                for f in range(FT):
                    last = e.matmul(ps_out[:, a * TC:(a + 1) * TC],
                                    wdn_t[:, f * D + a * 128:f * D + a * 128 + 128],
                                    hgb[:, f * TC:(f + 1) * TC],
                                    start=(f == 0), stop=(f == FT - 1))
                return last
            dn = P.do("tensor", dnfull, waits=dw, inc=mm)
            dnm.append(dn)

        hs = P.do("scalar", lambda e: e.activation(
            hrow[:], ps_misc[0:1, MU:MU + TC], AF.Sqrt, scale=1.0 / FF),
            waits=((mm, dnm[-1]),), inc=ss)

        def hrcp(e):
            e.tensor_scalar(hrow[:], hrow[:], 1e-8, None, ALU.add)
            return e.reciprocal(hrow[:], hrow[:])
        hv = P.do("vector", hrcp, waits=((ss, hs),), inc=vv)
        hbm = P.do("tensor", lambda e: e.matmul(
            ps_misc[:, B1:B1 + TC], ones_row[:], hrow[:],
            start=True, stop=True), waits=((vv, hv),), inc=mm)
        hbv = P.do("vector", lambda e: e.tensor_copy(
            sbc[:], ps_misc[:, B1:B1 + TC]), waits=((mm, hbm),), inc=vv)

        def yev(e, xo=xo):
            for a in range(DT):
                e.tensor_tensor(sq[:, a * TC:(a + 1) * TC],
                                ps_out[:, a * TC:(a + 1) * TC], sbc[:], ALU.mult)
                last = e.tensor_tensor(yout[:, a * TC:(a + 1) * TC],
                                       sq[:, a * TC:(a + 1) * TC],
                                       xt[:, xo + a * TC:xo + (a + 1) * TC], ALU.add)
            return last
        yw = [(vv, hbv)]
        if c >= 1:
            yw.append(store_ms[c - 1])
        ye = P.do("vector", yev, waits=yw, inc=vv)
        ye_prev = ye

        sret = STP.issue(P, "sync", lambda e, c=c: e.dma_start(
            out=rearr_dT(io["yT"], c * TC, (c + 1) * TC), in_=sb3(yout[:], DT)),
            waits=((vv, ye),))
        store_ms.append(sret)

    drain_pools(P, ctx)
    return store_ms


def plan_moe(P, ctx, io, uid="m"):
    FF = 8192
    FT = FF // 128
    NT = 512
    G = 4
    NG = FT // G

    ones_row = ctx.sbuf("ones_row", [1, 128], F32)
    ones_col = ctx.sbuf("ones_col", [128, 1], F32)
    cpool = ctx.sbuf("cpool", [128, 16], F32)

    xt = ctx.sbuf(uid + "xt", [128, DT * NT], F32)
    sq = ctx.sbuf(uid + "sq", [128, DT * NT], F32)
    xnb = ctx.sbuf(uid + "xnb", [128, DT * NT], BF16)
    w1s = ctx.sbuf(uid + "w1s", [128, 2 * DT * G * 128], BF16)
    w2s = ctx.sbuf(uid + "w2s", [128, 2 * FT * 128], BF16)
    hgb = ctx.sbuf(uid + "hgb", [128, FT * NT], BF16)
    rowa = ctx.sbuf(uid + "rowa", [1, 4 * NT], F32)
    wsr = ctx.sbuf(uid + "wsr", [1, 2 * NT], F32)
    wsbc = ctx.sbuf(uid + "wsbc", [128, NT], F32)
    yout = ctx.sbuf(uid + "yout", [128, DT * NT], F32)

    ps_h = ctx.psum(uid + "ps_h", [128, 2 * NT], F32)
    ps_y = ctx.psum(uid + "ps_y", [128, 2 * NT], F32)
    ps_misc = ctx.psum(uid + "ps_misc", [128, 2048], F32)

    mm = ctx.sem("mm"); vv = ctx.sem("vv"); ss = ctx.sem("ss")
    LD = ctx.pool("ld", 6)
    WQ = ctx.pool("wq", 4)
    STP = ctx.pool("st", 3)

    LD.issue(P, "sync", lambda e: e.dma_start(out=cpool[:], in_=io["cpool"][:]))
    LD.issue(P, "sync", lambda e: e.dma_start(out=ones_row[:], in_=io["ones_row"][:]))
    LD.issue(P, "sync", lambda e: e.dma_start(out=ones_col[:], in_=io["ones_col"][:]))
    xl = LD.issue(P, "sync", lambda e: e.dma_start(
        out=sb3(xt[:], DT), in_=rearr_dT(io["xT"], 0, NT)))
    el = LD.issue(P, "sync", lambda e: e.dma_start(
        out=wsr[:].rearrange("p (a t) -> p a t", a=2), in_=io["ewT"][:].unsqueeze(0)))

    w1R = io["w1"].rearrange("(a p) f -> p a f", p=128)
    w2R = io["w2S"]

    sv = P.do("scalar", lambda e: e.activation(sq[:], xt[:], AF.Square),
              waits=(xl,), inc=ss)
    MU, SQc, B1, B2 = 0, 512, 1024, 1536
    def xstat(e):
        for a in range(DT):
            e.matmul(ps_misc[0:1, MU:MU + NT], ones_col[:],
                     xt[:, a * NT:(a + 1) * NT], start=(a == 0), stop=(a == DT - 1))
        for a in range(DT):
            last = e.matmul(ps_misc[0:1, SQc:SQc + NT], ones_col[:],
                            sq[:, a * NT:(a + 1) * NT], start=(a == 0), stop=(a == DT - 1))
        return last
    xm = P.do("tensor", xstat, waits=((ss, sv), xl), inc=mm)

    def rowv1(e):
        e.tensor_scalar(rowa[0:1, 0:NT], ps_misc[0:1, MU:MU + NT], 1.0 / D, None, ALU.mult)
        e.tensor_scalar(rowa[0:1, NT:2 * NT], ps_misc[0:1, SQc:SQc + NT], 1.0 / D, None, ALU.mult)
        e.tensor_tensor(rowa[0:1, 2 * NT:3 * NT], rowa[0:1, 0:NT], rowa[0:1, 0:NT], ALU.mult)
        return e.tensor_tensor(rowa[0:1, NT:2 * NT], rowa[0:1, NT:2 * NT],
                               rowa[0:1, 2 * NT:3 * NT], ALU.subtract)
    rv1 = P.do("vector", rowv1, waits=((mm, xm),), inc=vv)

    def rows1(e):
        e.activation(rowa[0:1, 2 * NT:3 * NT], rowa[0:1, NT:2 * NT], AF.Ln,
                     bias=ctx.constcol(1e-5)[0:1, :])
        return e.activation(rowa[0:1, 2 * NT:3 * NT], rowa[0:1, 2 * NT:3 * NT],
                            AF.Exp, scale=-0.5)
    rs1 = P.do("scalar", rows1, waits=((vv, rv1),), inc=ss)
    rv2 = P.do("vector", lambda e: e.scalar_tensor_tensor(
        rowa[0:1, 3 * NT:4 * NT], rowa[0:1, 0:NT], -1.0,
        rowa[0:1, 2 * NT:3 * NT], ALU.mult, ALU.mult), waits=((ss, rs1),), inc=vv)

    def xbcast(e):
        e.matmul(ps_misc[:, B1:B1 + NT], ones_row[:], rowa[0:1, 2 * NT:3 * NT],
                 start=True, stop=True)
        return e.matmul(ps_misc[:, B2:B2 + NT], ones_row[:], rowa[0:1, 3 * NT:4 * NT],
                        start=True, stop=True)
    xbm = P.do("tensor", xbcast, waits=((vv, rv2),), inc=mm)

    def xnorm(e):
        for a in range(DT):
            e.tensor_tensor(sq[:, a * NT:(a + 1) * NT], xt[:, a * NT:(a + 1) * NT],
                            ps_misc[:, B1:B1 + NT], ALU.mult)
            last = e.tensor_tensor(xnb[:, a * NT:(a + 1) * NT],
                                   sq[:, a * NT:(a + 1) * NT],
                                   ps_misc[:, B2:B2 + NT], ALU.add)
        return last
    xn = P.do("vector", xnorm, waits=((mm, xbm),), inc=vv)

    GW = DT * G * 128
    gel = []
    wls = []
    for g in range(NG):
        par = g % 2
        wl = WQ.issue(P, "gpsimd", lambda e, g=g, par=par: e.dma_start(
            out=w1s[:, par * GW:(par + 1) * GW].rearrange("p (a f) -> p a f", a=DT),
            in_=w1R[:, :, g * G * 128:(g + 1) * G * 128]),
            waits=[(mm, wls[g - 2][1])] if g >= 2 else ())
        for i in range(G):
            f = g * G + i
            fo = (f % 2) * NT
            w = [(vv, xn), wl]
            if f >= 2:
                w.append((ss, gel[f - 2]))
            def upmm(e, par=par, i=i, fo=fo):
                for a in range(DT):
                    last = e.matmul(ps_h[:, fo:fo + NT],
                                    w1s[:, par * GW + a * G * 128 + i * 128:
                                        par * GW + a * G * 128 + (i + 1) * 128],
                                    xnb[:, a * NT:(a + 1) * NT],
                                    start=(a == 0), stop=(a == DT - 1))
                return last
            um = P.do("tensor", upmm, waits=w, inc=mm)
            ge = P.do("scalar", lambda e, fo=fo, f=f: e.activation(
                hgb[:, f * NT:(f + 1) * NT], ps_h[:, fo:fo + NT],
                AF.Gelu_apprx_tanh), waits=((mm, um),), inc=ss)
            gel.append(ge)
        wls.append((wl, mm.n))

    wv = P.do("vector", lambda e: e.tensor_tensor(
        rowa[0:1, 0:NT], wsr[0:1, 0:NT], wsr[0:1, NT:2 * NT], ALU.add),
        waits=(el,), inc=vv)
    wbm = P.do("tensor", lambda e: e.matmul(
        ps_misc[:, B1:B1 + NT], ones_row[:], rowa[0:1, 0:NT],
        start=True, stop=True), waits=((vv, wv), (ss, gel[-1])), inc=mm)
    wbv = P.do("vector", lambda e: e.tensor_copy(wsbc[:], ps_misc[:, B1:B1 + NT]),
               waits=((mm, wbm),), inc=vv)

    store_ms = []
    evs = []
    for a in range(DT):
        par = a % 2
        wl = WQ.issue(P, "gpsimd", lambda e, a=a, par=par: e.dma_start(
            out=w2s[:, par * FT * 128:(par + 1) * FT * 128].rearrange(
                "p (g d) -> p g d", g=FT),
            in_=w2R[:, a, :, :]),
            waits=[(mm, evs[a - 2][1])] if a >= 2 else ())
        def dnfull(e, a=a, par=par):
            for f in range(FT):
                last = e.matmul(ps_y[:, (a % 2) * NT:(a % 2) * NT + NT],
                                w2s[:, par * FT * 128 + f * 128:
                                    par * FT * 128 + (f + 1) * 128],
                                hgb[:, f * NT:(f + 1) * NT],
                                start=(f == 0), stop=(f == FT - 1))
            return last
        dw = [wl, (ss, gel[-1])]
        if a >= 2:
            dw.append((vv, evs[a - 2][0]))
        dn = P.do("tensor", dnfull, waits=dw, inc=mm)

        def yev(e, a=a):
            e.tensor_tensor(sq[:, a * NT:(a + 1) * NT],
                            ps_y[:, (a % 2) * NT:(a % 2) * NT + NT], wsbc[:], ALU.mult)
            return e.tensor_tensor(yout[:, a * NT:(a + 1) * NT],
                                   sq[:, a * NT:(a + 1) * NT],
                                   xt[:, a * NT:(a + 1) * NT], ALU.add)
        ye = P.do("vector", yev, waits=((mm, dn), (vv, wbv)), inc=vv)
        evs.append((ye, mm.n))
    sret = STP.issue(P, "sync", lambda e: e.dma_start(
        out=rearr_dT(io["yT"], 0, NT), in_=sb3(yout[:], DT)),
        waits=((vv, evs[-1][0]),))

    drain_pools(P, ctx)
    return [sret]


M1_TC = 256


def plan_mamba1(P, ctx, io, uid="b"):
    TC = M1_TC
    NCH = T // TC
    NST = 16

    ones_row = ctx.sbuf("ones_row", [1, 128], F32)
    ones_col = ctx.sbuf("ones_col", [128, 1], F32)
    ones_rb = ctx.sbuf("ones_rb", [1, 128], BF16)
    cpool = ctx.sbuf("cpool", [128, 16], F32)
    identB = ctx.sbuf("identB", [128, 128], BF16)

    ipw = ctx.sbuf(uid + "ipw", [128, DT * 2048], BF16)
    opw = ctx.sbuf(uid + "opw", [128, DT * 1024], BF16)
    xpB = ctx.sbuf(uid + "xpB", [128, DT * NST], BF16)
    xpC = ctx.sbuf(uid + "xpC", [128, DT * NST], BF16)
    xpD = ctx.sbuf(uid + "xpD", [128, DT], BF16)
    cwt = ctx.sbuf(uid + "cwt", [128, DT * 4], F32)
    cbt = ctx.sbuf(uid + "cbt", [128, DT], F32)
    dtwv = ctx.sbuf(uid + "dtwv", [128, DT], F32)
    dtbv = ctx.sbuf(uid + "dtbv", [128, DT], F32)
    amat = ctx.sbuf(uid + "amat", [128, DT * NST], F32)

    xt = ctx.sbuf(uid + "xt", [128, DT * TC], F32)
    sq = ctx.sbuf(uid + "sq", [128, DT * TC], F32)
    xnb = ctx.sbuf(uid + "xnb", [128, DT * TC], BF16)
    xin = ctx.sbuf(uid + "xin", [128, DT * (TC + 4)], F32)
    zs = ctx.sbuf(uid + "zs", [128, DT * TC], BF16)
    xcs = ctx.sbuf(uid + "xcs", [128, DT * TC], F32)
    xcb = ctx.sbuf(uid + "xcb", [128, DT * TC], BF16)
    dtt = ctx.sbuf(uid + "dtt", [128, DT * TC], F32)
    spB = ctx.sbuf(uid + "spB", [16, TC], BF16)
    spC = ctx.sbuf(uid + "spC", [16, TC], BF16)
    spfB = ctx.sbuf(uid + "spfB", [1, NST * TC], BF16)
    spfC = ctx.sbuf(uid + "spfC", [1, NST * TC], BF16)
    dtrow = ctx.sbuf(uid + "dtrow", [1, TC], F32)
    rowq = ctx.sbuf(uid + "rowq", [1, 4 * TC], F32)
    Bbc = ctx.sbuf(uid + "Bbc", [128, NST * TC], BF16)
    Cbc = ctx.sbuf(uid + "Cbc", [128, NST * TC], BF16)
    dab = ctx.sbuf(uid + "dab", [128, 2 * TC], F32)
    dxb = ctx.sbuf(uid + "dxb", [128, 2 * TC], F32)
    hbuf = ctx.sbuf(uid + "hbuf", [128, 2 * TC], F32)
    tmpb = ctx.sbuf(uid + "tmpb", [128, 2 * TC], BF16)
    state = ctx.sbuf(uid + "state", [128, 128], F32)
    ygb = ctx.sbuf(uid + "ygb", [128, DT * TC], BF16)
    yout = ctx.sbuf(uid + "yout", [128, 2 * TC], F32)

    ps_a = ctx.psum(uid + "ps_a", [128, 1024], F32)
    ps_b = ctx.psum(uid + "ps_b", [128, 1024], F32)
    ps_y = ctx.psum(uid + "ps_y", [128, 1024], F32)
    ps_o = ctx.psum(uid + "ps_o", [128, 1024], F32)

    mm = ctx.sem("mm"); vv = ctx.sem("vv"); ss = ctx.sem("ss")
    LD = ctx.pool("ld", 6)
    WG = ctx.pool("wg", 3)
    STP = ctx.pool("st", 3)
    SH = ctx.pool("sh", 2)

    LD.issue(P, "sync", lambda e: e.dma_start(out=cpool[:], in_=io["cpool"][:]))
    LD.issue(P, "sync", lambda e: e.dma_start(out=ones_row[:], in_=io["ones_row"][:]))
    LD.issue(P, "sync", lambda e: e.dma_start(out=ones_col[:], in_=io["ones_col"][:]))
    LD.issue(P, "sync", lambda e: e.dma_start(out=ones_rb[:], in_=io["ones_row_b"][:]))
    cl = LD.issue(P, "sync", lambda e: e.dma_start(out=identB[:], in_=io["identB"][:]))
    wl1 = WG.issue(P, "gpsimd", lambda e: e.dma_start(
        out=sb3(ipw[:], DT), in_=io["inprojT"].rearrange("(a p) f -> p a f", p=128)))
    wl2 = WG.issue(P, "gpsimd", lambda e: e.dma_start(
        out=sb3(opw[:], DT), in_=io["outprojT"].rearrange("(a p) f -> p a f", p=128)))
    wl3 = WG.issue(P, "gpsimd", lambda e: e.dma_start(
        out=sb3(xpB[:], DT), in_=io["xprojB"].rearrange("(a p) n -> p a n", p=128)))
    wl4 = WG.issue(P, "gpsimd", lambda e: e.dma_start(
        out=sb3(xpC[:], DT), in_=io["xprojC"].rearrange("(a p) n -> p a n", p=128)))
    wl5 = WG.issue(P, "gpsimd", lambda e: e.dma_start(
        out=xpD[:], in_=io["xprojD"][:]))
    wl6 = LD.issue(P, "sync", lambda e: e.dma_start(
        out=sb3(cwt[:], DT), in_=io["convW"].rearrange("(a p) k -> p a k", p=128)))
    wl7 = LD.issue(P, "sync", lambda e: e.dma_start(out=cbt[:], in_=io["convB"][:]))
    wl8 = LD.issue(P, "sync", lambda e: e.dma_start(out=dtwv[:], in_=io["dtW"][:]))
    wl9 = LD.issue(P, "sync", lambda e: e.dma_start(out=dtbv[:], in_=io["dtB"][:]))
    wl10 = LD.issue(P, "sync", lambda e: e.dma_start(
        out=sb3(amat[:], DT), in_=io["Alog"].rearrange("(a p) n -> p a n", p=128)))

    sA = P.do("scalar", lambda e: e.activation(amat[:], amat[:], AF.Exp),
              waits=(wl10,), inc=ss)
    vA = P.do("vector", lambda e: e.tensor_scalar(amat[:], amat[:], -1.0, None,
                                                  ALU.mult), waits=((ss, sA),), inc=vv)
    z0 = P.do("vector", lambda e: (e.memset(state[:], 0.0),
                                   e.memset(xin[:], 0.0))[-1], inc=vv)

    MU, SQc = 0, 512
    B1, B2 = 0, 512
    store_ms = []
    prev_stores = []
    ye_prev = None
    rv1_prev = None
    gate_prev = []
    scan_ms = {}
    ymm_ms = {}
    tmp_ms = {}
    for c in range(NCH):
        waits = []
        if c >= 1:
            waits.extend(prev_stores)
        xl = LD.issue(P, "sync", lambda e, c=c: e.dma_start(
            out=sb3(xt[:], DT), in_=rearr_dT(io["xT"], c * TC, (c + 1) * TC)),
            waits=waits)

        swaits = [xl]
        if ye_prev is not None:
            swaits.append((vv, ye_prev))
        sv = P.do("scalar", lambda e: e.activation(sq[:], xt[:], AF.Square),
                  waits=swaits, inc=ss)
        def xstat(e):
            for a in range(DT):
                e.matmul(ps_a[0:1, MU:MU + TC], ones_col[:],
                         xt[:, a * TC:(a + 1) * TC], start=(a == 0), stop=(a == DT - 1))
            for a in range(DT):
                last = e.matmul(ps_a[0:1, SQc:SQc + TC], ones_col[:],
                                sq[:, a * TC:(a + 1) * TC], start=(a == 0), stop=(a == DT - 1))
            return last
        xw = [(ss, sv)]
        if rv1_prev is not None:
            xw.append((vv, rv1_prev))
        xm = P.do("tensor", xstat, waits=xw, inc=mm)
        def rowv1(e):
            e.tensor_scalar(rowq[0:1, 0:TC], ps_a[0:1, MU:MU + TC], 1.0 / D, None, ALU.mult)
            e.tensor_scalar(rowq[0:1, TC:2 * TC], ps_a[0:1, SQc:SQc + TC], 1.0 / D, None, ALU.mult)
            e.tensor_tensor(rowq[0:1, 2 * TC:3 * TC], rowq[0:1, 0:TC], rowq[0:1, 0:TC], ALU.mult)
            return e.tensor_tensor(rowq[0:1, TC:2 * TC], rowq[0:1, TC:2 * TC],
                                   rowq[0:1, 2 * TC:3 * TC], ALU.subtract)
        rv1 = P.do("vector", rowv1, waits=((mm, xm),), inc=vv)
        rv1_prev = rv1
        def rows1(e):
            e.activation(rowq[0:1, 2 * TC:3 * TC], rowq[0:1, TC:2 * TC], AF.Ln,
                         bias=ctx.constcol(1e-5)[0:1, :])
            return e.activation(rowq[0:1, 2 * TC:3 * TC], rowq[0:1, 2 * TC:3 * TC],
                                AF.Exp, scale=-0.5)
        rs1 = P.do("scalar", rows1, waits=((vv, rv1),), inc=ss)
        rv2 = P.do("vector", lambda e: e.scalar_tensor_tensor(
            rowq[0:1, 3 * TC:4 * TC], rowq[0:1, 0:TC], -1.0,
            rowq[0:1, 2 * TC:3 * TC], ALU.mult, ALU.mult), waits=((ss, rs1),), inc=vv)
        def xbcast(e):
            e.matmul(ps_b[:, B1:B1 + TC], ones_row[:], rowq[0:1, 2 * TC:3 * TC],
                     start=True, stop=True)
            return e.matmul(ps_b[:, B2:B2 + TC], ones_row[:], rowq[0:1, 3 * TC:4 * TC],
                            start=True, stop=True)
        xbm = P.do("tensor", xbcast, waits=((vv, rv2),), inc=mm)
        def xnorm(e):
            for a in range(DT):
                e.tensor_tensor(sq[:, a * TC:(a + 1) * TC], xt[:, a * TC:(a + 1) * TC],
                                ps_b[:, B1:B1 + TC], ALU.mult)
                last = e.tensor_tensor(xnb[:, a * TC:(a + 1) * TC],
                                       sq[:, a * TC:(a + 1) * TC],
                                       ps_b[:, B2:B2 + TC], ALU.add)
            return last
        xn = P.do("vector", xnorm, waits=((mm, xbm),), inc=vv)

        ipm = []
        for f in range(2 * DT):
            fo = (f % 2) * 512
            w = [(vv, xn)]
            if c == 0 and f == 0:
                w += [wl1, (vv, z0)]
            if f >= 2:
                w.append((ss, ipm[f - 2]))
            elif c > 0 and gate_prev:
                w.append((vv, gate_prev[-1]))
            def ipmm(e, f=f, fo=fo):
                for a in range(DT):
                    last = e.matmul(ps_y[:, fo:fo + TC],
                                    ipw[:, a * 2048 + f * 128:a * 2048 + (f + 1) * 128],
                                    xnb[:, a * TC:(a + 1) * TC],
                                    start=(a == 0), stop=(a == DT - 1))
                return last
            um = P.do("tensor", ipmm, waits=w, inc=mm)
            if f < DT:
                ev = P.do("scalar", lambda e, f=f, fo=fo: e.activation(
                    xin[:, f * (TC + 4) + 4:f * (TC + 4) + 4 + TC],
                    ps_y[:, fo:fo + TC], AF.Copy), waits=((mm, um),), inc=ss)
            else:
                ev = P.do("scalar", lambda e, f=f, fo=fo: e.activation(
                    zs[:, (f - DT) * TC:(f - DT + 1) * TC],
                    ps_y[:, fo:fo + TC], AF.Silu), waits=((mm, um),), inc=ss)
            ipm.append(ev)

        def conv_v(e):
            for a in range(DT):
                o = a * (TC + 4)
                e.tensor_scalar(xcs[:, a * TC:(a + 1) * TC],
                                xin[:, o + 1:o + 1 + TC], cwt[:, a * 4 + 0:a * 4 + 1],
                                None, ALU.mult)
                for k in (1, 2):
                    e.scalar_tensor_tensor(xcs[:, a * TC:(a + 1) * TC],
                                           xin[:, o + 1 + k:o + 1 + k + TC],
                                           cwt[:, a * 4 + k:a * 4 + k + 1],
                                           xcs[:, a * TC:(a + 1) * TC],
                                           ALU.mult, ALU.add)
                last = e.scalar_tensor_tensor(xcs[:, a * TC:(a + 1) * TC],
                                              xin[:, o + 4:o + 4 + TC],
                                              cwt[:, a * 4 + 3:a * 4 + 4],
                                              xcs[:, a * TC:(a + 1) * TC],
                                              ALU.mult, ALU.add)
            return last
        cv = P.do("vector", conv_v, waits=((ss, ipm[DT - 1]), wl6), inc=vv)
        def halo_v(e):
            for a in range(DT):
                o = a * (TC + 4)
                last = e.tensor_copy(xin[:, o + 1:o + 4], xin[:, o + 1 + TC:o + 4 + TC])
            return last
        hv = P.do("vector", halo_v, inc=vv)
        def silu_s(e):
            for a in range(DT):
                e.activation(xcs[:, a * TC:(a + 1) * TC], xcs[:, a * TC:(a + 1) * TC],
                             AF.Silu, bias=cbt[:, a:a + 1])
                last = e.activation(xcb[:, a * TC:(a + 1) * TC],
                                    xcs[:, a * TC:(a + 1) * TC], AF.Copy)
            return last
        cs = P.do("scalar", silu_s, waits=((vv, cv), wl7), inc=ss)

        def spmm(e):
            for a in range(DT):
                e.matmul(ps_a[0:16, 0:TC], xpB[:, a * NST:(a + 1) * NST],
                         xcb[:, a * TC:(a + 1) * TC], start=(a == 0), stop=(a == DT - 1))
            for a in range(DT):
                e.matmul(ps_a[0:16, 512:512 + TC], xpC[:, a * NST:(a + 1) * NST],
                         xcb[:, a * TC:(a + 1) * TC], start=(a == 0), stop=(a == DT - 1))
            for a in range(DT):
                last = e.matmul(ps_b[0:1, B1:B1 + TC], xpD[:, a:a + 1],
                                xcb[:, a * TC:(a + 1) * TC], start=(a == 0), stop=(a == DT - 1))
            return last
        spm = P.do("tensor", spmm,
                   waits=((ss, cs), wl3, wl4, wl5, (vv, rv1), (vv, xn)), inc=mm)

        spe = P.do("vector", lambda e: (
            e.tensor_copy(spB[:], ps_a[0:16, 0:TC]),
            e.tensor_copy(spC[:], ps_a[0:16, 512:512 + TC]),
            e.tensor_copy(dtrow[:], ps_b[0:1, B1:B1 + TC]))[-1],
            waits=((mm, spm),), inc=vv)

        fb0 = SH.issue(P, "sync", lambda e: e.dma_start(out=io["scrB"][:], in_=spB[:]),
                       waits=((vv, spe),))
        fc0 = SH.issue(P, "sync", lambda e: e.dma_start(out=io["scrC"][:], in_=spC[:]),
                       waits=((vv, spe),))
        fb = SH.issue(P, "sync", lambda e: e.dma_start(
            out=spfB[:], in_=io["scrB"].rearrange("n t -> (n t)").unsqueeze(0)),
            waits=(fb0,))
        fc = SH.issue(P, "sync", lambda e: e.dma_start(
            out=spfC[:], in_=io["scrC"].rearrange("n t -> (n t)").unsqueeze(0)),
            waits=(fc0,))

        dbm = P.do("tensor", lambda e: e.matmul(
            ps_b[:, B2:B2 + TC], ones_row[:], dtrow[:], start=True, stop=True),
            waits=((vv, spe), (vv, xn)), inc=mm)
        def dtv(e):
            for a in range(DT):
                last = e.tensor_scalar(dtt[:, a * TC:(a + 1) * TC], ps_b[:, B2:B2 + TC],
                                       dtwv[:, a:a + 1], dtbv[:, a:a + 1],
                                       ALU.mult, ALU.add)
            return last
        dv = P.do("vector", dtv, waits=((mm, dbm), wl8, wl9), inc=vv)
        def softp(e):
            for a in range(DT):
                sl = slice(a * TC, (a + 1) * TC)
                o = a * (TC + 4)
                e.activation(sq[:, sl], dtt[:, sl], AF.Abs)
                e.activation(sq[:, sl], sq[:, sl], AF.Exp, scale=-1.0)
                e.activation(sq[:, sl], sq[:, sl], AF.Ln,
                             bias=ctx.constcol(1.0))
                last = e.activation(xin[:, o + 4:o + 4 + TC], dtt[:, sl], AF.Relu)
            return last
        sp2 = P.do("scalar", softp, waits=((vv, dv), (vv, hv)), inc=ss)
        def dtfin(e):
            for a in range(DT):
                sl = slice(a * TC, (a + 1) * TC)
                o = a * (TC + 4)
                e.tensor_tensor(dtt[:, sl], sq[:, sl], xin[:, o + 4:o + 4 + TC], ALU.add)
                last = e.tensor_tensor(xcs[:, sl], dtt[:, sl], xcs[:, sl], ALU.mult)
            return last
        gv = P.do("vector", dtfin, waits=((ss, sp2),), inc=vv)

        bc_ms = []
        for n in range(NST):
            po = (n % 2) * 512
            w = [fb, fc, cl, (vv, dv)]
            if n >= 2:
                w.append((vv, bc_ms[n - 2]))
            def bcmm(e, n=n, po=po):
                e.matmul(ps_a[:, po:po + TC], ones_rb[:],
                         spfB[0:1, n * TC:(n + 1) * TC], start=True, stop=True)
                return e.matmul(ps_b[:, po:po + TC], ones_rb[:],
                                spfC[0:1, n * TC:(n + 1) * TC], start=True, stop=True)
            bm = P.do("tensor", bcmm, waits=w, inc=mm)
            bv = P.do("vector", lambda e, n=n, po=po: (
                e.tensor_copy(Bbc[:, n * TC:(n + 1) * TC], ps_a[:, po:po + TC]),
                e.tensor_copy(Cbc[:, n * TC:(n + 1) * TC], ps_b[:, po:po + TC]))[-1],
                waits=((mm, bm),), inc=vv)
            bc_ms.append(bv)

        new_gate = []
        for a in range(DT):
            for n in range(NST):
                idx = a * NST + n
                p = idx % 2
                po = p * TC
                sl = slice(a * TC, (a + 1) * TC)
                ew = [(vv, gv), (vv, bc_ms[-1])]
                if (c, idx) in scan_ms or True:
                    prev = scan_ms.get((c, idx - 2)) or (scan_ms.get((c - 1, 126 + idx)) if idx < 2 else None)
                    if prev:
                        ew.append((vv, prev))
                se = P.do("scalar", lambda e, sl=sl, po=po, a=a, n=n: e.activation(
                    dab[:, po:po + TC], dtt[:, sl], AF.Exp,
                    scale=amat[:, a * NST + n:a * NST + n + 1]), waits=ew, inc=ss)
                def scanv(e, sl=sl, po=po, a=a, n=n, idx=idx):
                    e.tensor_tensor(dxb[:, po:po + TC], xcs[:, sl],
                                    Bbc[:, n * TC:(n + 1) * TC], ALU.mult)
                    e.tensor_tensor_scan(hbuf[:, po:po + TC], dab[:, po:po + TC],
                                         dxb[:, po:po + TC],
                                         state[:, idx:idx + 1], ALU.mult, ALU.add)
                    e.tensor_copy(state[:, idx:idx + 1], hbuf[:, po + TC - 1:po + TC])
                    return e.tensor_tensor(tmpb[:, po:po + TC], hbuf[:, po:po + TC],
                                           Cbc[:, n * TC:(n + 1) * TC], ALU.mult)
                tw = [(ss, se)]
                pm = ymm_ms.get((c, idx - 2)) or (ymm_ms.get((c - 1, 126 + idx)) if idx < 2 else None)
                if pm:
                    tw.append((mm, pm))
                sc = P.do("vector", scanv, waits=tw, inc=vv)
                scan_ms[(c, idx)] = sc
                yw = [(vv, sc)]
                if n == 0:
                    if a >= 2:
                        yw.append((vv, new_gate[a - 2]))
                    elif gate_prev:
                        yw.append((vv, gate_prev[a - 2]))
                    yw.append((ss, ipm[-1]))
                ym = P.do("tensor", lambda e, a=a, n=n, po=po: e.matmul(
                    ps_y[:, (a % 2) * 512:(a % 2) * 512 + TC], identB[:],
                    tmpb[:, po:po + TC], start=(n == 0), stop=(n == NST - 1)),
                    waits=yw, inc=mm)
                ymm_ms[(c, idx)] = ym
            ga = P.do("vector", lambda e, a=a: e.tensor_tensor(
                ygb[:, a * TC:(a + 1) * TC], ps_y[:, (a % 2) * 512:(a % 2) * 512 + TC],
                zs[:, a * TC:(a + 1) * TC], ALU.mult),
                waits=((mm, ymm_ms[(c, a * NST + NST - 1)]),), inc=vv)
            new_gate.append(ga)
        gate_prev = new_gate

        op_ms = []
        ostores = []
        for dto in range(DT):
            w = [(vv, new_gate[-1])]
            if c == 0 and dto == 0:
                w.append(wl2)
            if dto >= 2:
                w.append((vv, op_ms[dto - 2]))
            def opmm(e, dto=dto):
                for a in range(DT):
                    last = e.matmul(ps_o[:, (dto % 2) * 512:(dto % 2) * 512 + TC],
                                    opw[:, a * 1024 + dto * 128:a * 1024 + (dto + 1) * 128],
                                    ygb[:, a * TC:(a + 1) * TC],
                                    start=(a == 0), stop=(a == DT - 1))
                return last
            om = P.do("tensor", opmm, waits=w, inc=mm)
            ow = [(mm, om)]
            if dto >= 2:
                ow.append(ostores[dto - 2])
            oe = P.do("vector", lambda e, dto=dto: e.tensor_tensor(
                yout[:, (dto % 2) * TC:(dto % 2) * TC + TC],
                ps_o[:, (dto % 2) * 512:(dto % 2) * 512 + TC],
                xt[:, dto * TC:(dto + 1) * TC], ALU.add),
                waits=ow, inc=vv)
            op_ms.append(oe)
            osr = STP.issue(P, "sync", lambda e, c=c, dto=dto: e.dma_start(
                out=io["yT"][dto * 128:(dto + 1) * 128, c * TC:(c + 1) * TC],
                in_=yout[:, (dto % 2) * TC:(dto % 2) * TC + TC]),
                waits=((vv, oe),))
            ostores.append(osr)
        ye = op_ms[-1]
        ye_prev = ye
        store_ms.append(ostores[-1])
        prev_stores = [ostores[-3], ostores[-2], ostores[-1]]

    drain_pools(P, ctx)
    return store_ms


def plan_attn_a(P, ctx, io, uid="a"):
    TC = 512
    NCH = T // TC
    NH = 8
    SCALE = 1.0 / 8.0
    EG = 4

    ones_row = ctx.sbuf("ones_row", [1, 128], F32)
    ones_col = ctx.sbuf("ones_col", [128, 1], F32)
    cpool = ctx.sbuf("cpool", [128, 16], F32)
    halfones = ctx.sbuf(uid + "halfones", [128, 2], F32)

    mq = ctx.sbuf(uid + "mq", [128, DT * 512], BF16)
    mk = ctx.sbuf(uid + "mk", [128, DT * 512], BF16)
    mv = ctx.sbuf(uid + "mv", [128, DT * 1024], BF16)
    xt = ctx.sbuf(uid + "xt", [128, DT * TC], F32)
    sq = ctx.sbuf(uid + "sq", [128, DT * TC], F32)
    xnb = ctx.sbuf(uid + "xnb", [128, DT * T], BF16)
    rowq = ctx.sbuf(uid + "rowq", [1, 4 * TC], F32)
    QB = ctx.sbuf(uid + "QB", [128, 4 * 512], BF16)
    KB = ctx.sbuf(uid + "KB", [128, 4 * 512], BF16)
    QBo = ctx.sbuf(uid + "QBo", [64, 4 * 512], BF16)
    KBo = ctx.sbuf(uid + "KBo", [64, 4 * 512], BF16)
    VB = ctx.sbuf(uid + "VB", [128, 4 * 1024], BF16)
    Ebuf = ctx.sbuf(uid + "Ebuf", [128, EG * T], BF16)
    zpart = ctx.sbuf(uid + "zpart", [128, 16], F32)
    vn = ctx.sbuf(uid + "vn", [128, EG * 128], BF16)
    bscr = ctx.sbuf(uid + "bscr", [1, 16], F32)
    bcol = ctx.sbuf(uid + "bcol", [128, 1], F32)
    acc = xt

    ps_s = ctx.psum(uid + "ps_s", [128, 2 * 512], F32)
    ps_o = ctx.psum(uid + "ps_o", [128, 2 * 512], F32)
    ps_m = ctx.psum(uid + "ps_m", [128, 1024], F32)
    ps_p = ctx.psum(uid + "ps_p", [128, 2 * 512], F32)

    mm = ctx.sem("mm"); vv = ctx.sem("vv"); ss = ctx.sem("ss")
    LD = ctx.pool("ld", 6)
    WG = ctx.pool("wg", 3)
    STP = ctx.pool("st", 3)

    LD.issue(P, "sync", lambda e: e.dma_start(out=cpool[:], in_=io["cpool"][:]))
    LD.issue(P, "sync", lambda e: e.dma_start(out=ones_row[:], in_=io["ones_row"][:]))
    LD.issue(P, "sync", lambda e: e.dma_start(out=ones_col[:], in_=io["ones_col"][:]))
    hl = LD.issue(P, "sync", lambda e: e.dma_start(out=halfones[:], in_=io["halfones"][:]))
    wq = WG.issue(P, "gpsimd", lambda e: e.dma_start(
        out=sb3(mq[:], DT), in_=io["Mq"].rearrange("(a p) f -> p a f", p=128)))
    wk = WG.issue(P, "gpsimd", lambda e: e.dma_start(
        out=sb3(mk[:], DT), in_=io["Mk"].rearrange("(a p) f -> p a f", p=128)))
    wv = WG.issue(P, "gpsimd", lambda e: e.dma_start(
        out=sb3(mv[:], DT), in_=io["Mv"].rearrange("(a p) f -> p a f", p=128)))

    rv1_prev = None
    xn_ms = []
    for c in range(NCH):
        xl = LD.issue(P, "sync", lambda e, c=c: e.dma_start(
            out=sb3(xt[:], DT), in_=rearr_dT(io["xT"], c * TC, (c + 1) * TC)),
            waits=[(vv, xn_ms[-1])] if xn_ms else ())
        sv = P.do("scalar", lambda e: e.activation(sq[:], xt[:], AF.Square),
                  waits=(xl,) + (((vv, xn_ms[-1]),) if xn_ms else ()), inc=ss)
        def xstat(e):
            for a in range(DT):
                e.matmul(ps_m[0:1, 0:TC], ones_col[:],
                         xt[:, a * TC:(a + 1) * TC], start=(a == 0), stop=(a == DT - 1))
            for a in range(DT):
                last = e.matmul(ps_m[0:1, 512:512 + TC], ones_col[:],
                                sq[:, a * TC:(a + 1) * TC], start=(a == 0), stop=(a == DT - 1))
            return last
        xw = [(ss, sv)]
        if rv1_prev is not None:
            xw.append((vv, rv1_prev))
        xm = P.do("tensor", xstat, waits=xw, inc=mm)
        def rowv1(e):
            e.tensor_scalar(rowq[0:1, 0:TC], ps_m[0:1, 0:TC], 1.0 / D, None, ALU.mult)
            e.tensor_scalar(rowq[0:1, TC:2 * TC], ps_m[0:1, 512:512 + TC], 1.0 / D, None, ALU.mult)
            e.tensor_tensor(rowq[0:1, 2 * TC:3 * TC], rowq[0:1, 0:TC], rowq[0:1, 0:TC], ALU.mult)
            return e.tensor_tensor(rowq[0:1, TC:2 * TC], rowq[0:1, TC:2 * TC],
                                   rowq[0:1, 2 * TC:3 * TC], ALU.subtract)
        rv1 = P.do("vector", rowv1, waits=((mm, xm),), inc=vv)
        rv1_prev = rv1
        def rows1(e):
            e.activation(rowq[0:1, 2 * TC:3 * TC], rowq[0:1, TC:2 * TC], AF.Ln,
                         bias=ctx.constcol(1e-5)[0:1, :])
            return e.activation(rowq[0:1, 2 * TC:3 * TC], rowq[0:1, 2 * TC:3 * TC],
                                AF.Exp, scale=-0.5)
        rs1 = P.do("scalar", rows1, waits=((vv, rv1),), inc=ss)
        rv2 = P.do("vector", lambda e: e.scalar_tensor_tensor(
            rowq[0:1, 3 * TC:4 * TC], rowq[0:1, 0:TC], -1.0,
            rowq[0:1, 2 * TC:3 * TC], ALU.mult, ALU.mult), waits=((ss, rs1),), inc=vv)
        def xbcast(e):
            e.matmul(ps_m[:, 0:TC], ones_row[:], rowq[0:1, 2 * TC:3 * TC],
                     start=True, stop=True)
            return e.matmul(ps_m[:, 512:512 + TC], ones_row[:], rowq[0:1, 3 * TC:4 * TC],
                            start=True, stop=True)
        xbm = P.do("tensor", xbcast, waits=((vv, rv2),), inc=mm)
        def xnorm(e, c=c):
            for a in range(DT):
                e.tensor_tensor(sq[:, a * TC:(a + 1) * TC], xt[:, a * TC:(a + 1) * TC],
                                ps_m[:, 0:TC], ALU.mult)
                last = e.tensor_tensor(xnb[:, a * T + c * TC:a * T + (c + 1) * TC],
                                       sq[:, a * TC:(a + 1) * TC],
                                       ps_m[:, 512:512 + TC], ALU.add)
            return last
        xn = P.do("vector", xnorm, waits=((mm, xbm),), inc=vv)
        xn_ms.append(xn)
    ln_done = xn_ms[-1]

    prev_head_store = []
    prev_pv_mm = None
    prev_vev_last = None
    prev_ra_last = None
    for h in range(NH):
        hb = h * TC

        pev = []
        for t in range(4):
            po = (t % 2) * 512
            w = [(vv, ln_done), wq]
            if t >= 2:
                w.append((ss, pev[t - 2]))
            else:
                if prev_pv_mm is not None:
                    w.append((mm, prev_pv_mm))
                if prev_vev_last is not None:
                    w.append((ss, prev_vev_last))
                if prev_ra_last is not None:
                    w.append((vv, prev_ra_last))
            def qmm(e, t=t, po=po, hb=hb):
                for a in range(DT):
                    last = e.matmul(ps_p[:, po:po + 512],
                                    mq[:, a * 512 + t * 128:a * 512 + (t + 1) * 128],
                                    xnb[:, a * T + hb:a * T + hb + TC],
                                    start=(a == 0), stop=(a == DT - 1))
                return last
            qm = P.do("tensor", qmm, waits=w, inc=mm)
            qe = P.do("scalar", lambda e, t=t, po=po: e.activation(
                QB[:, t * 512:(t + 1) * 512], ps_p[:, po:po + 512], AF.Copy,
                scale=SCALE), waits=((mm, qm),), inc=ss)
            pev.append(qe)
        for t in range(4):
            po = (t % 2) * 512
            w = [wk, (ss, pev[2 + t])]
            def kmm(e, t=t, po=po, hb=hb):
                for a in range(DT):
                    last = e.matmul(ps_p[:, po:po + 512],
                                    mk[:, a * 512 + t * 128:a * 512 + (t + 1) * 128],
                                    xnb[:, a * T + hb:a * T + hb + TC],
                                    start=(a == 0), stop=(a == DT - 1))
                return last
            km = P.do("tensor", kmm, waits=w, inc=mm)
            ke = P.do("scalar", lambda e, t=t, po=po: e.activation(
                KB[:, t * 512:(t + 1) * 512], ps_p[:, po:po + 512], AF.Copy),
                waits=((mm, km),), inc=ss)
            pev.append(ke)
        for idx in range(8):
            t, nchunk = idx // 2, idx % 2
            po = (idx % 2) * 512
            w = [wv, (ss, pev[6 + idx])]
            def vmm(e, t=t, nchunk=nchunk, po=po, hb=hb):
                for a in range(DT):
                    last = e.matmul(
                        ps_p[:, po:po + 512],
                        xnb[:, a * T + hb + t * 128:a * T + hb + (t + 1) * 128],
                        mv[:, a * 1024 + nchunk * 512:a * 1024 + (nchunk + 1) * 512],
                        start=(a == 0), stop=(a == DT - 1))
                return last
            vm = P.do("tensor", vmm, waits=w, inc=mm)
            ve = P.do("scalar", lambda e, t=t, nchunk=nchunk, po=po: e.activation(
                VB[:, t * 1024 + nchunk * 512:t * 1024 + (nchunk + 1) * 512],
                ps_p[:, po:po + 512], AF.Copy), waits=((mm, vm),), inc=ss)
            pev.append(ve)
        prev_vev_last = pev[-1]
        qo = STP.issue(P, "sync", lambda e: e.dma_start(
            out=QBo[:], in_=QB[64:128, :]), waits=((ss, pev[3]),))
        ko = STP.issue(P, "sync", lambda e: e.dma_start(
            out=KBo[:], in_=KB[64:128, :]), waits=((ss, pev[7]),))

        mxv = []
        for wi, buf in ((0, QB), (1, KB)):
            sqw = [(ss, pev[-1])]
            if wi == 1:
                sqw.append((vv, mxv[3]))
            sq2 = P.do("scalar", lambda e, buf=buf: e.activation(
                sq[:, 0:2048], buf[:], AF.Square), waits=sqw, inc=ss)
            for t in range(4):
                nm = P.do("tensor", lambda e, t=t: e.matmul(
                    ps_m[0:2, (t % 2) * 512:(t % 2) * 512 + 512], halfones[:],
                    sq[:, t * 512:(t + 1) * 512], start=True, stop=True),
                    waits=((ss, sq2), hl) + (((vv, mxv[-2]),) if len(mxv) >= 2 else ()),
                    inc=mm)
                rv = P.do("vector", lambda e, t=t, wi=wi: e.tensor_reduce(
                    zpart[0:2, wi * 4 + t:wi * 4 + t + 1],
                    ps_m[0:2, (t % 2) * 512:(t % 2) * 512 + 512], AX, ALU.max),
                    waits=((mm, nm),), inc=vv)
                mxv.append(rv)
        cm = P.do("vector", lambda e: (
            e.tensor_reduce(zpart[0:2, 8:9], zpart[0:2, 0:4], AX, ALU.max),
            e.tensor_reduce(zpart[0:2, 9:10], zpart[0:2, 4:8], AX, ALU.max))[-1],
            waits=((vv, mxv[-1]),), inc=vv)
        db1 = STP.issue(P, "sync", lambda e: e.dma_start(
            out=io["scrA"][:], in_=zpart[0:2, 8:10]), waits=((vv, cm),))
        db2 = STP.issue(P, "sync", lambda e: e.dma_start(
            out=bscr[0:1, 0:4], in_=io["scrA"].rearrange("p o -> (p o)").unsqueeze(0)),
            waits=(db1,))
        bv1 = P.do("vector", lambda e: e.tensor_tensor(
            bscr[0:1, 4:6], bscr[0:1, 0:2], bscr[0:1, 2:4], ALU.max),
            waits=(db2,), inc=vv)
        bs1 = P.do("scalar", lambda e: e.activation(
            bscr[0:1, 6:8], bscr[0:1, 4:6], AF.Sqrt), waits=((vv, bv1),), inc=ss)
        bv2 = P.do("vector", lambda e: e.scalar_tensor_tensor(
            bscr[0:1, 8:9], bscr[0:1, 6:7], -1.0, bscr[0:1, 7:8],
            ALU.mult, ALU.mult), waits=((ss, bs1),), inc=vv)
        bb1 = P.do("tensor", lambda e: e.matmul(
            ps_m[:, 0:1], ones_row[:], bscr[0:1, 8:9], start=True, stop=True),
            waits=((vv, bv2),), inc=mm)
        bb2 = P.do("vector", lambda e: e.tensor_copy(bcol[:], ps_m[:, 0:1]),
                   waits=((mm, bb1),), inc=vv)
        if "dbga_bcol" in io:
            STP.issue(P, "sync", lambda e, h=h: e.dma_start(
                out=io["dbga_bcol"][h], in_=bcol[:]), waits=((vv, bb2),))
            STP.issue(P, "sync", lambda e, h=h: e.dma_start(
                out=io["dbga_zp"][h], in_=zpart[0:2, 0:16]), waits=((vv, bb2),))
            STP.issue(P, "sync", lambda e, h=h: e.dma_start(
                out=io["dbga_qb"][h], in_=QB[:]), waits=((ss, pev[3]),))
            STP.issue(P, "sync", lambda e, h=h: e.dma_start(
                out=io["dbga_kb"][h], in_=KB[:]), waits=((ss, pev[7]),))

        exp_ms = {}
        pa_ms = []
        zr_prev = None
        last_pm_group = None
        for g in range(8):
            vn_ms = []
            for jj in range(EG):
                jt = g * EG + jj
                r_j, ut_j = jt // 4, jt % 4
                for ic in range(8):
                    po = (ic % 2) * 512
                    w = []
                    if (jt, ic - 2) in exp_ms:
                        w.append((ss, exp_ms[(jt, ic - 2)]))
                    elif (jt - 1, 6 + ic) in exp_ms:
                        w.append((ss, exp_ms[(jt - 1, 6 + ic)]))
                    if jj == 0 and ic == 0 and last_pm_group is not None:
                        w.append((mm, last_pm_group))
                    def smm(e, r_j=r_j, ut_j=ut_j, ic=ic, po=po):
                        ksrc = KB if r_j % 2 == 0 else KBo
                        kb_sl = ksrc[0:64,
                                     (r_j // 2) * 512 + ut_j * 128:
                                     (r_j // 2) * 512 + (ut_j + 1) * 128]
                        qsrc = QB if ic % 2 == 0 else QBo
                        qb_sl = qsrc[0:64, (ic // 2) * 512:(ic // 2) * 512 + 512]
                        return e.matmul(ps_s[:, po:po + 512], kb_sl, qb_sl,
                                        start=True, stop=True)
                    if jt == 0 and ic == 0:
                        w += [qo, ko]
                    sm = P.do("tensor", smm, waits=w, inc=mm)
                    ew = [(mm, sm), (vv, bb2)]
                    if ic == 0 and zr_prev is not None:
                        ew.append((vv, zr_prev))
                    ex = P.do("scalar", lambda e, jj=jj, ic=ic, po=po: e.activation(
                        Ebuf[:, jj * T + ic * 512:jj * T + (ic + 1) * 512],
                        ps_s[:, po:po + 512], AF.Exp, bias=bcol[:],
                        accum_out=zpart[:, ic:ic + 1]),
                        waits=ew, inc=ss)
                    exp_ms[(jt, ic)] = ex
                zr1 = P.do("vector", lambda e, jj=jj: e.tensor_reduce(
                    zpart[:, 8 + jj:9 + jj], zpart[:, 0:8], AX, ALU.add),
                    waits=((ss, exp_ms[(jt, 7)]),), inc=vv)
                zs1 = P.do("scalar", lambda e, jj=jj: (
                    e.activation(zpart[:, 8 + jj:9 + jj], zpart[:, 8 + jj:9 + jj],
                                 AF.Ln),
                    e.activation(zpart[:, 8 + jj:9 + jj], zpart[:, 8 + jj:9 + jj],
                                 AF.Exp, scale=-1.0))[-1],
                    waits=((vv, zr1),), inc=ss)
                def zvn(e, jj=jj, r_j=r_j, ut_j=ut_j):
                    vo = ut_j * 1024 + r_j * 128
                    return e.tensor_scalar(vn[:, jj * 128:(jj + 1) * 128],
                                           VB[:, vo:vo + 128],
                                           zpart[:, 8 + jj:9 + jj], None, ALU.mult)
                zr = P.do("vector", zvn, waits=((ss, zs1),), inc=vv)
                zr_prev = zr
                vn_ms.append(zr)
            for ic in range(8):
                po = (ic % 2) * 512
                def pvmm(e, ic=ic, po=po):
                    for jj in range(EG):
                        last = e.matmul(ps_o[:, po:po + 512],
                                        vn[:, jj * 128:(jj + 1) * 128],
                                        Ebuf[:, jj * T + ic * 512:jj * T + (ic + 1) * 512],
                                        start=(jj == 0), stop=(jj == EG - 1))
                    return last
                w = [(vv, vn_ms[-1])]
                if len(pa_ms) >= 2:
                    w.append((vv, pa_ms[-2]))
                pm = P.do("tensor", pvmm, waits=w, inc=mm)
                if ic == 7:
                    last_pm_group = mm.n
                paw = [(mm, pm)]
                if g == 0 and prev_head_store:
                    paw.append(prev_head_store[ic])
                pa = P.do("vector", lambda e, ic=ic, po=po, g=g:
                          (e.tensor_copy(acc[:, ic * 512:(ic + 1) * 512],
                                         ps_o[:, po:po + 512]) if g == 0 else
                           e.tensor_tensor(acc[:, ic * 512:(ic + 1) * 512],
                                           acc[:, ic * 512:(ic + 1) * 512],
                                           ps_o[:, po:po + 512], ALU.add)),
                          waits=paw, inc=vv)
                pa_ms.append(pa)
        prev_pv_mm = mm.n

        ra_last = None
        phs = list(prev_head_store)
        for ic in range(8):
            rlw = []
            if phs:
                rlw.append(phs[ic])
            rl = LD.issue(P, "sync", lambda e, ic=ic, hb=hb: e.dma_start(
                out=sq[:, ic * 512:ic * 512 + 512],
                in_=io["xT"][ic * 128:(ic + 1) * 128, hb:hb + TC]), waits=rlw)
            ra = P.do("vector", lambda e, ic=ic: e.tensor_tensor(
                acc[:, ic * 512:(ic + 1) * 512], acc[:, ic * 512:(ic + 1) * 512],
                sq[:, ic * 512:ic * 512 + 512], ALU.add),
                waits=(rl, (vv, pa_ms[-1])), inc=vv)
            sr = STP.issue(P, "sync", lambda e, ic=ic, hb=hb: e.dma_start(
                out=io["yT"][ic * 128:(ic + 1) * 128, hb:hb + TC],
                in_=acc[:, ic * 512:(ic + 1) * 512]), waits=((vv, ra),))
            ra_last = ra
            if len(prev_head_store) <= ic:
                prev_head_store.append(sr)
            else:
                prev_head_store[ic] = sr
        prev_ra_last = ra_last

    drain_pools(P, ctx)
    return prev_head_store


def plan_mla(P, ctx, io, uid="l"):
    TC = 512
    NCH = T // TC
    NHD = 16
    SCL = 1.0 / math.sqrt(96.0)

    ones_row = ctx.sbuf("ones_row", [1, 128], F32)
    ones_col = ctx.sbuf("ones_col", [128, 1], F32)
    cpool = ctx.sbuf("cpool", [128, 16], F32)

    wkv = ctx.sbuf(uid + "wkv", [128, DT * 128], BF16)
    wkup = ctx.sbuf(uid + "wkup", [128, 1024], BF16)
    wvup = ctx.sbuf(uid + "wvup", [128, 1024], BF16)
    wqh = ctx.sbuf(uid + "wqh", [128, 2 * 512], BF16)
    wqr = ctx.sbuf(uid + "wqr", [128, DT * 32], BF16)
    wkr = ctx.sbuf(uid + "wkr", [128, 32], BF16)
    wod = ctx.sbuf(uid + "wod", [64, 2 * 2048], BF16)
    cosT = ctx.sbuf(uid + "cosT", [16, T], F32)
    sinT = ctx.sbuf(uid + "sinT", [16, T], F32)
    strip = ctx.sbuf(uid + "strip", [128, 1024], BF16)

    xt = ctx.sbuf(uid + "xt", [128, DT * TC], F32)
    sq = ctx.sbuf(uid + "sq", [128, DT * TC], F32)
    xnb = ctx.sbuf(uid + "xnb", [128, DT * T], BF16)
    rowq = ctx.sbuf(uid + "rowq", [1, 4 * TC], F32)
    ckv = ctx.sbuf(uid + "ckv", [128, T], BF16)
    ta = ctx.sbuf(uid + "ta", [16, 512], F32)
    tb = ctx.sbuf(uid + "tb", [16, 512], F32)
    tc_ = ctx.sbuf(uid + "tc", [16, 512], F32)
    kqch = ctx.sbuf(uid + "kqch", [16, 4 * 512], BF16)
    kcat = ctx.sbuf(uid + "kcat", [96, T], BF16)
    qcat = ctx.sbuf(uid + "qcat", [96, T], BF16)
    vaug = ctx.sbuf(uid + "vaug", [128, 32 * 65], BF16)
    ebuf = ctx.sbuf(uid + "ebuf", [128, 2 * 512], BF16)
    zbc = ctx.sbuf(uid + "zbc", [64, 512], F32)
    anst = ctx.sbuf(uid + "anst", [64, 2 * 512], BF16)
    anL = xt[0:64, :].bitcast(BF16)
    nrm = ctx.sbuf(uid + "nrm", [1, 512], F32)
    bscr = ctx.sbuf(uid + "bscr", [1, 16], F32)
    bcol = ctx.sbuf(uid + "bcol", [128, 1], F32)
    yout = ctx.sbuf(uid + "yout", [128, 2 * 512], F32)

    ps_p = ctx.psum(uid + "ps_p", [128, 2 * 512], F32)
    ps_s = ctx.psum(uid + "ps_s", [128, 2 * 512], F32)
    ps_at = ctx.psum(uid + "ps_at", [128, 2 * 512], F32)
    ps_m = ctx.psum(uid + "ps_m", [128, 1024], F32)

    mm = ctx.sem("mm"); vv = ctx.sem("vv"); ss = ctx.sem("ss")
    LD = ctx.pool("ld", 6)
    WG = ctx.pool("wg", 3)
    STP = ctx.pool("st", 3)
    SH = ctx.pool("sh", 2)

    LD.issue(P, "sync", lambda e: e.dma_start(out=cpool[:], in_=io["cpool"][:]))
    LD.issue(P, "sync", lambda e: e.dma_start(out=ones_row[:], in_=io["ones_row"][:]))
    LD.issue(P, "sync", lambda e: e.dma_start(out=ones_col[:], in_=io["ones_col"][:]))
    w1 = WG.issue(P, "gpsimd", lambda e: e.dma_start(
        out=sb3(wkv[:], DT), in_=io["Wkv"].rearrange("(a p) f -> p a f", p=128)))
    w2 = WG.issue(P, "gpsimd", lambda e: e.dma_start(out=wkup[:], in_=io["Wkup"][:]))
    w3 = WG.issue(P, "gpsimd", lambda e: e.dma_start(out=wvup[:], in_=io["Wvup"][:]))
    w5 = WG.issue(P, "gpsimd", lambda e: e.dma_start(
        out=sb3(wqr[:], DT), in_=io["Wqr"].rearrange("(a p) f -> p a f", p=128)))
    w6 = WG.issue(P, "gpsimd", lambda e: e.dma_start(out=wkr[:], in_=io["Wkr"][:]))
    c1 = LD.issue(P, "sync", lambda e: e.dma_start(out=cosT[:], in_=io["cosT"][:]))
    c2 = LD.issue(P, "sync", lambda e: e.dma_start(out=sinT[:], in_=io["sinT"][:]))
    c3 = LD.issue(P, "sync", lambda e: e.dma_start(out=strip[:], in_=io["strip"][:]))

    rv1_prev = None
    xn_ms = []
    for c in range(NCH):
        xl = LD.issue(P, "sync", lambda e, c=c: e.dma_start(
            out=sb3(xt[:], DT), in_=rearr_dT(io["xT"], c * TC, (c + 1) * TC)),
            waits=[(vv, xn_ms[-1])] if xn_ms else ())
        sv = P.do("scalar", lambda e: e.activation(sq[:], xt[:], AF.Square),
                  waits=(xl,) + (((vv, xn_ms[-1]),) if xn_ms else ()), inc=ss)
        def xstat(e):
            for a in range(DT):
                e.matmul(ps_m[0:1, 0:TC], ones_col[:],
                         xt[:, a * TC:(a + 1) * TC], start=(a == 0), stop=(a == DT - 1))
            for a in range(DT):
                last = e.matmul(ps_m[0:1, 512:512 + TC], ones_col[:],
                                sq[:, a * TC:(a + 1) * TC], start=(a == 0), stop=(a == DT - 1))
            return last
        xw = [(ss, sv)]
        if rv1_prev is not None:
            xw.append((vv, rv1_prev))
        xm = P.do("tensor", xstat, waits=xw, inc=mm)
        def rowv1(e):
            e.tensor_scalar(rowq[0:1, 0:TC], ps_m[0:1, 0:TC], 1.0 / D, None, ALU.mult)
            e.tensor_scalar(rowq[0:1, TC:2 * TC], ps_m[0:1, 512:512 + TC], 1.0 / D, None, ALU.mult)
            e.tensor_tensor(rowq[0:1, 2 * TC:3 * TC], rowq[0:1, 0:TC], rowq[0:1, 0:TC], ALU.mult)
            return e.tensor_tensor(rowq[0:1, TC:2 * TC], rowq[0:1, TC:2 * TC],
                                   rowq[0:1, 2 * TC:3 * TC], ALU.subtract)
        rv1 = P.do("vector", rowv1, waits=((mm, xm),), inc=vv)
        rv1_prev = rv1
        def rows1(e):
            e.activation(rowq[0:1, 2 * TC:3 * TC], rowq[0:1, TC:2 * TC], AF.Ln,
                         bias=ctx.constcol(1e-5)[0:1, :])
            return e.activation(rowq[0:1, 2 * TC:3 * TC], rowq[0:1, 2 * TC:3 * TC],
                                AF.Exp, scale=-0.5)
        rs1 = P.do("scalar", rows1, waits=((vv, rv1),), inc=ss)
        rv2 = P.do("vector", lambda e: e.scalar_tensor_tensor(
            rowq[0:1, 3 * TC:4 * TC], rowq[0:1, 0:TC], -1.0,
            rowq[0:1, 2 * TC:3 * TC], ALU.mult, ALU.mult), waits=((ss, rs1),), inc=vv)
        def xbcast(e):
            e.matmul(ps_m[:, 0:TC], ones_row[:], rowq[0:1, 2 * TC:3 * TC],
                     start=True, stop=True)
            return e.matmul(ps_m[:, 512:512 + TC], ones_row[:], rowq[0:1, 3 * TC:4 * TC],
                            start=True, stop=True)
        xbm = P.do("tensor", xbcast, waits=((vv, rv2),), inc=mm)
        def xnorm(e, c=c):
            for a in range(DT):
                e.tensor_tensor(sq[:, a * TC:(a + 1) * TC], xt[:, a * TC:(a + 1) * TC],
                                ps_m[:, 0:TC], ALU.mult)
                last = e.tensor_tensor(xnb[:, a * T + c * TC:a * T + (c + 1) * TC],
                                       sq[:, a * TC:(a + 1) * TC],
                                       ps_m[:, 512:512 + TC], ALU.add)
            return last
        xn = P.do("vector", xnorm, waits=((mm, xbm),), inc=vv)
        xn_ms.append(xn)
    ln_done = xn_ms[-1]

    ck_ms = []
    for c in range(NCH):
        po = (c % 2) * 512
        w = [(vv, ln_done), w1]
        if c >= 2:
            w.append((ss, ck_ms[c - 2]))
        def ckmm(e, c=c, po=po):
            for a in range(DT):
                last = e.matmul(ps_p[:, po:po + 512],
                                wkv[:, a * 128:(a + 1) * 128],
                                xnb[:, a * T + c * TC:a * T + (c + 1) * TC],
                                start=(a == 0), stop=(a == DT - 1))
            return last
        cm = P.do("tensor", ckmm, waits=w, inc=mm)
        ce = P.do("scalar", lambda e, c=c, po=po: e.activation(
            ckv[:, c * TC:(c + 1) * TC], ps_p[:, po:po + 512], AF.Copy),
            waits=((mm, cm),), inc=ss)
        ck_ms.append(ce)
    ckv_done = ck_ms[-1]

    rp_ms = []
    for c in range(NCH):
        po = (c % 2) * 512
        w = [(ss, ckv_done), w6, w5, c1, c2]
        if c >= 1:
            w.extend(rp_ms[c - 1][2])
        def rmm(e, c=c, po=po):
            e.matmul(ps_s[0:16, po:po + 512], wkr[:, 0:16],
                     ckv[:, c * TC:(c + 1) * TC], start=True, stop=True)
            e.matmul(ps_at[0:16, po:po + 512], wkr[:, 16:32],
                     ckv[:, c * TC:(c + 1) * TC], start=True, stop=True)
            for a in range(DT):
                e.matmul(ps_m[0:16, 0:512], wqr[:, a * 32:a * 32 + 16],
                         xnb[:, a * T + c * TC:a * T + (c + 1) * TC],
                         start=(a == 0), stop=(a == DT - 1))
            for a in range(DT):
                last = e.matmul(ps_m[0:16, 512:1024], wqr[:, a * 32 + 16:a * 32 + 32],
                                xnb[:, a * T + c * TC:a * T + (c + 1) * TC],
                                start=(a == 0), stop=(a == DT - 1))
            return last
        rm = P.do("tensor", rmm, waits=w, inc=mm)
        def rot(e, c=c, po=po):
            cs = cosT[:, c * TC:(c + 1) * TC]
            sn = sinT[:, c * TC:(c + 1) * TC]
            e.tensor_tensor(ta[:], ps_s[0:16, po:po + 512], cs, ALU.mult)
            e.tensor_tensor(tb[:], ps_at[0:16, po:po + 512], sn, ALU.mult)
            e.tensor_tensor(kqch[:, 0:512], ta[:], tb[:], ALU.subtract)
            e.tensor_tensor(ta[:], ps_at[0:16, po:po + 512], cs, ALU.mult)
            e.tensor_tensor(tb[:], ps_s[0:16, po:po + 512], sn, ALU.mult)
            e.tensor_tensor(kqch[:, 512:1024], ta[:], tb[:], ALU.add)
            e.tensor_tensor(ta[:], ps_m[0:16, 0:512], cs, ALU.mult)
            e.tensor_tensor(tb[:], ps_m[0:16, 512:1024], sn, ALU.mult)
            e.tensor_tensor(tc_[:], ta[:], tb[:], ALU.subtract)
            e.tensor_scalar(kqch[:, 1024:1536], tc_[:], SCL, None, ALU.mult)
            e.tensor_tensor(ta[:], ps_m[0:16, 512:1024], cs, ALU.mult)
            e.tensor_tensor(tb[:], ps_m[0:16, 0:512], sn, ALU.mult)
            e.tensor_tensor(tc_[:], ta[:], tb[:], ALU.add)
            return e.tensor_scalar(kqch[:, 1536:2048], tc_[:], SCL, None, ALU.mult)
        rv = P.do("vector", rot, waits=((mm, rm),), inc=vv)
        sds = []
        for wpl in range(4):
            sd = SH.issue(P, "sync", lambda e, c=c, wpl=wpl: e.dma_start(
                out=io["scrR"][wpl, :, c, :],
                in_=kqch[:, wpl * 512:(wpl + 1) * 512]),
                waits=((vv, rv),))
            sds.append(sd)
        rp_ms.append((rv, sds[-1], sds))
    allrp = [sd for ent in rp_ms for sd in ent[2]]
    kc = SH.issue(P, "sync", lambda e: e.dma_start(
        out=kcat[64:96, :],
        in_=io["scrR"][0:2, :, :, :].rearrange("w p c t -> (w p) (c t)")),
        waits=tuple(allrp))
    qc_ = SH.issue(P, "sync", lambda e: e.dma_start(
        out=qcat[64:96, :],
        in_=io["scrR"][2:4, :, :, :].rearrange("w p c t -> (w p) (c t)")),
        waits=(kc,))

    if "dbgR" in io:
        STP.issue(P, "sync", lambda e: e.dma_start(
            out=io["dbgR"][0, :, :], in_=kcat[64:96, :].bitcast(BF16)), waits=(kc,))
        STP.issue(P, "sync", lambda e: e.dma_start(
            out=io["dbgR"][1, :, :], in_=qcat[64:96, :].bitcast(BF16)), waits=(qc_,))

    st_ms = []
    head_end_mm = None
    prev_last_evac = None
    an_ms = {}
    wq_prev = None
    for h in range(NHD):
        hpar = h % 2
        wqld = WG.issue(P, "gpsimd", lambda e, h=h, hpar=hpar: e.dma_start(
            out=wqh[:, hpar * 512:(hpar + 1) * 512],
            in_=io["WqS"][:, h, :]),
            waits=[(mm, wq_prev)] if wq_prev is not None else ())
        kev = []
        for c in range(NCH):
            po = (c % 2) * 512
            w = [(ss, ckv_done), w2]
            if c >= 2:
                w.append((ss, kev[c - 2]))
            else:
                if head_end_mm is not None:
                    w.append((mm, head_end_mm))
                if prev_last_evac is not None:
                    w.append((ss, prev_last_evac))
            km = P.do("tensor", lambda e, c=c, po=po, h=h: e.matmul(
                ps_p[0:64, po:po + 512], wkup[:, h * 64:(h + 1) * 64],
                ckv[:, c * TC:(c + 1) * TC], start=True, stop=True),
                waits=w, inc=mm)
            ke = P.do("scalar", lambda e, c=c, po=po: e.activation(
                kcat[0:64, c * TC:(c + 1) * TC], ps_p[0:64, po:po + 512], AF.Copy),
                waits=((mm, km), kc, qc_), inc=ss)
            kev.append(ke)
        for c in range(NCH):
            po = (c % 2) * 512
            w = [wqld, (ss, kev[6 + c] if c >= 2 else kev[-1])]
            def qmm(e, c=c, po=po, hpar=hpar):
                for a in range(DT):
                    last = e.matmul(ps_p[0:64, po:po + 512],
                                    wqh[:, hpar * 512 + a * 64:hpar * 512 + (a + 1) * 64],
                                    xnb[:, a * T + c * TC:a * T + (c + 1) * TC],
                                    start=(a == 0), stop=(a == DT - 1))
                return last
            qm = P.do("tensor", qmm, waits=w, inc=mm)
            qe = P.do("scalar", lambda e, c=c, po=po: e.activation(
                qcat[0:64, c * TC:(c + 1) * TC], ps_p[0:64, po:po + 512], AF.Copy,
                scale=SCL), waits=((mm, qm),), inc=ss)
            kev.append(qe)
        wq_prev = mm.n
        vz = P.do("vector", lambda e: e.memset(vaug[:], 1.0),
                  waits=((mm, head_end_mm),) if head_end_mm is not None else (),
                  inc=vv)
        for tt in range(32):
            po = (tt % 2) * 512
            w = [w3, (vv, vz)]
            if tt >= 2:
                w.append((ss, kev[14 + tt]))
            else:
                w.append((ss, kev[15]))
            vm = P.do("tensor", lambda e, tt=tt, po=po, h=h: e.matmul(
                ps_p[0:128, po:po + 64], ckv[:, tt * 128:(tt + 1) * 128],
                wvup[:, h * 64:(h + 1) * 64], start=True, stop=True),
                waits=w, inc=mm)
            vev = P.do("scalar", lambda e, tt=tt, po=po: e.activation(
                vaug[:, tt * 65:tt * 65 + 64], ps_p[0:128, po:po + 64], AF.Copy),
                waits=((mm, vm),), inc=ss)
            kev.append(vev)
        prev_last_evac = kev[-1]

        sq2 = P.do("scalar", lambda e: e.activation(
            sq[0:96, 0:T], kcat[:, :], AF.Square),
            waits=((ss, kev[15]), kc, (vv, xn_ms[-1])), inc=ss)
        nm_ms = []
        for c in range(NCH):
            nm = P.do("tensor", lambda e, c=c: e.matmul(
                ps_m[0:1, (c % 2) * 512:(c % 2) * 512 + 512], ones_col[0:96, :],
                sq[0:96, c * TC:(c + 1) * TC], start=True, stop=True),
                waits=((ss, sq2),) + (((vv, nm_ms[c - 2]),) if c >= 2 else ()), inc=mm)
            rv = P.do("vector", lambda e, c=c: e.tensor_reduce(
                bscr[0:1, c:c + 1],
                ps_m[0:1, (c % 2) * 512:(c % 2) * 512 + 512], AX, ALU.max),
                waits=((mm, nm),), inc=vv)
            nm_ms.append(rv)
        kn = P.do("vector", lambda e: e.tensor_reduce(
            bscr[0:1, 8:9], bscr[0:1, 0:8], AX, ALU.max), waits=((vv, nm_ms[-1]),), inc=vv)
        sq3 = P.do("scalar", lambda e: e.activation(
            sq[0:96, 0:T], qcat[:, :], AF.Square), waits=((vv, kn), qc_), inc=ss)
        nm_ms2 = []
        for c in range(NCH):
            nm = P.do("tensor", lambda e, c=c: e.matmul(
                ps_m[0:1, (c % 2) * 512:(c % 2) * 512 + 512], ones_col[0:96, :],
                sq[0:96, c * TC:(c + 1) * TC], start=True, stop=True),
                waits=((ss, sq3),) + (((vv, nm_ms2[c - 2]),) if c >= 2 else ()), inc=mm)
            rv = P.do("vector", lambda e, c=c: e.tensor_reduce(
                bscr[0:1, c:c + 1],
                ps_m[0:1, (c % 2) * 512:(c % 2) * 512 + 512], AX, ALU.max),
                waits=((mm, nm),), inc=vv)
            nm_ms2.append(rv)
        qn = P.do("vector", lambda e: e.tensor_reduce(
            bscr[0:1, 9:10], bscr[0:1, 0:8], AX, ALU.max), waits=((vv, nm_ms2[-1]),), inc=vv)
        bs = P.do("scalar", lambda e: e.activation(
            bscr[0:1, 10:12], bscr[0:1, 8:10], AF.Sqrt), waits=((vv, qn),), inc=ss)
        bv = P.do("vector", lambda e: (
            e.tensor_tensor(bscr[0:1, 12:13], bscr[0:1, 10:11], bscr[0:1, 11:12], ALU.mult),
            e.tensor_scalar(bscr[0:1, 12:13], bscr[0:1, 12:13], -1.0, None, ALU.mult))[-1],
            waits=((ss, bs),), inc=vv)
        bb = P.do("tensor", lambda e: e.matmul(
            ps_m[:, 0:1], ones_row[:], bscr[0:1, 12:13], start=True, stop=True),
            waits=((vv, bv),), inc=mm)
        bc2 = P.do("vector", lambda e: e.tensor_copy(bcol[:], ps_m[:, 0:1]),
                   waits=((mm, bb),), inc=vv)

        exp_prev = {}
        pv_ms = {}
        for qcn in range(NCH):
            ktmax = 4 * qcn + 4
            pvn = None
            for kt in range(ktmax):
                po = (kt % 2) * 512
                w = [(vv, bc2)]
                if (qcn, kt - 2) in exp_prev:
                    w.append((ss, exp_prev[(qcn, kt - 2)]))
                elif qcn >= 1 and kt < 2:
                    w.append((ss, exp_prev[(qcn - 1, 4 * qcn + kt - 2)]))
                sm = P.do("tensor", lambda e, kt=kt, qcn=qcn, po=po: e.matmul(
                    ps_s[:, po:po + 512], kcat[:, kt * 128:(kt + 1) * 128],
                    qcat[:, qcn * TC:(qcn + 1) * TC], start=True, stop=True),
                    waits=w, inc=mm)
                ew = [(mm, sm)]
                if (qcn, kt - 2) in pv_ms:
                    ew.append((mm, pv_ms[(qcn, kt - 2)]))
                elif qcn >= 1 and kt < 2:
                    ew.append((mm, pv_ms[(qcn - 1, 4 * qcn + kt - 2)]))
                if "dbgKQ" in io and h == 0 and qcn == 7 and kt == 28:
                    STP.issue(P, "sync", lambda e: e.dma_start(
                        out=io["dbgKQ"][0, :, :], in_=kcat[0:64, 3584:4096]),
                        waits=((mm, sm),))
                    STP.issue(P, "sync", lambda e: e.dma_start(
                        out=io["dbgKQ"][1, :, :], in_=qcat[0:64, 3584:4096]),
                        waits=((mm, sm),))
                if "dbg" in io and h == 0 and qcn == 7 and kt in (28, 30):
                    sdump = P.do("vector", lambda e, po=po: e.tensor_copy(
                        sq[:, 0:512], ps_s[:, po:po + 512]), waits=((mm, sm),), inc=vv)
                    sdd = STP.issue(P, "sync", lambda e, kt=kt: e.dma_start(
                        out=io["dbg5"][(kt - 28) // 2, :, :], in_=sq[:, 0:512]),
                        waits=((vv, sdump),))
                    ew.append(sdd)
                ex = P.do("scalar", lambda e, kt=kt, po=po: e.activation(
                    ebuf[:, po:po + 512], ps_s[:, po:po + 512], AF.Exp,
                    bias=bcol[:]), waits=ew, inc=ss)
                exp_prev[(qcn, kt)] = ex
                lastdep = (ss, ex)
                if kt >= 4 * qcn:
                    off = 512 - (kt - 4 * qcn) * 128
                    mk = P.do("vector", lambda e, po=po, off=off: e.tensor_tensor(
                        ebuf[:, po:po + 512], ebuf[:, po:po + 512],
                        strip[:, off:off + 512], ALU.mult),
                        waits=((ss, ex), c3), inc=vv)
                    lastdep = (vv, mk)
                if "dbg" in io and h == 0 and qcn == 7 and kt == 28:
                    STP.issue(P, "sync", lambda e: e.dma_start(
                        out=io["dbg4"][:, :], in_=vaug[:, 28 * 65:32 * 65]),
                        waits=(lastdep,))
                if "dbg" in io and h == 0 and qcn == 7 and kt >= 28:
                    dsd2 = STP.issue(P, "sync", lambda e, kt=kt, po=po: e.dma_start(
                        out=io["dbg2"][kt - 28, :, :], in_=ebuf[:, po:po + 512]),
                        waits=(lastdep,))
                    lastdep = dsd2
                pw = [lastdep]
                if kt == 0 and (h, qcn - 2) in an_ms:
                    pw.append((vv, an_ms[(h, qcn - 2)][0]))
                elif kt == 0 and h >= 1 and qcn < 2:
                    pw.append((vv, an_ms[(h - 1, NCH - 2 + qcn)][0]))
                pv = P.do("tensor", lambda e, kt=kt, po=po, qcn=qcn, ktmax=ktmax:
                          e.matmul(
                    ps_at[0:65, (qcn % 2) * 512:(qcn % 2) * 512 + 512],
                    vaug[:, kt * 65:(kt + 1) * 65], ebuf[:, po:po + 512],
                    start=(kt == 0), stop=(kt == ktmax - 1)),
                    waits=pw, inc=mm)
                pv_ms[(qcn, kt)] = pv
                pvn = pv
            if "dbg" in io and h == 0:
                zraw = P.do("vector", lambda e, qcn=qcn: e.tensor_copy(
                    ta[0:1, 0:512], ps_at[64:65, (qcn % 2) * 512:(qcn % 2) * 512 + 512]),
                    waits=((mm, pvn),), inc=vv)
                STP.issue(P, "sync", lambda e, qcn=qcn: e.dma_start(
                    out=io["dbg3"][qcn, :, :], in_=ta[0:1, 0:512]), waits=((vv, zraw),))
            zr = P.do("vector", lambda e, qcn=qcn: e.reciprocal(
                nrm[0:1, 0:512], ps_at[64:65, (qcn % 2) * 512:(qcn % 2) * 512 + 512]),
                waits=((mm, pvn),), inc=vv)
            zb = P.do("tensor", lambda e: e.matmul(
                ps_m[0:64, 0:512], ones_row[:, 0:64], nrm[0:1, 0:512],
                start=True, stop=True), waits=((vv, zr),), inc=mm)
            ze = P.do("vector", lambda e: e.tensor_copy(zbc[:], ps_m[0:64, 0:512]),
                      waits=((mm, zb),), inc=vv)
            if "dbg" in io and h == 0:
                zd0 = STP.issue(P, "sync", lambda e, qcn=qcn: e.dma_start(
                    out=io["dbg"][qcn, 0, 1:2, :], in_=nrm[0:1, 0:512]),
                    waits=((vv, zr),))
                zd1 = STP.issue(P, "sync", lambda e, qcn=qcn: e.dma_start(
                    out=io["dbg"][qcn, 0, 0:1, :], in_=zbc[0:1, :]),
                    waits=((vv, ze), zd0))
                dbv = P.do("vector", lambda e, qcn=qcn: e.tensor_copy(
                    zbc[:], ps_at[0:64, (qcn % 2) * 512:(qcn % 2) * 512 + 512]),
                    waits=((vv, ze), zd1), inc=vv)
                dsd = STP.issue(P, "sync", lambda e, qcn=qcn: e.dma_start(
                    out=io["dbg"][qcn, 1, :, :], in_=zbc[:]), waits=((vv, dbv),))
                zrec = P.do("vector", lambda e, qcn=qcn: e.tensor_copy(
                    zbc[:], ps_m[0:64, 0:512]), waits=(dsd,), inc=vv)
            anw = []
            if (h, qcn - 2) in an_ms:
                anw.append(an_ms[(h, qcn - 2)][1])
            elif h >= 1 and qcn < 2:
                anw.append(an_ms[(h - 1, NCH - 2 + qcn)][1])
            if "dbg" in io and h == 0:
                anw.append((vv, zrec))
            an = P.do("vector", lambda e, qcn=qcn: e.tensor_tensor(
                anst[:, (qcn % 2) * 512:(qcn % 2) * 512 + 512],
                ps_at[0:64, (qcn % 2) * 512:(qcn % 2) * 512 + 512], zbc[:], ALU.mult),
                waits=anw, inc=vv)
            ansd = STP.issue(P, "sync", lambda e, h=h, qcn=qcn: e.dma_start(
                out=io["anD"][h, qcn, :, :],
                in_=anst[:, (qcn % 2) * 512:(qcn % 2) * 512 + 512]),
                waits=((vv, an),))
            an_ms[(h, qcn)] = (an, ansd)
        head_end_mm = mm.n

    last_an_dma = an_ms[(NHD - 1, NCH - 1)][1]
    ev_ms = []
    om_by_c = {}
    for c in range(NCH):
        anlw = [last_an_dma]
        if c >= 1:
            anlw.append((mm, om_by_c[c - 1]))
        anl = LD.issue(P, "sync", lambda e, c=c: e.dma_start(
            out=anL.rearrange("p (h t) -> p h t", h=NHD),
            in_=io["anD"][:, c, :, :].rearrange("h p t -> p h t")), waits=anlw)
        for dto in range(DT):
            i = c * DT + dto
            po = (i % 2) * 512
            dpar = dto % 2
            wol = WG.issue(P, "gpsimd", lambda e, c=c, dto=dto, dpar=dpar: e.dma_start(
                out=wod[:, dpar * 2048:(dpar + 1) * 2048].rearrange(
                    "p (h j) -> p h j", h=NHD),
                in_=io["WoS2"][:, dto, :, :]),
                waits=[(mm, ev_ms[-2][1])] if len(ev_ms) >= 2 else ())
            w = [anl, wol]
            def womm(e, c=c, dto=dto, po=po, dpar=dpar):
                for h in range(NHD):
                    last = e.matmul(ps_p[:, po:po + 512],
                                    wod[:, dpar * 2048 + h * 128:dpar * 2048 + (h + 1) * 128],
                                    anL[:, h * 512:(h + 1) * 512],
                                    start=(h == 0), stop=(h == NHD - 1))
                return last
            if len(ev_ms) >= 2:
                w.append((vv, ev_ms[-2][0]))
            om = P.do("tensor", womm, waits=w, inc=mm)
            om_by_c[c] = mm.n
            rlw = [last_an_dma]
            if len(ev_ms) >= 8:
                rlw.append((vv, ev_ms[-8][0]))
            rl = LD.issue(P, "sync", lambda e, c=c, dto=dto: e.dma_start(
                out=sq[:, dto * 512:dto * 512 + 512],
                in_=io["xT"][dto * 128:(dto + 1) * 128, c * TC:(c + 1) * TC]),
                waits=rlw)
            ow = [(mm, om), rl]
            if len(st_ms) >= 2:
                ow.append(st_ms[-2])
            oe = P.do("vector", lambda e, po=po, dto=dto: e.tensor_tensor(
                yout[:, po:po + 512], ps_p[:, po:po + 512],
                sq[:, dto * 512:dto * 512 + 512], ALU.add),
                waits=ow, inc=vv)
            ev_ms.append((oe, mm.n))
            sr = STP.issue(P, "sync", lambda e, c=c, dto=dto, po=po: e.dma_start(
                out=io["yT"][dto * 128:(dto + 1) * 128, c * TC:(c + 1) * TC],
                in_=yout[:, po:po + 512]), waits=((vv, oe),))
            st_ms.append(sr)

    drain_pools(P, ctx)
    return st_ms


def plan_mamba2(P, ctx, io, uid="s", nlayers=4, nchunks=None, stages=99):
    QC = 128
    NCH = T // QC if nchunks is None else nchunks
    NH2 = 32
    DIP = 4256

    ones_row = ctx.sbuf("ones_row", [1, 128], F32)
    ones_col = ctx.sbuf("ones_col", [128, 1], F32)
    onescb = ctx.sbuf(uid + "onescb", [128, 1], BF16)
    orb = ctx.sbuf("ones_rb", [1, 128], BF16)
    cpool = ctx.sbuf("cpool", [128, 16], F32)
    identB = ctx.sbuf("identB", [128, 128], BF16)
    identF = ctx.sbuf(uid + "identF", [128, 128], F32)
    maskB = ctx.sbuf(uid + "maskB", [128, 512], BF16)

    inw = ctx.sbuf(uid + "inw", [128, DT * DIP], BF16)
    owS = ctx.sbuf(uid + "owS", [128, 16 * 1024], BF16)
    cwX = ctx.sbuf(uid + "cwX", [128, 16 * 4], F32)
    cbX = ctx.sbuf(uid + "cbX", [128, 16], F32)
    cwBC = ctx.sbuf(uid + "cwBC", [64, 2 * 4], F32)
    cbBC = ctx.sbuf(uid + "cbBC", [64, 2], F32)
    lnw = ctx.sbuf(uid + "lnw", [128, DT], F32)
    lnb = ctx.sbuf(uid + "lnb", [128, DT], F32)
    dtb2 = ctx.sbuf(uid + "dtb2", [32, 1], F32)
    amat2 = ctx.sbuf(uid + "amat2", [32, 1], F32)
    dpS = ctx.sbuf(uid + "dpS", [128, 32], F32)
    nwS = ctx.sbuf(uid + "nwS", [128, 32], F32)

    xt = ctx.sbuf(uid + "xt", [128, DT * QC], F32)
    sq = ctx.sbuf(uid + "sq", [128, DT * QC], F32)
    xnb = ctx.sbuf(uid + "xnb", [128, DT * QC], BF16)
    rowq = ctx.sbuf(uid + "rowq", [1, 4 * QC], F32)
    zsil = ctx.sbuf(uid + "zsil", [128, 16 * QC], BF16)
    xse = ctx.sbuf(uid + "xse", [128, 16 * (QC + 4)], BF16)
    bce = ctx.sbuf(uid + "bce", [64, 2 * (QC + 4)], F32)
    xcs = ctx.sbuf(uid + "xcs", [128, 16 * QC], BF16)
    xcb = xcs
    Bcb = ctx.sbuf(uid + "Bcb", [64, 2 * QC], BF16)
    dtr = ctx.sbuf(uid + "dtr", [32, 5 * QC], F32)
    nlamT = ctx.sbuf(uid + "nlamT", [128, 32], F32)
    dtT = ctx.sbuf(uid + "dtT", [128, 32], F32)
    wall = ctx.sbuf(uid + "wall", [128, 32], F32)
    gv = ctx.sbuf(uid + "gv", [128, 32], F32)
    lamF = ctx.sbuf(uid + "lamF", [1, NH2 * QC], F32)
    lsc = ctx.sbuf(uid + "lsc", [32, QC], F32)
    onesF = ctx.sbuf(uid + "onesF", [32, QC], F32)
    lamef = ctx.sbuf(uid + "lamef", [1, NH2 * QC], BF16)
    geb = ctx.sbuf(uid + "geb", [128, 4 * QC], BF16)
    Gt = ctx.sbuf(uid + "Gt", [128, QC], BF16)
    Bst = ctx.sbuf(uid + "Bst", [128, 64], BF16)
    Xst = ctx.sbuf(uid + "Xst", [128, 2 * 128], BF16)
    Xd = ctx.sbuf(uid + "Xd", [128, 2 * 64], BF16)
    Xw = ctx.sbuf(uid + "Xw", [128, 2 * 64], BF16)
    Hbuf = ctx.sbuf(uid + "Hbuf", [128, 16 * 64], F32)
    Hbb = ctx.sbuf(uid + "Hbb", [64, 2 * 64], BF16)
    CEb = ctx.sbuf(uid + "CEb", [64, 2 * QC], BF16)
    ybuf = ctx.sbuf(uid + "ybuf", [128, 16 * QC], BF16)
    ygn = ctx.sbuf(uid + "ygn", [128, 16 * QC], BF16)
    ysq = ygn
    yout = ctx.sbuf(uid + "yout", [128, 2 * QC], F32)

    ps_a = ctx.psum(uid + "ps_a", [128, 1024], F32)
    ps_d = ctx.psum(uid + "ps_d", [128, 2048], F32)
    ps_y = ctx.psum(uid + "ps_y", [64, 512], F32)
    ps_t = ctx.psum(uid + "ps_t", [128, 512], F32)

    mm = ctx.sem("mm"); vv = ctx.sem("vv"); ss = ctx.sem("ss")
    LD = ctx.pool("ld", 6)
    WG = ctx.pool("wg", 3)
    STP = ctx.pool("st", 3)

    LD.issue(P, "sync", lambda e: e.dma_start(out=cpool[:], in_=io["cpool"][:]))
    LD.issue(P, "sync", lambda e: e.dma_start(out=ones_row[:], in_=io["ones_row"][:]))
    LD.issue(P, "sync", lambda e: e.dma_start(out=ones_col[:], in_=io["ones_col"][:]))
    LD.issue(P, "sync", lambda e: e.dma_start(out=onescb[:], in_=io["ones_col_b"][:]))
    LD.issue(P, "sync", lambda e: e.dma_start(out=orb[:], in_=io["ones_row_b"][:]))
    cid = LD.issue(P, "sync", lambda e: e.dma_start(out=identB[:], in_=io["identB"][:]))
    cif = LD.issue(P, "sync", lambda e: e.dma_start(out=identF[:], in_=io["identF"][:]))
    cmk = LD.issue(P, "sync", lambda e: e.dma_start(out=maskB[:], in_=io["maskrep"][:]))
    ycp = LD.issue(P, "sync", lambda e: e.dma_start(out=io["yT"][:], in_=io["xT"][:]))

    prev_layer_store = []
    for li in range(nlayers):
        wlin = WG.issue(P, "gpsimd", lambda e, li=li: e.dma_start(
            out=sb3(inw[:], DT),
            in_=io["in_w"][li].rearrange("(a p) f -> p a f", p=128)),
            waits=tuple(prev_layer_store))
        wlo = WG.issue(P, "gpsimd", lambda e, li=li: e.dma_start(
            out=owS[:].rearrange("p (h f) -> p h f", h=16), in_=io["outS"][li]),
            waits=tuple(prev_layer_store))
        wl3 = LD.issue(P, "sync", lambda e, li=li: e.dma_start(
            out=sb3(cwX[:], 16), in_=io["convX"][li]))
        wl4 = LD.issue(P, "sync", lambda e, li=li: e.dma_start(
            out=cbX[:], in_=io["convbX"][li]))
        wl5 = LD.issue(P, "sync", lambda e, li=li: e.dma_start(
            out=sb3(cwBC[:], 2), in_=io["convBC"][li]))
        wl6 = LD.issue(P, "sync", lambda e, li=li: e.dma_start(
            out=cbBC[:], in_=io["convbBC"][li]))
        wl7 = LD.issue(P, "sync", lambda e, li=li: e.dma_start(
            out=lnw[:], in_=io["lnwS"][li]))
        wl8 = LD.issue(P, "sync", lambda e, li=li: e.dma_start(
            out=lnb[:], in_=io["lnbS"][li]))
        wl9 = LD.issue(P, "sync", lambda e, li=li: e.dma_start(
            out=dtb2[:], in_=io["dtbS"][li]))
        wl10 = LD.issue(P, "sync", lambda e, li=li: e.dma_start(
            out=amat2[:], in_=io["AlogS"][li]))
        wl11 = LD.issue(P, "sync", lambda e, li=li: e.dma_start(
            out=dpS[:], in_=io["DpS"][li]))
        wl12 = LD.issue(P, "sync", lambda e, li=li: e.dma_start(
            out=nwS[:], in_=io["nwSS"][li]))
        wls = [wlin, wlo, wl3, wl4, wl5, wl6, wl7, wl8, wl9, wl10, wl11, wl12,
               cid, cif, cmk]
        if li == 0 and "_pro_stores" in io:
            wls = wls + list(io["_pro_stores"][-3:])

        sA = P.do("scalar", lambda e: e.activation(amat2[:], amat2[:], AF.Exp),
                  waits=(wl10,) + tuple(prev_layer_store), inc=ss)
        vA = P.do("vector", lambda e: e.tensor_scalar(amat2[:], amat2[:], -1.0, None,
                                                      ALU.mult), waits=((ss, sA),), inc=vv)
        z0 = P.do("vector", lambda e: (e.memset(Hbuf[:], 0.0),
                                       e.memset(xse[:], 0.0),
                                       e.memset(onesF[:], 1.0),
                                       e.memset(bce[:], 0.0))[-1],
                  waits=tuple(prev_layer_store), inc=vv)
        if "dbg_amat" in io:
            STP.issue(P, "sync", lambda e, li=li: e.dma_start(
                out=io["dbg_amat"][li], in_=amat2[:]), waits=((vv, vA),))

        store_ms = []
        ye_prev = None
        rv1_prev = None
        hup_last = None
        prev_geb_users = None
        prev_dt_users = None
        ostores = []
        for c in range(NCH):
            waits = list(wls) + ([store_ms[c - 2]] if c >= 2 else [])
            if ye_prev is not None:
                waits.append((vv, ye_prev))
            xl = LD.issue(P, "sync", lambda e, c=c: e.dma_start(
                out=sb3(xt[:], DT), in_=rearr_dT(io["innerT"], c * QC, (c + 1) * QC)),
                waits=waits)

            swaits = [xl]
            if ye_prev is not None:
                swaits.append((vv, ye_prev))
            sv = P.do("scalar", lambda e: e.activation(sq[:], xt[:], AF.Square),
                      waits=swaits, inc=ss)
            def xstat(e):
                for a in range(DT):
                    e.matmul(ps_a[0:1, 0:QC], ones_col[:],
                             xt[:, a * QC:(a + 1) * QC], start=(a == 0), stop=(a == DT - 1))
                for a in range(DT):
                    last = e.matmul(ps_a[0:1, 512:512 + QC], ones_col[:],
                                    sq[:, a * QC:(a + 1) * QC], start=(a == 0), stop=(a == DT - 1))
                return last
            xw = [(ss, sv)]
            if rv1_prev is not None:
                xw.append((vv, rv1_prev))
            xm = P.do("tensor", xstat, waits=xw, inc=mm)
            def rowv1(e):
                e.tensor_scalar(rowq[0:1, 0:QC], ps_a[0:1, 0:QC], 1.0 / D, None, ALU.mult)
                e.tensor_scalar(rowq[0:1, QC:2 * QC], ps_a[0:1, 512:512 + QC], 1.0 / D, None, ALU.mult)
                e.tensor_tensor(rowq[0:1, 2 * QC:3 * QC], rowq[0:1, 0:QC], rowq[0:1, 0:QC], ALU.mult)
                return e.tensor_tensor(rowq[0:1, QC:2 * QC], rowq[0:1, QC:2 * QC],
                                       rowq[0:1, 2 * QC:3 * QC], ALU.subtract)
            rv1 = P.do("vector", rowv1, waits=((mm, xm),), inc=vv)
            rv1_prev = rv1
            def rows1(e):
                e.activation(rowq[0:1, 2 * QC:3 * QC], rowq[0:1, QC:2 * QC], AF.Ln,
                             bias=ctx.constcol(1e-5)[0:1, :])
                return e.activation(rowq[0:1, 2 * QC:3 * QC], rowq[0:1, 2 * QC:3 * QC],
                                    AF.Exp, scale=-0.5)
            rs1 = P.do("scalar", rows1, waits=((vv, rv1),), inc=ss)
            rv2 = P.do("vector", lambda e: e.scalar_tensor_tensor(
                rowq[0:1, 3 * QC:4 * QC], rowq[0:1, 0:QC], -1.0,
                rowq[0:1, 2 * QC:3 * QC], ALU.mult, ALU.mult), waits=((ss, rs1),), inc=vv)
            def xbcast(e):
                e.matmul(ps_a[:, 0:QC], ones_row[:], rowq[0:1, 2 * QC:3 * QC],
                         start=True, stop=True)
                return e.matmul(ps_a[:, 512:512 + QC], ones_row[:], rowq[0:1, 3 * QC:4 * QC],
                                start=True, stop=True)
            xbm = P.do("tensor", xbcast, waits=((vv, rv2),), inc=mm)
            def xnorm(e):
                for a in range(DT):
                    e.tensor_tensor(sq[:, a * QC:(a + 1) * QC], xt[:, a * QC:(a + 1) * QC],
                                    ps_a[:, 0:QC], ALU.mult)
                    e.tensor_tensor(sq[:, a * QC:(a + 1) * QC], sq[:, a * QC:(a + 1) * QC],
                                    ps_a[:, 512:512 + QC], ALU.add)
                    last = e.tensor_scalar(xnb[:, a * QC:(a + 1) * QC],
                                           sq[:, a * QC:(a + 1) * QC],
                                           lnw[:, a:a + 1], lnb[:, a:a + 1],
                                           ALU.mult, ALU.add)
                return last
            xn = P.do("vector", xnorm, waits=((mm, xbm), wl7, wl8), inc=vv)

            if stages < 2:
                continue
            ipm = []
            for f in range(32):
                fo = (f % 2) * 512
                w = [(vv, xn)]
                if c == 0 and f == 0:
                    w += [wlin, (vv, z0)]
                if f >= 2:
                    w.append((ss, ipm[f - 2]))
                def ipmm(e, f=f, fo=fo):
                    for a in range(DT):
                        last = e.matmul(ps_a[:, fo:fo + QC],
                                        inw[:, a * DIP + f * 128:a * DIP + (f + 1) * 128],
                                        xnb[:, a * QC:(a + 1) * QC],
                                        start=(a == 0), stop=(a == DT - 1))
                    return last
                um = P.do("tensor", ipmm, waits=w, inc=mm)
                if f < 16:
                    ev = P.do("scalar", lambda e, f=f, fo=fo: e.activation(
                        zsil[:, f * QC:(f + 1) * QC], ps_a[:, fo:fo + QC], AF.Silu),
                        waits=((mm, um),), inc=ss)
                else:
                    fx = f - 16
                    ev = P.do("scalar", lambda e, fx=fx, fo=fo: e.activation(
                        xse[:, fx * (QC + 4) + 4:fx * (QC + 4) + 4 + QC],
                        ps_a[:, fo:fo + QC], AF.Copy), waits=((mm, um),), inc=ss)
                ipm.append(ev)
            def bcd_mm(e):
                for a in range(DT):
                    e.matmul(ps_a[0:64, 0:QC], inw[:, a * DIP + 4096:a * DIP + 4160],
                             xnb[:, a * QC:(a + 1) * QC], start=(a == 0), stop=(a == DT - 1))
                for a in range(DT):
                    e.matmul(ps_a[0:64, 512:512 + QC], inw[:, a * DIP + 4160:a * DIP + 4224],
                             xnb[:, a * QC:(a + 1) * QC], start=(a == 0), stop=(a == DT - 1))
                for a in range(DT):
                    last = e.matmul(ps_t[0:32, 0:QC], inw[:, a * DIP + 4224:a * DIP + 4256],
                                    xnb[:, a * QC:(a + 1) * QC], start=(a == 0), stop=(a == DT - 1))
                return last
            bw = [(ss, ipm[-1]), (vv, rv1)]
            if hup_last is not None:
                bw.append((vv, hup_last))
            bm = P.do("tensor", bcd_mm, waits=bw, inc=mm)
            bev = P.do("scalar", lambda e: (
                e.activation(bce[:, 4:4 + QC], ps_a[0:64, 0:QC], AF.Copy),
                e.activation(bce[:, (QC + 4) + 4:(QC + 4) + 4 + QC],
                             ps_a[0:64, 512:512 + QC], AF.Copy))[-1],
                waits=((mm, bm),), inc=ss)

            if stages < 3:
                continue
            dw = [(mm, bm), wl9, (vv, vA)]
            if prev_dt_users is not None:
                dw.append(prev_dt_users)
            dv1 = P.do("vector", lambda e: e.tensor_scalar(
                dtr[:, 0:QC], ps_t[0:32, 0:QC], dtb2[:, 0:1], None, ALU.add),
                waits=dw, inc=vv)
            def dtsp(e):
                e.activation(dtr[:, QC:2 * QC], dtr[:, 0:QC], AF.Abs)
                e.activation(dtr[:, QC:2 * QC], dtr[:, QC:2 * QC], AF.Exp, scale=-1.0)
                e.activation(dtr[:, QC:2 * QC], dtr[:, QC:2 * QC], AF.Ln,
                             bias=ctx.constcol(1.0)[0:32, :])
                return e.activation(dtr[:, 3 * QC:4 * QC], dtr[:, 0:QC], AF.Relu)
            sp1 = P.do("scalar", dtsp, waits=((vv, dv1),), inc=ss)
            def dtfin(e):
                e.tensor_tensor(dtr[:, 0:QC], dtr[:, QC:2 * QC], dtr[:, 3 * QC:4 * QC], ALU.add)
                e.tensor_scalar(dtr[:, QC:2 * QC], dtr[:, 0:QC], amat2[:, 0:1], None, ALU.mult)
                e.tensor_tensor_scan(lsc[:], onesF[:],
                                     dtr[:, QC:2 * QC], 0.0, ALU.mult, ALU.add)
                e.tensor_copy(dtr[:, 2 * QC:3 * QC], lsc[:])
                return e.tensor_scalar(dtr[:, 3 * QC:4 * QC], dtr[:, 2 * QC:3 * QC],
                                       -1.0, None, ALU.mult)
            dv2 = P.do("vector", dtfin, waits=((ss, sp1),), inc=vv)
            sp2 = P.do("scalar", lambda e: e.activation(
                dtr[:, 4 * QC:5 * QC], dtr[:, 2 * QC:3 * QC], AF.Exp),
                waits=((vv, dv2),), inc=ss)
            if "dbg_lam" in io:
                STP.issue(P, "sync", lambda e, c=c: e.dma_start(
                    out=io["dbg_lam"][c, :, 0:128], in_=lsc[:]), waits=((vv, dv2),))
                STP.issue(P, "sync", lambda e, c=c: e.dma_start(
                    out=io["dbg_lam"][c, :, 128:256], in_=dtr[:, QC:2 * QC]),
                    waits=((vv, dv2),))

            trm = P.do("tensor", lambda e: (
                e.transpose(ps_t[:, 256:256 + 32], dtr[:, 3 * QC:4 * QC],
                            identF[0:32, 0:32]),
                e.transpose(ps_t[:, 288:288 + 32], dtr[:, 0:QC], identF[0:32, 0:32]),
                e.transpose(ps_t[0:1, 320:352], dtr[:, 4 * QC - 1:4 * QC],
                            identF[0:32, 0:32]))[-1],
                waits=((vv, dv2), cif), inc=mm)
            tre = P.do("vector", lambda e: (
                e.tensor_copy(nlamT[:], ps_t[:, 256:256 + 32]),
                e.tensor_copy(dtT[:], ps_t[:, 288:288 + 32]),
                e.tensor_copy(rowq[0:1, 0:32], ps_t[0:1, 320:352]))[-1],
                waits=((mm, trm),), inc=vv)
            lb = P.do("tensor", lambda e: e.matmul(
                ps_t[:, 352:352 + 32], ones_row[:], rowq[0:1, 0:32],
                start=True, stop=True), waits=((vv, tre),), inc=mm)
            ge1 = P.do("scalar", lambda e: e.activation(
                rowq[0:1, 32:64], rowq[0:1, 0:32], AF.Exp, scale=-1.0),
                waits=((vv, tre),), inc=ss)
            gbm = P.do("tensor", lambda e: e.matmul(
                ps_t[:, 384:384 + 32], ones_row[:], rowq[0:1, 32:64],
                start=True, stop=True), waits=((ss, ge1),), inc=mm)
            def wallv(e):
                e.tensor_copy(gv[:, 0:32], ps_t[:, 384:384 + 32])
                return e.tensor_tensor(wall[:], nlamT[:], ps_t[:, 352:352 + 32],
                                       ALU.subtract)
            wv1 = P.do("vector", wallv, waits=((mm, gbm), (mm, lb)), inc=vv)
            wv2 = P.do("scalar", lambda e: e.activation(wall[:], wall[:], AF.Exp),
                       waits=((vv, wv1),), inc=ss)
            wv3 = P.do("vector", lambda e: e.tensor_tensor(wall[:], wall[:], dtT[:],
                                                           ALU.mult),
                       waits=((ss, wv2),), inc=vv)

            if stages < 4:
                continue
            def conv_x(e):
                for a in range(16):
                    o = a * (QC + 4)
                    e.tensor_scalar(xcs[:, a * QC:(a + 1) * QC],
                                    xse[:, o + 1:o + 1 + QC], cwX[:, a * 4:a * 4 + 1],
                                    None, ALU.mult)
                    for k in (1, 2):
                        e.scalar_tensor_tensor(xcs[:, a * QC:(a + 1) * QC],
                                               xse[:, o + 1 + k:o + 1 + k + QC],
                                               cwX[:, a * 4 + k:a * 4 + k + 1],
                                               xcs[:, a * QC:(a + 1) * QC], ALU.mult, ALU.add)
                    e.scalar_tensor_tensor(xcs[:, a * QC:(a + 1) * QC],
                                           xse[:, o + 4:o + 4 + QC],
                                           cwX[:, a * 4 + 3:a * 4 + 4],
                                           xcs[:, a * QC:(a + 1) * QC], ALU.mult, ALU.add)
                for j in range(2):
                    o = j * (QC + 4)
                    e.tensor_scalar(Bcb[:, j * QC:(j + 1) * QC],
                                    bce[:, o + 1:o + 1 + QC], cwBC[:, j * 4:j * 4 + 1],
                                    None, ALU.mult)
                    for k in (1, 2):
                        e.scalar_tensor_tensor(Bcb[:, j * QC:(j + 1) * QC],
                                               bce[:, o + 1 + k:o + 1 + k + QC],
                                               cwBC[:, j * 4 + k:j * 4 + k + 1],
                                               Bcb[:, j * QC:(j + 1) * QC], ALU.mult, ALU.add)
                    last = e.scalar_tensor_tensor(Bcb[:, j * QC:(j + 1) * QC],
                                                  bce[:, o + 4:o + 4 + QC],
                                                  cwBC[:, j * 4 + 3:j * 4 + 4],
                                                  Bcb[:, j * QC:(j + 1) * QC], ALU.mult, ALU.add)
                return last
            cv = P.do("vector", conv_x, waits=((ss, ipm[-1]), (ss, bev), wl3, wl5), inc=vv)
            def halo_v(e):
                for a in range(16):
                    o = a * (QC + 4)
                    e.tensor_copy(xse[:, o + 1:o + 4], xse[:, o + 1 + QC:o + 4 + QC])
                for j in range(2):
                    o = j * (QC + 4)
                    last = e.tensor_copy(bce[:, o + 1:o + 4], bce[:, o + 1 + QC:o + 4 + QC])
                return last
            hv = P.do("vector", halo_v, inc=vv)
            def silu_s(e):
                for a in range(16):
                    e.activation(xcs[:, a * QC:(a + 1) * QC], xcs[:, a * QC:(a + 1) * QC],
                                 AF.Silu, bias=cbX[:, a:a + 1])
                for j in range(2):
                    last = e.activation(Bcb[:, j * QC:(j + 1) * QC],
                                        Bcb[:, j * QC:(j + 1) * QC],
                                        AF.Silu, bias=cbBC[:, j:j + 1])
                return last
            cs = P.do("scalar", silu_s, waits=((vv, cv), wl4, wl6), inc=ss)

            if stages < 5:
                continue
            f1 = WG.issue(P, "gpsimd", lambda e: e.dma_start(
                out=io["scrL2"][0, :, :], in_=dtr[:, 2 * QC:3 * QC]),
                waits=((vv, dv2),))
            f3 = WG.issue(P, "gpsimd", lambda e: e.dma_start(
                out=io["scrLe"][:, :], in_=dtr[:, 4 * QC:5 * QC]),
                waits=((ss, sp2),))
            lw = [(vv, z0), f1]
            if prev_geb_users is not None:
                lw.append(prev_geb_users)
            f1b = WG.issue(P, "gpsimd", lambda e: e.dma_start(
                out=lamF[:], in_=io["scrL2"][0].rearrange("p t -> (p t)").unsqueeze(0)),
                waits=tuple(lw))
            f3b = WG.issue(P, "gpsimd", lambda e: e.dma_start(
                out=lamef[:], in_=io["scrLe"].rearrange("p t -> (p t)").unsqueeze(0)),
                waits=(f3,))

            if stages < 6:
                continue
            gw = [(ss, cs), (vv, wv1)]
            if prev_geb_users is not None:
                gw.append(prev_geb_users)
            gm = P.do("tensor", lambda e: e.matmul(
                ps_t[:, 128:128 + QC], Bcb[0:64, 0:QC], Bcb[0:64, QC:2 * QC],
                start=True, stop=True), waits=gw, inc=mm)
            gev = P.do("scalar", lambda e: e.activation(
                Gt[:], ps_t[:, 128:128 + QC], AF.Copy), waits=((mm, gm),), inc=ss)

            if stages < 6.5:
                continue
            btm = P.do("tensor", lambda e: e.transpose(
                ps_t[:, 448:448 + 32].bitcast(BF16), Bcb[0:64, 0:QC],
                identB[0:64, 0:64]),
                waits=((ss, cs), (vv, wv1), (ss, gev), cid), inc=mm)
            bte = P.do("vector", lambda e: e.tensor_copy(
                Bst[:, 0:64], ps_t[:, 448:448 + 32].bitcast(BF16)),
                waits=((mm, btm),), inc=vv)

            if stages < 7:
                continue
            xev_ms = {}
            yg_ms = {}
            dh_ms = {}
            exp_ms = {}
            ce_ms = {}
            hups = []
            ygl = None
            for h in range(NH2):
                hp = h % 2
                hq = h % 4
                bank = hq
                dcol = (h % 16) * 128
                dw_ = [f1b, cmk, cid]
                if h >= 1:
                    dw_.append((ss, exp_ms[h - 1]))
                def d2mm(e, h=h, dcol=dcol):
                    e.matmul(ps_d[:, dcol:dcol + QC], ones_row[:],
                             lamF[0:1, h * QC:(h + 1) * QC],
                             start=True, stop=False)
                    return e.matmul(ps_d[:, dcol:dcol + QC], identB[:],
                                    maskB[:, (h % 4) * 128:(h % 4) * 128 + 128],
                                    start=False, stop=True)
                dm2 = P.do("tensor", d2mm, waits=dw_, inc=mm)
                ew_ = [(mm, dm2), (vv, tre), (ss, gev)]
                if h >= 4:
                    ew_.append((vv, yg_ms[h - 4]))
                ex = P.do("scalar", lambda e, h=h, hq=hq, dcol=dcol: e.activation(
                    geb[:, hq * QC:(hq + 1) * QC], ps_d[:, dcol:dcol + QC],
                    AF.Exp, bias=nlamT[:, h:h + 1]), waits=ew_, inc=ss)
                exp_ms[h] = ex
                gemv = P.do("vector", lambda e, hq=hq: e.tensor_tensor(
                    geb[:, hq * QC:(hq + 1) * QC], Gt[:],
                    geb[:, hq * QC:(hq + 1) * QC], ALU.mult),
                    waits=((ss, ex),), inc=vv)
                lmw = [f3b]
                if h >= 4:
                    lmw.append((vv, yg_ms[h - 4]))
                if h >= 1:
                    lmw.append((vv, ce_ms[h - 1]))
                if h < 4:
                    lmw.append((ss, bev))
                lem = P.do("tensor", lambda e, h=h, hq=hq: e.matmul(
                    ps_a[0:64, hq * 128:(hq + 1) * 128], orb[:, 0:64],
                    lamef[0:1, h * QC:(h + 1) * QC], start=True, stop=True),
                    waits=lmw, inc=mm)
                cw_ = [(mm, lem), (ss, cs)]
                if h >= 2:
                    cw_.append((vv, yg_ms[h - 2]))
                cev = P.do("vector", lambda e, h=h, hp=hp, hq=hq: e.tensor_tensor(
                    CEb[:, hp * QC:(hp + 1) * QC], Bcb[0:64, QC:2 * QC],
                    ps_a[0:64, hq * 128:(hq + 1) * 128], ALU.mult),
                    waits=cw_, inc=vv)
                ce_ms[h] = cev
                if h % 2 == 0:
                    xw_ = [(ss, cs), cid, (vv, tre)]
                    if h == 0:
                        xw_.append((vv, bte))
                    if h >= 2:
                        xw_.append((vv, xev_ms[h - 2]))
                    if h >= 1:
                        xw_.append((vv, hups[h - 1]))
                    xtm = P.do("tensor", lambda e, h=h: e.transpose(
                        ps_t[:, 256:256 + 64].bitcast(BF16),
                        xcb[:, (h // 2) * QC:(h // 2) * QC + QC],
                        identB[:]), waits=xw_, inc=mm)
                    xev = P.do("vector", lambda e: e.tensor_copy(
                        Xst[:, 0:128],
                        ps_t[:, 256:256 + 64].bitcast(BF16)),
                        waits=((mm, xtm),), inc=vv)
                    xev_ms[h] = xev
                    xev_ms[h + 1] = xev
                xdv = P.do("vector", lambda e, h=h, hp=hp: (
                    e.tensor_scalar(Xd[:, hp * 64:(hp + 1) * 64],
                                    Xst[:, (h % 2) * 64:(h % 2) * 64 + 64],
                                    dtT[:, h:h + 1], None, ALU.mult),
                    e.tensor_scalar(Xw[:, hp * 64:(hp + 1) * 64],
                                    Xst[:, (h % 2) * 64:(h % 2) * 64 + 64],
                                    wall[:, h:h + 1], None, ALU.mult))[-1],
                    waits=((vv, wv3), (vv, xev_ms[h])), inc=vv)
                hbw = [(vv, z0)]
                if hup_last is not None and h < 2:
                    hbw.append((vv, hup_last))
                if h >= 2:
                    hbw.append((vv, hups[h - 2]))
                hbc = P.do("vector", lambda e, h=h, hp=hp: e.tensor_copy(
                    Hbb[:, hp * 64:(hp + 1) * 64],
                    Hbuf[(h % 2) * 64:(h % 2) * 64 + 64,
                         (h // 2) * 64:(h // 2) * 64 + 64]),
                    waits=hbw, inc=vv)
                yw_ = [(vv, xdv), (vv, gemv), (vv, cev), (vv, hbc)]
                if h >= 4:
                    yw_.append((vv, yg_ms[h - 4]))
                if h >= 1:
                    yw_.append((vv, yg_ms[h - 1]))
                def ymm(e, h=h, hp=hp, hq=hq):
                    col = hq * 128
                    e.matmul(ps_y[:, col:col + QC], Xd[:, hp * 64:(hp + 1) * 64],
                             geb[:, hq * QC:(hq + 1) * QC], start=True, stop=False)
                    return e.matmul(ps_y[:, col:col + QC], Hbb[:, hp * 64:(hp + 1) * 64],
                                    CEb[:, hp * QC:(hp + 1) * QC], start=False, stop=True)
                ym = P.do("tensor", ymm, waits=yw_, inc=mm)
                dhw = [(vv, xdv), (vv, bte), (vv, xev)]
                if h >= 1:
                    dhw.append((vv, hups[h - 1]))
                dhm = P.do("tensor", lambda e, h=h, hp=hp: e.matmul(
                    ps_t[0:64, 384 + hp * 64:384 + hp * 64 + 64],
                    Bst[:, 0:64], Xw[:, hp * 64:(hp + 1) * 64],
                    start=True, stop=True), waits=dhw, inc=mm)
                hup = P.do("vector", lambda e, h=h, hp=hp: e.scalar_tensor_tensor(
                    Hbuf[(h % 2) * 64:(h % 2) * 64 + 64,
                         (h // 2) * 64:(h // 2) * 64 + 64],
                    Hbuf[(h % 2) * 64:(h % 2) * 64 + 64,
                         (h // 2) * 64:(h // 2) * 64 + 64],
                    gv[(h % 2) * 64:(h % 2) * 64 + 64, h:h + 1],
                    ps_t[0:64, 384 + hp * 64:384 + hp * 64 + 64],
                    ALU.mult, ALU.add), waits=((mm, dhm), (vv, hbc)), inc=vv)
                hups.append(hup)
                def yev(e, h=h, hq=hq):
                    col = hq * 128
                    ysl = (slice((h % 2) * 64, (h % 2) * 64 + 64),
                           slice((h // 2) * QC, (h // 2) * QC + QC))
                    e.scalar_tensor_tensor(ybuf[ysl[0], ysl[1]],
                                           xcs[(h % 2) * 64:(h % 2) * 64 + 64,
                                               (h // 2) * QC:(h // 2) * QC + QC],
                                           dpS[(h % 2) * 64:(h % 2) * 64 + 64, h:h + 1],
                                           ps_y[:, col:col + QC],
                                           ALU.mult, ALU.add)
                    e.tensor_tensor(ybuf[ysl[0], ysl[1]], ybuf[ysl[0], ysl[1]],
                                    zsil[(h % 2) * 64:(h % 2) * 64 + 64,
                                         (h // 2) * QC:(h // 2) * QC + QC], ALU.mult)
                    return e.tensor_tensor(ysq[ysl[0], ysl[1]],
                                           ybuf[ysl[0], ysl[1]],
                                           ybuf[ysl[0], ysl[1]], ALU.mult)
                yg = P.do("vector", yev, waits=((mm, ym), wl11), inc=vv)
                yg_ms[h] = yg
                ygl = yg
            hup_last = hups[-1]
            prev_geb_users = (mm, mm.n)
            prev_dt_users = (vv, ygl)

            if stages < 8:
                continue
            def ysum(e):
                for a in range(16):
                    last = e.matmul(ps_a[0:1, 0:QC], onescb[:],
                                    ysq[:, a * QC:(a + 1) * QC],
                                    start=(a == 0), stop=(a == 15))
                return last
            ysm = P.do("tensor", ysum, waits=((vv, ygl),), inc=mm)
            yr1 = P.do("scalar", lambda e: e.activation(
                rowq[0:1, 0:QC], ps_a[0:1, 0:QC], AF.Ln,
                bias=ctx.constcol(1e-5)[0:1, :], scale=1.0 / 2048.0),
                waits=((mm, ysm),), inc=ss)
            yr2 = P.do("scalar", lambda e: e.activation(
                rowq[0:1, 0:QC], rowq[0:1, 0:QC], AF.Exp, scale=-0.5), inc=ss)
            ybm = P.do("tensor", lambda e: e.matmul(
                ps_t[:, 384:384 + QC], ones_row[:], rowq[0:1, 0:QC],
                start=True, stop=True), waits=((ss, yr2), (vv, hups[-1]),
                                               (vv, hups[-2])), inc=mm)
            ygv = P.do("vector", lambda e: (
                [e.scalar_tensor_tensor(
                    ygn[(h % 2) * 64:(h % 2) * 64 + 64,
                        (h // 2) * QC:(h // 2) * QC + QC],
                    ybuf[(h % 2) * 64:(h % 2) * 64 + 64,
                         (h // 2) * QC:(h // 2) * QC + QC],
                    nwS[(h % 2) * 64:(h % 2) * 64 + 64, h:h + 1],
                    ps_t[(h % 2) * 64:(h % 2) * 64 + 64, 384:384 + QC],
                    ALU.mult, ALU.mult)
                 for h in range(NH2)])[-1], waits=((mm, ybm), wl12), inc=vv)

            if "dbg_xnb" in io:
                STP.issue(P, "sync", lambda e, c=c: e.dma_start(
                    out=io["dbg_xnb"][c], in_=xnb[:]), waits=((vv, ygl),))
                STP.issue(P, "sync", lambda e, c=c: e.dma_start(
                    out=io["dbg_xcs"][c], in_=xcs[:]), waits=((vv, ygl),))
                STP.issue(P, "sync", lambda e, c=c: e.dma_start(
                    out=io["dbg_zsil"][c], in_=zsil[:]), waits=((vv, ygl),))
                STP.issue(P, "sync", lambda e, c=c: e.dma_start(
                    out=io["dbg_dtr"][c], in_=dtr[:]), waits=((vv, ygl),))
                STP.issue(P, "sync", lambda e, c=c: e.dma_start(
                    out=io["dbg_ybuf"][c], in_=ybuf[:]), waits=((vv, ygv),))
                STP.issue(P, "sync", lambda e, c=c: e.dma_start(
                    out=io["dbg_ygn"][c], in_=ygn[:]), waits=((vv, ygv),))
                STP.issue(P, "sync", lambda e, c=c: e.dma_start(
                    out=io["dbg_hbuf"][c], in_=Hbuf[:]), waits=((vv, hups[-1]), (vv, hups[-2])))
                STP.issue(P, "sync", lambda e, c=c: e.dma_start(
                    out=io["dbg_bcb"][c], in_=Bcb[:]), waits=((vv, ygl),))
                STP.issue(P, "sync", lambda e, c=c: e.dma_start(
                    out=io["dbg_geb"][c], in_=geb[:]), waits=((vv, ygl),))

            if stages < 9:
                continue
            op_ms = []
            for dto in range(DT):
                fo = (dto % 2) * 512
                yo = (dto % 2) * QC
                w = [(vv, ygv)]
                if c == 0 and dto == 0:
                    w.append(wlo)
                if dto >= 2:
                    w.append((vv, op_ms[dto - 2]))
                def opmm(e, dto=dto, fo=fo):
                    for a in range(16):
                        last = e.matmul(
                            ps_a[:, fo:fo + QC],
                            owS[:, a * 1024 + dto * 128:a * 1024 + (dto + 1) * 128],
                            ygn[:, a * QC:(a + 1) * QC],
                            start=(a == 0), stop=(a == 15))
                    return last
                om = P.do("tensor", opmm, waits=w, inc=mm)
                ow_ = [(mm, om)]
                if len(ostores) >= 2:
                    ow_.append(ostores[-2])
                def oev(e, dto=dto, fo=fo, yo=yo):
                    return e.tensor_tensor(yout[:, yo:yo + QC], ps_a[:, fo:fo + QC],
                                           xt[:, dto * QC:(dto + 1) * QC], ALU.add)
                oe = P.do("vector", oev, waits=ow_, inc=vv)
                op_ms.append(oe)
                if li == nlayers - 1:
                    osr = WG.issue(P, "gpsimd", lambda e, c=c, dto=dto, yo=yo:
                                   e.dma_start(
                        out=io["yT"][dto * 128:(dto + 1) * 128, c * QC:(c + 1) * QC],
                        in_=yout[:, yo:yo + QC], accum_op=ALU.add),
                        waits=((vv, oe), ycp))
                else:
                    osr = STP.issue(P, "sync", lambda e, c=c, dto=dto, yo=yo:
                                    e.dma_start(
                        out=io["innerT"][dto * 128:(dto + 1) * 128, c * QC:(c + 1) * QC],
                        in_=yout[:, yo:yo + QC]), waits=((vv, oe),))
                ostores.append(osr)
            ye_prev = op_ms[-1]
            store_ms.append(ostores[-1])
        prev_layer_store = [ostores[-3], ostores[-2], ostores[-1]] if stages >= 9 else []

    drain_pools(P, ctx)
    return prev_layer_store


def plan_m2_prologue(P, ctx, io):
    TC = 128
    NCH = T // TC
    ones_row = ctx.sbuf("ones_row", [1, 128], F32)
    ones_col = ctx.sbuf("ones_col", [128, 1], F32)
    cpool = ctx.sbuf("cpool", [128, 16], F32)
    xt = ctx.sbuf("sxt", [128, DT * TC], F32)
    sq = ctx.sbuf("ssq", [128, DT * TC], F32)
    xo = ctx.sbuf("pxo", [128, DT * TC], F32)
    rowq = ctx.sbuf("srowq", [1, 4 * TC], F32)
    ps_m = ctx.psum("sps_a", [128, 1024], F32)
    mm = ctx.sem("mm"); vv = ctx.sem("vv"); ss = ctx.sem("ss")
    LD = ctx.pool("ld", 6)
    STP = ctx.pool("st", 3)
    LD.issue(P, "sync", lambda e: e.dma_start(out=cpool[:], in_=io["cpool"][:]))
    LD.issue(P, "sync", lambda e: e.dma_start(out=ones_row[:], in_=io["ones_row"][:]))
    LD.issue(P, "sync", lambda e: e.dma_start(out=ones_col[:], in_=io["ones_col"][:]))
    rv1_prev = None
    stores = []
    xn_prev = None
    for c in range(NCH):
        xl = LD.issue(P, "sync", lambda e, c=c: e.dma_start(
            out=sb3(xt[:], DT), in_=rearr_dT(io["xT"], c * TC, (c + 1) * TC)),
            waits=[stores[c - 2]] if c >= 2 else [])
        sv = P.do("scalar", lambda e: e.activation(sq[:], xt[:], AF.Square),
                  waits=(xl,) + (((vv, xn_prev),) if xn_prev else ()), inc=ss)
        def xstat(e):
            for a in range(DT):
                e.matmul(ps_m[0:1, 0:TC], ones_col[:],
                         xt[:, a * TC:(a + 1) * TC], start=(a == 0), stop=(a == DT - 1))
            for a in range(DT):
                last = e.matmul(ps_m[0:1, 512:512 + TC], ones_col[:],
                                sq[:, a * TC:(a + 1) * TC], start=(a == 0), stop=(a == DT - 1))
            return last
        xw = [(ss, sv)]
        if rv1_prev is not None:
            xw.append((vv, rv1_prev))
        xm = P.do("tensor", xstat, waits=xw, inc=mm)
        def rowv1(e):
            e.tensor_scalar(rowq[0:1, 0:TC], ps_m[0:1, 0:TC], 1.0 / D, None, ALU.mult)
            e.tensor_scalar(rowq[0:1, TC:2 * TC], ps_m[0:1, 512:512 + TC], 1.0 / D, None, ALU.mult)
            e.tensor_tensor(rowq[0:1, 2 * TC:3 * TC], rowq[0:1, 0:TC], rowq[0:1, 0:TC], ALU.mult)
            return e.tensor_tensor(rowq[0:1, TC:2 * TC], rowq[0:1, TC:2 * TC],
                                   rowq[0:1, 2 * TC:3 * TC], ALU.subtract)
        rv1 = P.do("vector", rowv1, waits=((mm, xm),), inc=vv)
        rv1_prev = rv1
        def rows1(e):
            e.activation(rowq[0:1, 2 * TC:3 * TC], rowq[0:1, TC:2 * TC], AF.Ln,
                         bias=ctx.constcol(1e-5)[0:1, :])
            return e.activation(rowq[0:1, 2 * TC:3 * TC], rowq[0:1, 2 * TC:3 * TC],
                                AF.Exp, scale=-0.5)
        rs1 = P.do("scalar", rows1, waits=((vv, rv1),), inc=ss)
        rv2 = P.do("vector", lambda e: e.scalar_tensor_tensor(
            rowq[0:1, 3 * TC:4 * TC], rowq[0:1, 0:TC], -1.0,
            rowq[0:1, 2 * TC:3 * TC], ALU.mult, ALU.mult), waits=((ss, rs1),), inc=vv)
        def xbcast(e):
            e.matmul(ps_m[:, 0:TC], ones_row[:], rowq[0:1, 2 * TC:3 * TC],
                     start=True, stop=True)
            return e.matmul(ps_m[:, 512:512 + TC], ones_row[:], rowq[0:1, 3 * TC:4 * TC],
                            start=True, stop=True)
        xbm = P.do("tensor", xbcast, waits=((vv, rv2),), inc=mm)
        def xnorm(e):
            for a in range(DT):
                e.tensor_tensor(sq[:, a * TC:(a + 1) * TC], xt[:, a * TC:(a + 1) * TC],
                                ps_m[:, 0:TC], ALU.mult)
                last = e.tensor_tensor(xo[:, a * TC:(a + 1) * TC],
                                       sq[:, a * TC:(a + 1) * TC],
                                       ps_m[:, 512:512 + TC], ALU.add)
            return last
        xnw = [(mm, xbm)]
        if xn_prev is not None:
            xnw.append((vv, xn_prev))
        if c >= 1:
            xnw.append(stores[c - 1])
        xn = P.do("vector", xnorm, waits=xnw, inc=vv)
        xn_prev = xn
        sr = STP.issue(P, "sync", lambda e, c=c: e.dma_start(
            out=rearr_dT(io["innerT"], c * TC, (c + 1) * TC), in_=sb3(xo[:], DT)),
            waits=((vv, xn),))
        stores.append(sr)
    return stores


def _consts():
    return {
        "ones_row": np.ones((1, 128), np.float32),
        "ones_col": np.ones((128, 1), np.float32),
    }


def run_ffn_standalone(xT_cores, w_up, w_down):
    ctx = Ctx()
    nc = ctx.nc
    io = {
        "xT": nc.dram_tensor("xT", [D, T], F32, kind="ExternalInput").ap(),
        "w_upT": nc.dram_tensor("w_upT", [D, 4096], F32, kind="ExternalInput").ap(),
        "w_downT": nc.dram_tensor("w_downT", [4096, D], F32, kind="ExternalInput").ap(),
        "ones_row": nc.dram_tensor("ones_row_d", [1, 128], F32, kind="ExternalInput").ap(),
        "cpool": nc.dram_tensor("cpool_d", [128, 16], F32, kind="ExternalInput").ap(),
        "ones_col": nc.dram_tensor("ones_col_d", [128, 1], F32, kind="ExternalInput").ap(),
        "yT": nc.dram_tensor("yT", [D, T], F32, kind="ExternalOutput").ap(),
    }
    P = Plan(ctx)
    plan_ffn(P, ctx, io)
    P.emit()
    cs = _consts()
    wupT = np.ascontiguousarray(w_up.T)
    wdnT = np.ascontiguousarray(w_down.T)
    cp = ctx.const_values()
    in_maps = [
        {"xT": xT_cores[k], "w_upT": wupT, "w_downT": wdnT,
         "ones_row_d": cs["ones_row"], "ones_col_d": cs["ones_col"],
         "cpool_d": cp}
        for k in range(len(xT_cores))
    ]
    res = run_bass_kernel_spmd(nc, in_maps, core_ids=list(range(len(xT_cores))))
    out = [res.results[k]["yT"] for k in range(len(xT_cores))]
    ctx.close()
    return out


BF16NP = ml_dtypes.bfloat16


def _base_consts():
    return {
        "ones_row_d": np.ones((1, 128), np.float32),
        "ones_col_d": np.ones((128, 1), np.float32),
    }


def _base_io(nc):
    return {
        "ones_row": nc.dram_tensor("ones_row_d", [1, 128], F32, kind="ExternalInput").ap(),
        "ones_col": nc.dram_tensor("ones_col_d", [128, 1], F32, kind="ExternalInput").ap(),
        "cpool": nc.dram_tensor("cpool_d", [128, 16], F32, kind="ExternalInput").ap(),
    }


def _launch(plan_fns, io_build, maps_build, n_cores=NB):
    ctx = Ctx()
    nc = ctx.nc
    io = _base_io(nc)
    io.update(io_build(nc))
    P = Plan(ctx)
    for fn in plan_fns:
        fn(P, ctx, io)
    P.emit()
    common = _base_consts()
    common["cpool_d"] = ctx.const_values()
    in_maps = []
    for k in range(n_cores):
        m = dict(common)
        m.update(maps_build(k))
        in_maps.append(m)
    res = run_bass_kernel_spmd(nc, in_maps, core_ids=list(range(n_cores)))
    out = [np.asarray(res.results[k]["yT"]) for k in range(n_cores)]
    ctx.close()
    return out


def _m2_launch(xT_cores, inputs, nlayers=4, nchunks=None, skip_m2=False, stages=99,
               dbg=False):
    f32 = np.float32
    c = np.ascontiguousarray
    NLm2 = 4
    m2_inw = np.asarray(inputs["L1_attn_a_in_w"], f32)
    m2_cw = np.asarray(inputs["L1_attn_a_conv_w"], f32)
    m2_cb = np.asarray(inputs["L1_attn_a_conv_b"], f32)
    m2_dtb = np.asarray(inputs["L1_attn_a_dt_bias"], f32)
    m2_alog = np.asarray(inputs["L1_attn_a_A_log"], f32)
    m2_dp = np.asarray(inputs["L1_attn_a_Dp"], f32)
    m2_nw = np.asarray(inputs["L1_attn_a_norm_w"], f32)
    m2_ow = np.asarray(inputs["L1_attn_a_out_w"], f32)
    m2_lnw = np.asarray(inputs["L1_attn_a_ln_w"], f32)
    m2_lnb = np.asarray(inputs["L1_attn_a_ln_b"], f32)
    identB_h = np.eye(128).astype(BF16NP)
    mask01 = np.zeros((128, 128), f32)
    for p in range(128):
        mask01[p, :] = np.where(p <= np.arange(128), 0.0, -1e30)
    maskrep_h = np.tile(mask01, (1, 4)).astype(BF16NP)

    def io_m2(nc):
        d = {
            "xT": nc.dram_tensor("xT", [1024, T], F32, kind="ExternalInput").ap(),
            "innerT": nc.dram_tensor("innerT", [1024, T], F32, kind="Internal").ap(),
            "identB": nc.dram_tensor("identB", [128, 128], BF16, kind="ExternalInput").ap(),
            "identF": nc.dram_tensor("identF", [128, 128], F32, kind="ExternalInput").ap(),
            "maskrep": nc.dram_tensor("maskrep", [128, 512], BF16, kind="ExternalInput").ap(),
            "ones_row_b": nc.dram_tensor("ones_row_b", [1, 128], BF16, kind="ExternalInput").ap(),
            "ones_col_b": nc.dram_tensor("ones_col_b", [128, 1], BF16, kind="ExternalInput").ap(),
            "scrL2": nc.dram_tensor("scrL2", [2, 32, 128], F32, kind="Internal").ap(),
            "scrLe": nc.dram_tensor("scrLe", [32, 128], BF16, kind="Internal").ap(),
            "yT": nc.dram_tensor("yT", [1024, T], F32, kind="ExternalOutput").ap(),
        }
        d["in_w"] = [nc.dram_tensor(f"in_w{i}", [1024, 4256], F32, kind="ExternalInput").ap() for i in range(NLm2)]
        d["outS"] = [nc.dram_tensor(f"outS{i}", [128, 16, 1024], F32, kind="ExternalInput").ap() for i in range(NLm2)]
        d["convX"] = [nc.dram_tensor(f"convX{i}", [128, 16, 4], F32, kind="ExternalInput").ap() for i in range(NLm2)]
        d["convbX"] = [nc.dram_tensor(f"convbX{i}", [128, 16], F32, kind="ExternalInput").ap() for i in range(NLm2)]
        d["convBC"] = [nc.dram_tensor(f"convBC{i}", [64, 2, 4], F32, kind="ExternalInput").ap() for i in range(NLm2)]
        d["convbBC"] = [nc.dram_tensor(f"convbBC{i}", [64, 2], F32, kind="ExternalInput").ap() for i in range(NLm2)]
        d["lnwS"] = [nc.dram_tensor(f"lnwS{i}", [128, 8], F32, kind="ExternalInput").ap() for i in range(NLm2)]
        d["lnbS"] = [nc.dram_tensor(f"lnbS{i}", [128, 8], F32, kind="ExternalInput").ap() for i in range(NLm2)]
        d["dtbS"] = [nc.dram_tensor(f"dtbS{i}", [32, 1], F32, kind="ExternalInput").ap() for i in range(NLm2)]
        d["AlogS"] = [nc.dram_tensor(f"AlogS{i}", [32, 1], F32, kind="ExternalInput").ap() for i in range(NLm2)]
        d["DpS"] = [nc.dram_tensor(f"DpS{i}", [128, 32], F32, kind="ExternalInput").ap() for i in range(NLm2)]
        d["nwSS"] = [nc.dram_tensor(f"nwSS{i}", [128, 32], F32, kind="ExternalInput").ap() for i in range(NLm2)]
        if dbg:
            NCH0 = T // 128
            d["dbg_xnb"] = nc.dram_tensor("dbg_xnb", [NCH0, 128, 8 * 128], BF16, kind="ExternalOutput").ap()
            d["dbg_xcs"] = nc.dram_tensor("dbg_xcs", [NCH0, 128, 16 * 128], BF16, kind="ExternalOutput").ap()
            d["dbg_zsil"] = nc.dram_tensor("dbg_zsil", [NCH0, 128, 16 * 128], BF16, kind="ExternalOutput").ap()
            d["dbg_dtr"] = nc.dram_tensor("dbg_dtr", [NCH0, 32, 5 * 128], F32, kind="ExternalOutput").ap()
            d["dbg_ybuf"] = nc.dram_tensor("dbg_ybuf", [NCH0, 128, 16 * 128], BF16, kind="ExternalOutput").ap()
            d["dbg_ygn"] = nc.dram_tensor("dbg_ygn", [NCH0, 128, 16 * 128], BF16, kind="ExternalOutput").ap()
            d["dbg_hbuf"] = nc.dram_tensor("dbg_hbuf", [NCH0, 128, 16 * 64], F32, kind="ExternalOutput").ap()
            d["dbg_bcb"] = nc.dram_tensor("dbg_bcb", [NCH0, 64, 2 * 128], BF16, kind="ExternalOutput").ap()
            d["dbg_amat"] = nc.dram_tensor("dbg_amat", [4, 32, 1], F32, kind="ExternalOutput").ap()
            d["dbg_lam"] = nc.dram_tensor("dbg_lam", [NCH0, 32, 256], F32, kind="ExternalOutput").ap()
            d["dbg_geb"] = nc.dram_tensor("dbg_geb", [NCH0, 128, 4 * 128], BF16, kind="ExternalOutput").ap()
        return d

    def plans_m2(P, ctx, io):
        io["_pro_stores"] = plan_m2_prologue(P, ctx, io)
        if not skip_m2:
            plan_mamba2(P, ctx, io, nlayers=nlayers, nchunks=nchunks, stages=stages)

    m2_w = {
        "identB": identB_h, "identF": np.eye(128, dtype=f32),
        "maskrep": maskrep_h,
        "ones_row_b": np.ones((1, 128), BF16NP),
        "ones_col_b": np.ones((128, 1), BF16NP),
    }
    for i in range(NLm2):
        m2_w[f"in_w{i}"] = m2_inw[i]
        oS = np.zeros((128, 16, 1024), f32)
        for h in range(32):
            oS[(h % 2) * 64:(h % 2) * 64 + 64, h // 2, :] = m2_ow[i, h * 64:(h + 1) * 64, :]
        m2_w[f"outS{i}"] = oS
        m2_w[f"convX{i}"] = c(m2_cw[i, :, :2048].reshape(4, 16, 128).transpose(2, 1, 0))
        m2_w[f"convbX{i}"] = c(m2_cb[i, :2048].reshape(16, 128).T)
        m2_w[f"convBC{i}"] = c(m2_cw[i, :, 2048:].reshape(4, 2, 64).transpose(2, 1, 0))
        m2_w[f"convbBC{i}"] = c(m2_cb[i, 2048:].reshape(2, 64).T)
        m2_w[f"lnwS{i}"] = c(m2_lnw[i].reshape(8, 128).T)
        m2_w[f"lnbS{i}"] = c(m2_lnb[i].reshape(8, 128).T)
        m2_w[f"dtbS{i}"] = c(m2_dtb[i][:, None])
        m2_w[f"AlogS{i}"] = c(m2_alog[i][:, None])
        m2_w[f"DpS{i}"] = c(np.broadcast_to(m2_dp[i][None, :], (128, 32)))
        m2_w[f"nwSS{i}"] = c(np.tile(m2_nw[i].reshape(32, 64).T, (2, 1)))
    if dbg:
        ctx = Ctx()
        nc = ctx.nc
        io = _base_io(nc)
        io.update(io_m2(nc))
        P = Plan(ctx)
        plans_m2(P, ctx, io)
        P.emit()
        common = _base_consts()
        common["cpool_d"] = ctx.const_values()
        in_maps = [dict(common, **m2_w, xT=np.ascontiguousarray(xT_cores[k]))
                   for k in range(len(xT_cores))]
        res = run_bass_kernel_spmd(nc, in_maps, core_ids=list(range(len(xT_cores))))
        out = [dict(res.results[k]) for k in range(len(xT_cores))]
        ctx.close()
        return out
    return _launch([plans_m2], io_m2,
                   lambda k: dict(m2_w, xT=np.ascontiguousarray(xT_cores[k])),
                   n_cores=len(xT_cores))


def run_m2_standalone(xT_cores, inputs, nlayers=4, nchunks=None, skip_m2=False, stages=99,
                      dbg=False):
    return _m2_launch(xT_cores, inputs, nlayers=nlayers, nchunks=nchunks, skip_m2=skip_m2,
                      stages=stages, dbg=dbg)


def kernel(**inputs):
    f32 = np.float32
    c = np.ascontiguousarray
    x0 = np.asarray(inputs["L0_attn_a_x"], f32)

    Mq = np.asarray(inputs["L0_attn_a_Mq"], f32)
    Mk = np.asarray(inputs["L0_attn_a_Mk"], f32)
    Mv = np.asarray(inputs["L0_attn_a_Mv"], f32)
    ho = np.zeros((128, 2), f32); ho[0:64, 0] = 1; ho[64:128, 1] = 1

    def io_aa(nc):
        return {
            "xT": nc.dram_tensor("xT", [D, T], F32, kind="ExternalInput").ap(),
            "Mq": nc.dram_tensor("Mq", [1024, 512], F32, kind="ExternalInput").ap(),
            "Mk": nc.dram_tensor("Mk", [1024, 512], F32, kind="ExternalInput").ap(),
            "Mv": nc.dram_tensor("Mv", [1024, 1024], F32, kind="ExternalInput").ap(),
            "halfones": nc.dram_tensor("halfones", [128, 2], F32, kind="ExternalInput").ap(),
            "scrA": nc.dram_tensor("scrA", [2, 2], F32, kind="Internal").ap(),
            "yT": nc.dram_tensor("yT", [D, T], F32, kind="ExternalOutput").ap(),
        }
    y1 = _launch([plan_attn_a], io_aa,
                 lambda k: {"xT": c(x0[k].T), "Mq": Mq, "Mk": Mk, "Mv": Mv,
                            "halfones": ho})

    _ = inputs["L0_ffn_scores"], inputs["L0_ffn_top_experts"]
    ew = np.asarray(inputs["L0_ffn_expert_weights"], f32)
    w1 = np.asarray(inputs["L0_ffn_w1"], f32)
    w2 = np.asarray(inputs["L0_ffn_w2"], f32)
    w2S = c(w2.reshape(64, 128, 8, 128).transpose(1, 2, 0, 3))
    ewT = c(ew.T)

    def io_moe(nc):
        return {
            "xT": nc.dram_tensor("xT", [1024, 512], F32, kind="ExternalInput").ap(),
            "w1": nc.dram_tensor("w1", [1024, 8192], F32, kind="ExternalInput").ap(),
            "w2S": nc.dram_tensor("w2S", [128, 8, 64, 128], F32, kind="ExternalInput").ap(),
            "ewT": nc.dram_tensor("ewT", [2, 512], F32, kind="ExternalInput").ap(),
            "yT": nc.dram_tensor("yT", [1024, 512], F32, kind="ExternalOutput").ap(),
        }

    def run_moe(xb0T):
        outs = _launch([plan_moe], io_moe,
                       lambda k: {"xT": c(xb0T[:, k * 512:(k + 1) * 512]),
                                  "w1": w1, "w2S": w2S,
                                  "ewT": c(ewT[:, k * 512:(k + 1) * 512])})
        return np.concatenate(outs, axis=1)

    y2 = list(y1)
    y2[0] = run_moe(y1[0])

    ipw = np.asarray(inputs["L0_attn_b_in_proj_w"], f32)
    cw = np.asarray(inputs["L0_attn_b_conv_w"], f32)
    cb = np.asarray(inputs["L0_attn_b_conv_b"], f32)
    xpw = np.asarray(inputs["L0_attn_b_x_proj_w"], f32)
    dtw = np.asarray(inputs["L0_attn_b_dt_proj_w"], f32)
    dtb = np.asarray(inputs["L0_attn_b_dt_proj_b"], f32)
    alog = np.asarray(inputs["L0_attn_b_A_log"], f32)
    opw = np.asarray(inputs["L0_attn_b_out_proj_w"], f32)
    identB_h = np.eye(128).astype(BF16NP)

    def io_m1(nc):
        return {
            "xT": nc.dram_tensor("xT", [1024, T], F32, kind="ExternalInput").ap(),
            "inprojT": nc.dram_tensor("inprojT", [1024, 2048], F32, kind="ExternalInput").ap(),
            "convW": nc.dram_tensor("convW", [1024, 4], F32, kind="ExternalInput").ap(),
            "convB": nc.dram_tensor("convB", [128, 8], F32, kind="ExternalInput").ap(),
            "xprojB": nc.dram_tensor("xprojB", [1024, 16], F32, kind="ExternalInput").ap(),
            "xprojC": nc.dram_tensor("xprojC", [1024, 16], F32, kind="ExternalInput").ap(),
            "xprojD": nc.dram_tensor("xprojD", [128, 8], F32, kind="ExternalInput").ap(),
            "dtW": nc.dram_tensor("dtW", [128, 8], F32, kind="ExternalInput").ap(),
            "dtB": nc.dram_tensor("dtB", [128, 8], F32, kind="ExternalInput").ap(),
            "Alog": nc.dram_tensor("Alog", [1024, 16], F32, kind="ExternalInput").ap(),
            "outprojT": nc.dram_tensor("outprojT", [1024, 1024], F32, kind="ExternalInput").ap(),
            "identB": nc.dram_tensor("identB", [128, 128], BF16, kind="ExternalInput").ap(),
            "ones_row_b": nc.dram_tensor("ones_row_b", [1, 128], BF16, kind="ExternalInput").ap(),
            "scrB": nc.dram_tensor("scrB", [16, M1_TC], BF16, kind="Internal").ap(),
            "scrC": nc.dram_tensor("scrC", [16, M1_TC], BF16, kind="Internal").ap(),
            "yT": nc.dram_tensor("yT", [1024, T], F32, kind="ExternalOutput").ap(),
        }
    m1_w = {
        "inprojT": c(ipw.T), "convW": c(cw[:, 0, :]),
        "convB": c(cb.reshape(8, 128).T),
        "xprojB": c(xpw[:16].T), "xprojC": c(xpw[16:32].T),
        "xprojD": c(xpw[32].reshape(8, 128).T),
        "dtW": c(dtw[:, 0].reshape(8, 128).T), "dtB": c(dtb.reshape(8, 128).T),
        "Alog": alog, "outprojT": c(opw.T),
        "identB": identB_h, "ones_row_b": np.ones((1, 128), BF16NP),
    }
    y3 = _launch([plan_mamba1], io_m1,
                 lambda k: dict(m1_w, xT=c(y2[k])))

    y4 = list(y3)
    y4[0] = run_moe(y3[0])

    y5 = _m2_launch([c(y4[k]) for k in range(NB)], inputs)

    w_up = np.asarray(inputs["L1_ffn_w_up"], f32)
    w_down = np.asarray(inputs["L1_ffn_w_down"], f32)
    wupT = c(w_up.T)
    wdnT = c(w_down.T)

    def io_ffn(nc):
        return {
            "xT": nc.dram_tensor("xT", [D, T], F32, kind="ExternalInput").ap(),
            "w_upT": nc.dram_tensor("w_upT", [D, 4096], F32, kind="ExternalInput").ap(),
            "w_downT": nc.dram_tensor("w_downT", [4096, D], F32, kind="ExternalInput").ap(),
            "yT": nc.dram_tensor("yT", [D, T], F32, kind="ExternalOutput").ap(),
        }

    def run_ffn(xs):
        return _launch([plan_ffn], io_ffn,
                       lambda k: {"xT": c(xs[k]), "w_upT": wupT, "w_downT": wdnT})

    y6 = run_ffn(y5)

    Wkv = np.asarray(inputs["L1_attn_b_W_kv_down"], f32)
    Wkup = np.asarray(inputs["L1_attn_b_W_k_up"], f32)
    Wvup = np.asarray(inputs["L1_attn_b_W_v_up"], f32)
    Wq = np.asarray(inputs["L1_attn_b_W_q"], f32)
    Wqr = np.asarray(inputs["L1_attn_b_W_q_rope"], f32)
    Wkr = np.asarray(inputs["L1_attn_b_W_k_rope"], f32)
    Wo = np.asarray(inputs["L1_attn_b_W_o"], f32)
    inv_freq = 1.0 / 10000.0 ** (np.arange(0, 32, 2) / 32.0)
    freqs = np.outer(np.arange(T), inv_freq).astype(f32)
    cosT_h = c(np.cos(freqs).T)
    sinT_h = c(np.sin(freqs).T)
    stripm = np.zeros((128, 1024), BF16NP)
    for p in range(128):
        stripm[p, :] = (p <= np.arange(1024) - 512)
    WqS_h = np.zeros((128, 16, 512), f32)
    for h in range(16):
        for a in range(8):
            WqS_h[:, h, a * 64:(a + 1) * 64] = Wq[a * 128:(a + 1) * 128, h * 64:(h + 1) * 64]
    WoS2_h = np.zeros((64, 8, 16, 128), f32)
    for h in range(16):
        for dto in range(8):
            WoS2_h[:, dto, h, :] = Wo[h * 64:(h + 1) * 64, dto * 128:(dto + 1) * 128]
    NCHm = T // 512

    def io_mla(nc):
        return {
            "xT": nc.dram_tensor("xT", [1024, T], F32, kind="ExternalInput").ap(),
            "Wkv": nc.dram_tensor("Wkv", [1024, 128], F32, kind="ExternalInput").ap(),
            "Wkup": nc.dram_tensor("Wkup", [128, 1024], F32, kind="ExternalInput").ap(),
            "Wvup": nc.dram_tensor("Wvup", [128, 1024], F32, kind="ExternalInput").ap(),
            "WqS": nc.dram_tensor("WqS", [128, 16, 512], F32, kind="ExternalInput").ap(),
            "Wqr": nc.dram_tensor("Wqr", [1024, 32], F32, kind="ExternalInput").ap(),
            "Wkr": nc.dram_tensor("Wkr", [128, 32], F32, kind="ExternalInput").ap(),
            "WoS2": nc.dram_tensor("WoS2", [64, 8, 16, 128], F32, kind="ExternalInput").ap(),
            "cosT": nc.dram_tensor("cosT", [16, T], F32, kind="ExternalInput").ap(),
            "sinT": nc.dram_tensor("sinT", [16, T], F32, kind="ExternalInput").ap(),
            "strip": nc.dram_tensor("strip", [128, 1024], BF16, kind="ExternalInput").ap(),
            "scrR": nc.dram_tensor("scrR", [4, 16, NCHm, 512], BF16, kind="Internal").ap(),
            "anD": nc.dram_tensor("anD", [16, NCHm, 64, 512], BF16, kind="Internal").ap(),
            "yT": nc.dram_tensor("yT", [1024, T], F32, kind="ExternalOutput").ap(),
        }
    mla_w = {"Wkv": Wkv, "Wkup": Wkup, "Wvup": Wvup, "WqS": WqS_h, "Wqr": Wqr,
             "Wkr": Wkr, "WoS2": WoS2_h, "cosT": cosT_h, "sinT": sinT_h,
             "strip": stripm}
    y7 = _launch([plan_mla], io_mla,
                 lambda k: dict(mla_w, xT=c(y6[k])))

    y8 = run_ffn(y7)

    return np.stack([y8[k].T for k in range(NB)]).astype(np.float32)
